# version 22
# baseline (speedup 1.0000x reference)
import contextlib
import numpy as np
import concourse.bass as bass
import concourse.mybir as mybir

F32 = mybir.dt.float32
BF16 = mybir.dt.bfloat16
AF = mybir.ActivationFunctionType
ALU = mybir.AluOpType

D = 1024
NCH = 8
H = 16
TPG = 4
NEG = -1e30
EPS = 1e-5

ENGS = ["sync", "tensor", "scalar", "vector", "gpsimd"]


class Cx:
    DMA_PROD = {"xd": "sync", "sd": "sync", "gd": "gpsimd", "vd": "scalar"}

    def __init__(self, nc, es):
        self.nc = nc
        self.es = es
        self._s = {}
        self._bound = {}

    def _rec(self, n):
        if n not in self._s:
            self._s[n] = [self.es.enter_context(self.nc.semaphore(n)), 0]
            self._bound[n] = 0
        return self._s[n]

    def sem(self, n):
        return self._rec(n)[0]

    def inc(self, n, instr, k=1):
        r = self._rec(n)
        instr.then_inc(r[0], k)
        r[1] += k
        return r[1]

    def val(self, n):
        return self._rec(n)[1]

    def wait(self, eng, n, v=None):
        r = self._rec(n)
        tv = r[1] if v is None else v
        prod = self.DMA_PROD.get(n)
        if prod is not None and tv > self._bound[n]:
            pe_eng = getattr(self.nc, prod)
            if getattr(eng, "engine", None) != getattr(pe_eng, "engine", None):
                pe_eng.wait_ge(r[0], tv)
            self._bound[n] = tv
        eng.wait_ge(r[0], tv)


def build(n_groups=4, upto="full", max_g2=None, max_tiles=None, tile_stage=99, max_j=None, sdpa_stage=99):
    NT = TPG * n_groups
    NTOK = 128 * NT

    nc = bass.Bass("TRN2", target_bir_lowering=False)
    es = contextlib.ExitStack()
    cx = Cx(nc, es)

    x_in = nc.dram_tensor("x", [NTOK, D], F32, kind="ExternalInput").ap()
    xp_in = nc.dram_tensor("xp", [NTOK, D], F32, kind="ExternalInput").ap()
    y_out = nc.dram_tensor("y", [NTOK, D], F32, kind="ExternalOutput").ap()
    w_in = {}
    for nm, shp in [("wbT", [D, D]), ("bb", [1, D]), ("rb", [1, 8]),
                    ("rmid", [D, 16]), ("b2", [64, D]),
                    ("qkT", [D, D]), ("qv", [D, D]), ("kkT", [D, D]), ("kv", [D, D]),
                    ("vkT", [D, D]), ("vv", [D, D]), ("pkT", [D, D]), ("pv", [D, D]),
                    ("fkT", [D, 4 * D]), ("fv", [4 * D, D]),

                    ("ident", [128, 128])]:
        w_in[nm] = nc.dram_tensor(nm, shp, F32, kind="ExternalInput").ap()
    w_in["maskA"] = nc.dram_tensor("maskA", [128, 128], BF16, kind="ExternalInput").ap()
    w_in["maskB"] = nc.dram_tensor("maskB", [128, 1], BF16, kind="ExternalInput").ap()

    wsp = {}
    for nm in ["wbT", "rmid", "qkT", "qv", "kkT", "kv", "vkT", "vv",
               "pkT", "pv", "fkT", "fv"]:
        shp = list(w_in[nm].shape)
        wsp[nm + "_h"] = nc.dram_tensor(nm + "_h", shp, BF16).ap()
        wsp[nm + "_l"] = nc.dram_tensor(nm + "_l", shp, BF16).ap()

    x1_sp = nc.dram_tensor("x1_sp", [NTOK, D], F32).ap()
    h2_sp = nc.dram_tensor("h2_sp", [NTOK, D], F32).ap()
    o_sp = nc.dram_tensor("o_sp", [NTOK, D], F32).ap()
    q_sp = nc.dram_tensor("q_sp", [NTOK, D], F32).ap()
    CCW = 4128
    cc_out = [nc.dram_tensor(f"cc_out{g}", [2 * TPG * 128, CCW], BF16, addr_space="Local").ap()
              for g in range(n_groups)]

    def ccpos(g):
        return g // (2 * TPG), 512 * (g % 2) + 128 * ((g % (2 * TPG)) // 2)

    with es:
        ident = es.enter_context(nc.sbuf_tensor("identt", [128, 128], F32))
        ones_f = es.enter_context(nc.sbuf_tensor("ones_f", [128, 128], F32))
        ones_b = es.enter_context(nc.sbuf_tensor("ones_b", [128, 128], BF16))
        maskA = es.enter_context(nc.sbuf_tensor("maskA_t", [128, 128], BF16))
        maskB = es.enter_context(nc.sbuf_tensor("maskB_t", [128, 1], BF16))
        bbp = es.enter_context(nc.sbuf_tensor("bbp", [1, 2 * D], BF16))
        bbh, bbl = bbp[0:1, 0:D], bbp[0:1, D:2 * D]
        rbt = es.enter_context(nc.sbuf_tensor("rbt", [1, 8], F32))
        b2h = es.enter_context(nc.sbuf_tensor("b2h", [64, D], BF16))
        b2l = es.enter_context(nc.sbuf_tensor("b2l", [64, D], BF16))
        rmsb_h = es.enter_context(nc.sbuf_tensor("rmsb_h", [128, 128], BF16))
        rmsb_l = es.enter_context(nc.sbuf_tensor("rmsb_l", [128, 128], BF16))
        wa_h = es.enter_context(nc.sbuf_tensor("wa_h", [128, NCH * 1024], BF16))
        wa_l = es.enter_context(nc.sbuf_tensor("wa_l", [128, NCH * 1024], BF16))
        wb_h = es.enter_context(nc.sbuf_tensor("wb_h", [128, NCH * 1024], BF16))
        wb_l = es.enter_context(nc.sbuf_tensor("wb_l", [128, NCH * 1024], BF16))
        stg = es.enter_context(nc.sbuf_tensor("stg", [128, 2048], F32))
        stg_h = es.enter_context(nc.sbuf_tensor("stg_h", [128, 2048], BF16))
        stg_l = es.enter_context(nc.sbuf_tensor("stg_l", [128, 2048], BF16))
        h4T_h = es.enter_context(nc.sbuf_tensor("h4T_h", [128, NCH * 512], BF16))
        h4T_l = es.enter_context(nc.sbuf_tensor("h4T_l", [128, NCH * 512], BF16))
        o_facc = es.enter_context(nc.sbuf_tensor("o_facc", [128, 4096], F32))
        h3T_h = es.enter_context(nc.sbuf_tensor("h3T_h", [128, NCH * 512], BF16))
        h3T_l = es.enter_context(nc.sbuf_tensor("h3T_l", [128, NCH * 512], BF16))
        PT_h = es.enter_context(nc.sbuf_tensor("PT_h", [128, NCH * 512], BF16))
        PT_l = es.enter_context(nc.sbuf_tensor("PT_l", [128, NCH * 512], BF16))
        Pf = [es.enter_context(nc.sbuf_tensor(f"Pf{i}", [128, 512], F32)) for i in range(2)]
        xt = [es.enter_context(nc.sbuf_tensor(f"xt{i}", [128, D], F32)) for i in range(TPG)]
        ht = es.enter_context(nc.sbuf_tensor("ht", [128, D], F32))
        ht2 = es.enter_context(nc.sbuf_tensor("ht2", [128, D], F32))
        h1TB_h = es.enter_context(nc.sbuf_tensor("h1TB_h", [128, D], BF16))
        h1TB_l = es.enter_context(nc.sbuf_tensor("h1TB_l", [128, D], BF16))
        h2t = es.enter_context(nc.sbuf_tensor("h2t", [128, D], F32))
        h1T_h = es.enter_context(nc.sbuf_tensor("h1T_h", [128, D], BF16))
        h1T_l = es.enter_context(nc.sbuf_tensor("h1T_l", [128, D], BF16))
        trash = es.enter_context(nc.sbuf_tensor("trash", [128, D], BF16))
        stat = es.enter_context(nc.sbuf_tensor("stat", [128, 16], F32))
        g8 = es.enter_context(nc.sbuf_tensor("g8", [128, 8], F32))
        g8b = es.enter_context(nc.sbuf_tensor("g8b", [128, 8], BF16))
        P64 = es.enter_context(nc.sbuf_tensor("P64", [128, 64], F32))
        P64h = es.enter_context(nc.sbuf_tensor("P64h", [64, 128], BF16))
        P64l = es.enter_context(nc.sbuf_tensor("P64l", [64, 128], BF16))
        ev32 = es.enter_context(nc.sbuf_tensor("ev32", [128, 512], F32))
        rl = es.enter_context(nc.sbuf_tensor("rl", [128, 2048], F32))
        epsb = es.enter_context(nc.sbuf_tensor("epsb", [128, 1], F32))

        psA = es.enter_context(nc.psum_tensor("psA", [128, 2048], F32))
        psB = es.enter_context(nc.psum_tensor("psB", [128, 2048], F32))

        sy, te, sc, ve, gp = nc.sync, nc.tensor, nc.scalar, nc.vector, nc.gpsimd

        def dma(eng, out, in_, sem="xd", cut=True):
            v = cx.inc(sem, eng.dma_start(out=out, in_=in_), 16)
            if cut:
                eng.nop(nofuse=True)
            return v

        def barrier():
            for snm in ("xd", "sd"):
                cx.wait(sy, snm)
            cx.wait(gp, "gd")
            r = cx._rec("bar")
            for e in ENGS:
                getattr(nc, e).sem_inc(r[0], 1)
                r[1] += 1
            v = r[1]
            for e in ENGS:
                getattr(nc, e).wait_ge(r[0], v)

        def mm3(pap, lh, ll, rh, rlo, start, stop, tp=None):
            te.matmul(pap, lh, rh, start=start, stop=False, tile_position=tp)
            te.matmul(pap, lh, rlo, start=False, stop=False, tile_position=tp)
            return te.matmul(pap, ll, rh, start=False, stop=stop, tile_position=tp)

        def ln(src_ap, sb=0):
            s1, sq = stat[:, sb + 2:sb + 3], stat[:, sb + 3:sb + 4]
            mu, rstd = stat[:, sb:sb + 1], stat[:, sb + 1:sb + 2]
            t1, t2 = stat[:, sb + 4:sb + 5], stat[:, sb + 5:sb + 6]
            cx.wait(sc, "dv")
            cx.inc("ac", sc.activation(trash[:, 0:D], src_ap, AF.Copy, accum_out=s1))
            cx.wait(sc, "ac")
            cx.inc("ac", sc.activation(trash[:, 0:D], src_ap, AF.Square, accum_out=sq))
            cx.wait(ve, "ac")
            ve.tensor_scalar(mu, s1, 1.0 / D, None, ALU.mult)
            cx.inc("dv", ve.tensor_scalar(t1, sq, 1.0 / D, None, ALU.mult))
            cx.wait(ve, "dv")
            cx.inc("dv", ve.tensor_tensor(t2, mu, mu, ALU.mult))
            cx.wait(ve, "dv")
            cx.inc("dv", ve.tensor_sub(t1, t1, t2))
            cx.wait(sc, "dv")
            cx.inc("ac", sc.activation(t2, t1, AF.Sqrt, bias=epsb[:, 0:1]))
            cx.wait(ve, "ac")
            cx.inc("dv", ve.reciprocal(rstd, t2))
            cx.wait(ve, "dv")

        def ln_apply(dst_ap, src_ap, sb=0):
            return ve.tensor_scalar(dst_ap, src_ap, stat[:, sb:sb + 1],
                                    stat[:, sb + 1:sb + 2], ALU.subtract, ALU.mult)

        def transp_split(src, scol_fn, nch, dsth, dstl, dcol_fn):
            for b0 in range(0, nch, 4):
                nb = min(4, nch - b0)
                pts = psB[:, 1536:2048]
                cx.wait(te, "dv", cx.val("dv"))
                for i in range(nb):
                    c = scol_fn(b0 + i)
                    cx.inc("pe", te.matmul(pts[:, 128 * i:128 * (i + 1)], src[:, c:c + 128],
                                           ident[:], is_transpose=True))
                cx.wait(sc, "pe")
                for i in range(nb):
                    d = dcol_fn(b0 + i)
                    cx.inc("ac", sc.activation(dsth[:, d:d + 128],
                                               pts[:, 128 * i:128 * (i + 1)], AF.Copy))
                cx.wait(ve, "ac")
                for i in range(nb):
                    d = dcol_fn(b0 + i)
                    cx.inc("dv", ve.tensor_sub(dstl[:, d:d + 128],
                                               pts[:, 128 * i:128 * (i + 1)],
                                               dsth[:, d:d + 128]))


        def load_pair(dsth, dstl, nm, r0, r1, c0, c1, dst_off=0):
            cw = c1 - c0
            for i, rr in enumerate(range(r0, r1, 128)):
                dma(sy, dsth[:, dst_off + i * cw:dst_off + (i + 1) * cw],
                    wsp[nm + "_h"][rr:rr + 128, c0:c1])
                dma(sy, dstl[:, dst_off + i * cw:dst_off + (i + 1) * cw],
                    wsp[nm + "_l"][rr:rr + 128, c0:c1])

        cx.inc("gp", gp.memset(ones_f[:], 1.0))
        cx.inc("gp", gp.memset(epsb[:], EPS))
        cx.inc("gp", gp.memset(ones_b[:], 1.0))
        dma(sy, ident[:], w_in["ident"][:, :])
        dma(sy, maskA[:], w_in["maskA"][:, :])
        dma(sy, maskB[:], w_in["maskB"][:, :])
        dma(sy, rbt[:], w_in["rb"][:, :])
        dma(sy, stg[0:1, 0:D], w_in["bb"][:, :])
        cx.wait(sc, "xd")
        cx.inc("ac", sc.activation(bbh, stg[0:1, 0:D], AF.Copy))
        cx.wait(ve, "ac")
        cx.inc("dv", ve.tensor_sub(bbl, stg[0:1, 0:D], bbh))
        dma(sy, stg[0:64, 1024:1024 + D], w_in["b2"][:, :])
        cx.wait(sc, "xd")
        cx.inc("ac", sc.activation(b2h[:], stg[0:64, 1024:1024 + D], AF.Copy))
        cx.wait(ve, "ac")
        cx.inc("dv", ve.tensor_sub(b2l[:], stg[0:64, 1024:1024 + D], b2h[:]))
        cx.wait(sy, "dv")

        for nm in ["wbT", "rmid", "qkT", "qv", "kkT", "kv", "vkT", "vv",
                   "pkT", "pv", "fkT", "fv"]:
            src = w_in[nm]
            R, C = src.shape
            for r0 in range(0, R, 128):
                for c0 in range(0, C, 2048):
                    cw = min(2048, C - c0)
                    cx.wait(sy, "gd", cx.val("gd"))
                    dma(sy, stg[:, 0:cw], src[r0:r0 + 128, c0:c0 + cw])
                    dma(gp, stg_h[:, 0:cw], src[r0:r0 + 128, c0:c0 + cw], sem="gd")
                    cx.wait(gp, "xd")
                    cx.wait(gp, "gd")
                    cx.inc("gp", gp.tensor_sub(stg_l[:, 0:cw], stg[:, 0:cw], stg_h[:, 0:cw]))
                    cx.wait(gp, "gp")
                    dma(gp, wsp[nm + "_h"][r0:r0 + 128, c0:c0 + cw], stg_h[:, 0:cw], sem="gd")
                    dma(gp, wsp[nm + "_l"][r0:r0 + 128, c0:c0 + cw], stg_l[:, 0:cw], sem="gd")
        barrier()
        if upto == "prologue":
            return nc, cx, locals()

        AR = [(wa_h, wa_l), (wb_h, wb_l)]
        load_pair(rmsb_h, rmsb_l, "rmid", 0, D, 0, 16)
        afree = {0: cx.val("pe"), 1: cx.val("pe")}
        wv_of = {}

        def prefetch(nm, idx, c0=0, c1=None, r0=0, r1=None):
            sh = wsp[nm + "_h"]
            if c1 is None:
                c1 = sh.shape[1]
            if r1 is None:
                r1 = sh.shape[0]
            cx.wait(sy, "pe", afree[idx])
            load_pair(AR[idx][0], AR[idx][1], nm, r0, r1, c0, c1)
            wv_of[(nm, idx)] = cx.val("xd")
            cx.wait(sy, "xd", wv_of[(nm, idx)])

        for G2 in range(2 * n_groups):
            if max_g2 is not None and G2 >= max_g2:
                break
            G, half = G2 // 2, G2 % 2
            spill = (half == 0)
            xsrc = x_in if half == 0 else xp_in
            t0 = G * TPG
            cx.wait(sy, "dv", cx.val("dv"))
            cx.wait(sy, "ac", cx.val("ac"))
            for i in range(TPG):
                dma(sy, xt[i][:], xsrc[128 * (t0 + i):128 * (t0 + i + 1), :])
            xv = cx.val("xd")
            cx.wait(sy, "xd", xv)
            fa = G2 % 2
            fWh, fWl = AR[fa]
            if G2 == 0:
                prefetch("wbT", 0)
            xd0 = wv_of[("wbT", fa)]
            prefetch("kkT", 1 - fa)

            dv_h3 = {}
            lnbuf = [(ht, h1T_h, h1T_l), (ht2, h1TB_h, h1TB_l)]
            ntile = TPG if max_tiles is None else min(TPG, max_tiles)
            cx.wait(sc, "xd", max(xd0, xv))
            ln(xt[0][:], sb=8)
            cx.wait(ve, "dv")
            cx.inc("dv", ln_apply(lnbuf[0][0][:], xt[0][:], sb=8))
            cx.wait(te, "dv")
            transp_split(lnbuf[0][0], lambda c: 128 * c, NCH,
                         lnbuf[0][1], lnbuf[0][2], lambda c: 128 * c)
            dv_h1 = cx.val("dv")
            dv_stt = cx.val("dv")
            for i in range(ntile):
                t = t0 + i
                xa = xt[i][:]
                hb, hTh, hTl = lnbuf[i % 2]
                pbase = psA[:, 0:1024]
                p16 = psA[:, 1024:1040]
                cx.wait(te, "dv", dv_h1)
                cx.wait(te, "xd", max(xd0, xv))
                for cc in range(NCH):
                    lh = hTh[:, 128 * cc:128 * (cc + 1)]
                    llo = hTl[:, 128 * cc:128 * (cc + 1)]
                    mm3(p16[:, 0:16], lh, llo, rmsb_h[:, 16 * cc:16 * (cc + 1)],
                        rmsb_l[:, 16 * cc:16 * (cc + 1)], start=(cc == 0), stop=False)
                cx.inc("pe", te.matmul(p16[:, 0:8], ones_f[0:1, :], rbt[:], start=False, stop=True))
                pe_p16 = cx.val("pe")
                cx.wait(te, "dv", dv_stt)
                first = True
                for cc in range(NCH):
                    lh = hTh[:, 128 * cc:128 * (cc + 1)]
                    llo = hTl[:, 128 * cc:128 * (cc + 1)]
                    for hf in range(2):
                        mm3(pbase[:, 512 * hf:512 * (hf + 1)], lh, llo,
                            fWh[:, 1024 * cc + 512 * hf:1024 * cc + 512 * (hf + 1)],
                            fWl[:, 1024 * cc + 512 * hf:1024 * cc + 512 * (hf + 1)],
                            start=first, stop=False)
                    first = False
                te.matmul(pbase[:, 0:512], ones_b[0:1, :], bbp[0:1, 0:512], start=False, stop=False)
                te.matmul(pbase[:, 512:1024], ones_b[0:1, :], bbp[0:1, 512:1024], start=False, stop=False)
                te.matmul(pbase[:, 0:512], ones_b[0:1, :], bbp[0:1, 1024:1536], start=False, stop=False)
                cx.inc("pe", te.matmul(pbase[:, 512:1024], ones_b[0:1, :], bbp[0:1, 1536:2048], start=False, stop=False))
                cx.wait(sc, "pe", pe_p16)
                cx.inc("ac", sc.activation(g8[:], p16[:, 0:8], AF.Exp, accum_out=stat[:, 6:7]))
                cx.wait(ve, "ac")
                cx.inc("dv", ve.reciprocal(stat[:, 7:8], stat[:, 6:7]))
                cx.wait(ve, "dv")
                cx.inc("dv", ve.tensor_scalar(g8b[:], g8[:], stat[:, 7:8], None, ALU.mult))
                cx.wait(ve, "dv")
                cx.inc("dv", ve.tensor_copy(g8[:], g8b[:]))
                cx.wait(ve, "dv")
                for e in range(8):
                    cx.inc("dv", ve.tensor_scalar(P64[:, 8 * e:8 * (e + 1)], p16[:, 8:16],
                                     g8[:, e:e + 1], 2.0, ALU.mult, ALU.mult))
                if i + 1 < ntile:
                    nb, nTh, nTl = lnbuf[(i + 1) % 2]
                    cx.wait(sc, "xd", max(xd0, xv))
                    ln(xt[i + 1][:], sb=8)
                    cx.wait(ve, "dv")
                    cx.inc("dv", ln_apply(nb[:], xt[i + 1][:], sb=8))
                    cx.wait(te, "dv")
                    transp_split(nb, lambda c: 128 * c, NCH, nTh, nTl,
                                 lambda c: 128 * c)
                    dv_h1 = cx.val("dv")
                pt64 = psB[:, 1024:1152]
                cx.wait(te, "dv")
                cx.inc("pe", te.matmul(pt64[0:64, 0:128], P64[:], ident[:], is_transpose=True))
                cx.wait(sc, "pe")
                cx.inc("ac", sc.activation(P64h[:], pt64[0:64, 0:128], AF.Copy))
                cx.wait(ve, "ac")
                cx.inc("dv", ve.tensor_sub(P64l[:], pt64[0:64, 0:128], P64h[:]))
                cx.wait(te, "dv")
                for hf in range(2):
                    cx.inc("pe", mm3(pbase[:, 512 * hf:512 * (hf + 1)], P64h[:], P64l[:],
                        b2h[:, 512 * hf:512 * (hf + 1)], b2l[:, 512 * hf:512 * (hf + 1)],
                        start=False, stop=True))
                cx.wait(ve, "pe")
                cx.inc("dv", ve.scalar_tensor_tensor(xa, pbase[:, 0:1024], 0.5, xa,
                                                     ALU.mult, ALU.add))
                dv_stt = cx.val("dv")
                if spill:
                    cx.wait(sy, "dv")
                    dma(sy, x1_sp[128 * t:128 * (t + 1), :], xa, sem="sd")
                ln(xa)
                cx.wait(ve, "dv")
                cx.wait(ve, "sd", cx.val("sd"))
                cx.inc("dv", ln_apply(h2t[:], xa))
                if spill:
                    cx.wait(sy, "dv")
                    dma(sy, h2_sp[128 * t:128 * (t + 1), :], h2t[:], sem="sd")
                ln(h2t[:])
                cx.wait(ve, "dv")
                cx.wait(ve, "gp", cx.val("gp"))
                cx.inc("dv", ln_apply(o_facc[:, 1024 * i:1024 * (i + 1)], h2t[:]))
                dv_h3[i] = cx.val("dv")
            for i in sorted(dv_h3):
                cx.wait(te, "dv", dv_h3[i])
                transp_split(o_facc, lambda c, _i=i: 1024 * _i + 128 * c, NCH,
                             h3T_h, h3T_l, lambda c, _i=i: 512 * c + 128 * _i)
            afree[fa] = cx.val("pe")
            if upto == "ffn1":
                continue

            def w_pass(kind, nmK, nmV, ka, va, pre_s=(), pre_v=(), G=G, t0=t0, half=half):
                scale = 1.0 / 32.0
                kWh, kWl = AR[ka]
                vWh, vWl = AR[va]
                wv = wv_of[(nmK, ka)]
                for nm, idx in pre_s:
                    prefetch(nm, idx)
                lrow = psB[0:1, 0:512]
                cx.wait(te, "dv", cx.val("dv"))
                cx.wait(te, "ac", cx.val("ac"))
                ac_after = {}
                gp_after = {-1: cx.val("gp")}
                for ncx in range(NCH):
                    sp = psA[:, 512 * (ncx % 2):512 * (ncx % 2 + 1)]
                    if ncx >= 2:
                        cx.wait(te, "ac", ac_after[ncx - 2])
                    cx.wait(te, "xd", wv)
                    for cc in range(NCH):
                        last = mm3(sp,
                            kWh[:, 1024 * cc + 128 * ncx:1024 * cc + 128 * (ncx + 1)],
                            kWl[:, 1024 * cc + 128 * ncx:1024 * cc + 128 * (ncx + 1)],
                            h3T_h[:, 512 * cc:512 * (cc + 1)],
                            h3T_l[:, 512 * cc:512 * (cc + 1)],
                            start=(cc == 0), stop=(cc == NCH - 1))
                    cx.inc("pe", last)
                    cx.wait(sc, "pe")
                    if ncx >= 2:
                        sfe, sfv = gp_after[ncx - 2]
                        cx.wait(sc, sfe, sfv)
                    pf = Pf[ncx % 2]
                    cx.inc("ac", sc.activation(pf[:], sp, AF.Exp, scale=scale))
                    ac_after[ncx] = cx.val("ac")
                    cx.wait(ve, "ac")
                    cx.inc("dv", ve.tensor_copy(PT_h[:, 512 * ncx:512 * (ncx + 1)], pf[:]))
                    if ncx % 2 == 0:
                        cx.wait(gp, "dv")
                        cx.inc("gp", gp.tensor_sub(PT_l[:, 512 * ncx:512 * (ncx + 1)], pf[:],
                                                   PT_h[:, 512 * ncx:512 * (ncx + 1)]))
                        gp_after[ncx] = ("gp", cx.val("gp"))
                    else:
                        cx.wait(ve, "dv")
                        cx.inc("dv", ve.tensor_sub(PT_l[:, 512 * ncx:512 * (ncx + 1)], pf[:],
                                                   PT_h[:, 512 * ncx:512 * (ncx + 1)]))
                        gp_after[ncx] = ("dv", cx.val("dv"))
                afree[ka] = cx.val("pe")
                for nm, idx in pre_v:
                    prefetch(nm, idx)
                wvv = wv_of[(nmV, va)]
                cx.wait(te, "gp")
                cx.wait(te, "dv")
                for ncx in range(NCH):
                    te.matmul(lrow, ones_b[:, 0:1], PT_h[:, 512 * ncx:512 * (ncx + 1)],
                              start=(ncx == 0), stop=False)
                    last = te.matmul(lrow, ones_b[:, 0:1], PT_l[:, 512 * ncx:512 * (ncx + 1)],
                                     start=False, stop=(ncx == NCH - 1))
                cx.inc("pe", last)
                cx.wait(ve, "pe")
                cx.inc("dv", ve.reciprocal(rl[0:1, 0:512], lrow))
                if kind in ("q", "k"):
                    rbc = psB[:, 512:1024]
                    cx.wait(te, "dv")
                    cx.wait(te, "gp", cx.val("gp"))
                    cx.inc("pe", te.matmul(rbc, ones_f[0:1, :], rl[0:1, 0:512],
                                           start=True, stop=True))
                    cx.wait(ve, "pe")
                    cx.inc("dv", ve.tensor_copy(Pf[0][:], rbc))
                    rbc_sb = Pf[0][:]
                    cx.wait(te, "xd", wvv)
                    dv_evac = {}
                    for cc in range(NCH):
                        po = psA[:, 1024 + 512 * (cc % 2):1024 + 512 * (cc % 2 + 1)]
                        if cc >= 2:
                            cx.wait(te, "dv", dv_evac[cc - 2])
                        for ncx in range(NCH):
                            last = mm3(po,
                                vWh[:, 1024 * ncx + 128 * cc:1024 * ncx + 128 * (cc + 1)],
                                vWl[:, 1024 * ncx + 128 * cc:1024 * ncx + 128 * (cc + 1)],
                                PT_h[:, 512 * ncx:512 * (ncx + 1)],
                                PT_l[:, 512 * ncx:512 * (ncx + 1)],
                                start=(ncx == 0), stop=(ncx == NCH - 1))
                        cx.inc("pe", last)
                        cx.wait(ve, "pe")
                        cx.wait(ve, "sd", cx.val("sd"))
                        cx.wait(ve, "gd", cx.val("gd"))
                        cx.wait(ve, "dv")
                        cx.inc("dv", ve.tensor_tensor(ev32[:, 0:512], po, rbc_sb, ALU.mult))
                        dv_evac[cc] = cx.val("dv")
                        if kind == "q":
                            cx.wait(sy, "dv")
                            for tau in range(TPG):
                                dma(sy, q_sp[128 * (t0 + tau):128 * (t0 + tau + 1),
                                             128 * cc:128 * (cc + 1)],
                                    ev32[:, 128 * tau:128 * (tau + 1)], sem="sd")
                        else:
                            cx.wait(sc, "dv")
                            cx.wait(sc, "gd", cx.val("gd"))
                            cx.inc("ac", sc.activation(stg_h[:, 0:512], ev32[:, 0:512], AF.Copy))
                            cx.wait(gp, "ac")
                            cx.inc("gp", gp.tensor_sub(stg_l[:, 0:512], ev32[:, 0:512],
                                                       stg_h[:, 0:512]))
                            cx.wait(gp, "gp")
                            for tau in range(TPG):
                                rb0 = 512 * half + 128 * tau
                                dma(gp, cc_out[G][rb0:rb0 + 128,
                                                  128 * cc:128 * (cc + 1)],
                                    stg_h[:, 128 * tau:128 * (tau + 1)], sem="gd")
                                dma(gp, cc_out[G][rb0:rb0 + 128,
                                                  1024 + 128 * cc:1024 + 128 * (cc + 1)],
                                    stg_l[:, 128 * tau:128 * (tau + 1)], sem="gd")
                else:
                    cx.wait(gp, "pe", cx.val("pe"))
                    cx.inc("gp", gp.memset(o_facc[0:128, 0:1040], 1.0))
                    prc = psB[:, 1024:1152]
                    cx.wait(te, "dv")
                    cx.wait(te, "ac", cx.val("ac"))
                    for tau in range(TPG):
                        cx.inc("pe", te.matmul(prc[0:128, tau:tau + 1],
                                               rl[0:1, 128 * tau:128 * (tau + 1)],
                                               ident[0:1, 0:1], is_transpose=True))
                    cx.wait(ve, "pe")
                    for tau in range(TPG):
                        cx.inc("dv", ve.tensor_copy(rl[:, 1024 + tau:1024 + tau + 1],
                                                    prc[0:128, tau:tau + 1]))
                    cx.wait(te, "xd", wvv)
                    dv_vev = {}
                    for tau in range(TPG):
                        po = psA[:, 1024:2048] if tau % 2 else psA[:, 0:1024]
                        if tau < 2:
                            cx.wait(te, "ac", cx.val("ac"))
                        else:
                            cx.wait(te, "dv", dv_vev[tau - 2])
                        for ncx in range(NCH):
                            for hf in range(2):
                                last = mm3(po[:, 512 * hf:512 * (hf + 1)],
                                    PT_h[:, 512 * ncx + 128 * tau:512 * ncx + 128 * (tau + 1)],
                                    PT_l[:, 512 * ncx + 128 * tau:512 * ncx + 128 * (tau + 1)],
                                    vWh[:, 1024 * ncx + 512 * hf:1024 * ncx + 512 * (hf + 1)],
                                    vWl[:, 1024 * ncx + 512 * hf:1024 * ncx + 512 * (hf + 1)],
                                    start=(ncx == 0), stop=(ncx == NCH - 1))
                                if ncx == NCH - 1:
                                    cx.inc("pe", last)
                        cx.wait(ve, "pe")
                        cx.wait(ve, "dv")
                        cx.wait(ve, "gp", cx.val("gp"))
                        vstr = bass.AP(o_facc, 0, [[4096, 128], [65, 16], [1, 64]])
                        cx.inc("dv", ve.tensor_scalar(vstr, po[:, 0:1024],
                                                      rl[:, 1024 + tau:1024 + tau + 1],
                                                      None, ALU.mult))
                        dv_vev[tau] = cx.val("dv")
                        cx.wait(sc, "dv")
                        cx.wait(sc, "gd", cx.val("gd"))
                        cx.inc("ac", sc.activation(stg_h[:, 0:1040], o_facc[:, 0:1040], AF.Copy))
                        cx.wait(gp, "ac")
                        cx.inc("gp", gp.tensor_sub(stg_l[:, 0:1040], o_facc[:, 0:1040],
                                                   stg_h[:, 0:1040]))
                        cx.wait(gp, "gp")
                        rb0 = 512 * half + 128 * tau
                        dma(gp, cc_out[G][rb0:rb0 + 128, 2048:3088],
                            stg_h[:, 0:1040], sem="gd")
                        dma(gp, cc_out[G][rb0:rb0 + 128, 3088:4128],
                            stg_l[:, 0:1040], sem="gd")

                afree[va] = cx.val("pe")

            nxt = ("qkT" if half == 0 else "wbT")
            w_pass("k", "kkT", "kv", 1 - fa, fa,
                   pre_s=[("kv", fa)], pre_v=[("vkT", 1 - fa)])
            w_pass("v", "vkT", "vv", 1 - fa, fa,
                   pre_s=[("vv", fa)], pre_v=[(nxt, 1 - fa)])
            if half == 0:
                w_pass("q", "qkT", "qv", 1 - fa, fa,
                       pre_s=[("qv", fa)], pre_v=[("wbT", 1 - fa)])

        barrier()
        if upto in ("ffn1", "phaseA"):
            return nc, cx, locals()

        def pcol(h, par):
            return 2048 * par + 128 * (h // 2) + 1024 * (h % 2)

        dv_hi = {}
        gp_lo = {}
        ac_f32 = {}
        slot_pe = {0: 0, 1: 0, 2: 0}
        ev_state = {}

        def emit_av(i, np_):
            par = i % 2
            bi = i % 3
            vb = 3072 + 1040 * bi
            cx.wait(te, "dv", dv_hi[i])
            for h in range(0, H, 2):
                pc = pcol(h, par)
                ob = psB[0:65, 128 * h:128 * (h + 1)]
                te.matmul(ob, wb_h[:, vb + 65 * h:vb + 65 * (h + 1)],
                          PT_h[:, pc:pc + 128],
                          start=(i == 0 and h % 4 == 0), stop=False)
                te.matmul(ob, wb_l[:, vb + 65 * h:vb + 65 * (h + 1)],
                          PT_h[:, pc:pc + 128], start=False, stop=False)
                te.matmul(ob, wb_h[:, vb + 65 * h:vb + 65 * (h + 1)],
                          PT_l[:, pc:pc + 128], start=False, stop=False)
            cx.inc("pe", te.nop())
            cx.wait(te, "gp", gp_lo[i])
            for h in range(1, H, 2):
                pc = pcol(h, par)
                ob = psB[0:65, 128 * h:128 * (h + 1)]
                te.matmul(ob, wb_h[:, vb + 65 * h:vb + 65 * (h + 1)],
                          PT_h[:, pc:pc + 128], start=False, stop=False)
                te.matmul(ob, wb_l[:, vb + 65 * h:vb + 65 * (h + 1)],
                          PT_h[:, pc:pc + 128], start=False, stop=False)
                last = te.matmul(ob, wb_h[:, vb + 65 * h:vb + 65 * (h + 1)],
                                 PT_l[:, pc:pc + 128], start=False,
                                 stop=(i == np_ - 1 and h % 4 == 3))
            cx.inc("pe", last)
            slot_pe[bi] = cx.val("pe")

        def emit_evac(jp, np_):
            te.wait_ge(cx.sem("pe"), cx.val("pe"))
            emit_av(np_ - 1, np_)
            te.wait_ge(cx.sem("pe"), cx.val("pe"))
            cx.wait(ve, "pe")
            cx.inc("dv", ve.tensor_copy(rl[64:65, 0:2048], psB[64:65, 0:2048]))
            dv_rec = cx.val("dv")
            cx.wait(te, "ac", ac_f32[np_ - 1])
            cx.wait(te, "dv", dv_rec)
            for h in range(H):
                cx.inc("pe", te.matmul(psA[0:64, 128 * h:128 * (h + 1)],
                          ones_f[64:65, 0:64], rl[64:65, 128 * h:128 * (h + 1)],
                          start=True, stop=True, tile_position=(64, 0)))
            te.wait_ge(cx.sem("pe"), cx.val("pe"))
            cx.wait(ve, "pe")
            cx.wait(ve, "gd", ev_state.get("gd_odma", 0))
            cx.inc("dv", ve.tensor_copy(o_facc[0:64, 0:2048], psA[0:64, 0:2048]))
            dv_copy = cx.val("dv")
            cx.wait(ve, "dv")
            cx.inc("dv", ve.reciprocal(o_facc[0:64, 0:2048], o_facc[0:64, 0:2048]))
            cx.wait(ve, "dv")
            for h in range(H):
                ob = ht if h < 8 else h2t
                ve.tensor_tensor(ob[0:64, 128 * (h % 8):128 * (h % 8 + 1)],
                                 psB[0:64, 128 * h:128 * (h + 1)],
                                 o_facc[0:64, 128 * h:128 * (h + 1)], ALU.mult)
            cx.inc("dv", ve.engine_nop())
            dv_mult = cx.val("dv")
            cx.wait(gp, "dv", dv_mult)
            for c in range(NCH):
                ob = ht if c < 4 else h2t
                srcb = bass.AP(ob, 256 * (c % 4), [[1024, 64], [128, 2], [1, 128]])
                dst = bass.AP(o_sp.tensor, (128 * jp) * D + 128 * c,
                              [[D, 64], [64 * D, 2], [1, 128]])
                dma(gp, dst, srcb, sem="gd")
            ev_state["gd_odma"] = cx.val("gd")
            ev_state["dv_copy"] = dv_copy

        for j in range(NT if max_j is None else max_j):
            cx.wait(sy, "pe", cx.val("pe"))
            cx.wait(sy, "sd")
            dma(sy, stg[:, 0:1024], q_sp[128 * j:128 * (j + 1), 0:1024])
            cx.wait(gp, "gd", cx.val("gd"))
            dma(gp, wa_h[:, 0:1024], q_sp[128 * j:128 * (j + 1), 0:1024], sem="gd")
            cx.wait(gp, "xd")
            cx.wait(gp, "gd")
            cx.inc("gp", gp.tensor_sub(wa_l[:, 0:1024], stg[:, 0:1024], wa_h[:, 0:1024]))
            gp_q = cx.val("gp")
            cx.wait(sc, "sd", cx.val("sd"))
            npair = 2 * j + 2
            if sdpa_stage < 2:
                continue
            if sdpa_stage >= 5 and j >= 1:
                emit_evac(j - 1, npair - 2)

            for i in range(npair):
                par = i % 2
                bi = i % 3
                Gk, rb0 = ccpos(i)
                gate = slot_pe[bi]
                cx.wait(sy, "pe", gate)
                dma(sy, wb_h[:, 1024 * bi:1024 * (bi + 1)],
                    cc_out[Gk][rb0:rb0 + 128, 0:1024])
                dma(sy, wb_h[:, 3072 + 1040 * bi:3072 + 1040 * (bi + 1)],
                    cc_out[Gk][rb0:rb0 + 128, 2048:3088])
                cx.wait(sc, "pe", gate)
                dma(sc, wb_l[:, 1024 * bi:1024 * (bi + 1)],
                    cc_out[Gk][rb0:rb0 + 128, 1024:2048], sem="vd")
                dma(sc, wb_l[:, 3072 + 1040 * bi:3072 + 1040 * (bi + 1)],
                    cc_out[Gk][rb0:rb0 + 128, 3088:4128], sem="vd")
                kvld = cx.val("xd")
                kvld_v = cx.val("vd")
                cx.wait(te, "ac", ac_f32[i - 1] if i >= 1 else cx.val("ac"))
                if i == 0 and j >= 1 and sdpa_stage >= 5:
                    cx.wait(te, "dv", ev_state["dv_copy"])
                cx.wait(te, "xd", kvld)
                cx.wait(te, "vd", kvld_v)
                cx.wait(te, "gp", gp_q)
                for c in range(NCH):
                    for sub in range(2):
                        h = 2 * c + sub
                        tp = (64 * sub, 0)
                        kcol = 1024 * bi + 128 * c
                        qcol = 128 * c
                        pc = pcol(h, 0)
                        cx.inc("pe", mm3(psA[:, pc:pc + 128],
                            wb_h[64 * sub:64 * (sub + 1), kcol:kcol + 128],
                            wb_l[64 * sub:64 * (sub + 1), kcol:kcol + 128],
                            wa_h[64 * sub:64 * (sub + 1), qcol:qcol + 128],
                            wa_l[64 * sub:64 * (sub + 1), qcol:qcol + 128],
                            start=True, stop=True, tp=tp))
                pe_s = cx.val("pe")
                te.wait_ge(cx.sem("pe"), pe_s)
                if sdpa_stage >= 4 and i >= 1:
                    emit_av(i - 1, npair)
                if sdpa_stage < 3:
                    continue
                if i >= npair - 2:
                    if i == npair - 2:
                        mkap = bass.AP(maskA, 0, [[128, 128], [0, 16], [1, 128]])
                    else:
                        mkap = bass.AP(maskB, 0, [[1, 128], [0, 16], [0, 128]])
                    cx.wait(ve, "pe", pe_s)
                    cx.inc("dv", ve.tensor_tensor(psA[:, 0:2048], psA[:, 0:2048],
                                                  mkap, ALU.add))
                    cx.wait(sc, "dv")
                else:
                    cx.wait(sc, "pe", pe_s)
                po2 = 2048 * par
                pstg = stg[:, 0:2048] if par == 0 else o_facc[:, 2048:4096]
                if i >= 2:
                    cx.wait(sc, "gp", gp_lo[i - 2])
                elif i == 0:
                    cx.wait(sc, "gp", gp_q)
                cx.inc("ac", sc.activation(pstg, psA[:, 0:2048],
                                           AF.Exp, scale=0.125))
                ac_f32[i] = cx.val("ac")
                cx.wait(ve, "ac", ac_f32[i])
                cx.inc("dv", ve.tensor_copy(PT_h[:, po2:po2 + 2048], pstg))
                dv_cast = cx.val("dv")
                cx.wait(ve, "dv")
                cx.inc("dv", ve.tensor_sub(PT_l[:, po2:po2 + 1024], pstg[:, 0:1024],
                                           PT_h[:, po2:po2 + 1024]))
                dv_hi[i] = cx.val("dv")
                cx.wait(gp, "dv", dv_cast)
                cx.inc("gp", gp.tensor_sub(PT_l[:, po2 + 1024:po2 + 2048],
                                           pstg[:, 1024:2048],
                                           PT_h[:, po2 + 1024:po2 + 2048]))
                gp_lo[i] = cx.val("gp")
        if sdpa_stage >= 5:
            NTj = NT if max_j is None else max_j
            if NTj >= 1:
                emit_evac(NTj - 1, 2 * NTj)
        barrier()
        if upto == "sdpa":
            return nc, cx, locals()

        afree[0] = cx.val("pe")
        afree[1] = cx.val("pe")
        for G in range(n_groups):
            pa = G % 2
            ks, vs = 1 - pa, pa
            p2h, p2l = AR[ks]
            v2h, v2l = AR[vs]
            f2h, f2l = AR[ks]
            t0 = G * TPG
            cx.wait(sy, "pe", cx.val("pe"))
            cx.wait(gp, "pe", cx.val("pe"))
            for tau in range(TPG):
                t = t0 + tau
                half = tau % 2
                cx.wait(sy, "dv", cx.val("dv"))
                dma(sy, stg[:, 1024 * half:1024 * (half + 1)],
                    o_sp[128 * t:128 * (t + 1), :])
                dsth = bass.AP(h3T_h, 128 * tau, [[4096, 128], [512, 8], [1, 128]])
                dstl = bass.AP(h3T_l, 128 * tau, [[4096, 128], [512, 8], [1, 128]])
                cx.wait(gp, "dv", cx.val("dv"))
                dma(gp, dsth, o_sp[128 * t:128 * (t + 1), :], sem="gd")
                cx.wait(ve, "xd")
                cx.wait(ve, "gd")
                srcf = bass.AP(stg, 1024 * half, [[2048, 128], [128, 8], [1, 128]])
                cx.inc("dv", ve.tensor_tensor(dstl, srcf, dsth, ALU.subtract))

            if G == 0:
                prefetch("pkT", ks)
            wv = wv_of[("pkT", ks)]
            prefetch("pv", vs)
            wv_pv = wv_of[("pv", vs)]
            lrow = psB[0:1, 0:512]
            cx.wait(te, "dv", cx.val("dv"))
            cx.wait(te, "ac", cx.val("ac"))
            ac_after = {}
            gp_after = {-1: cx.val("gp")}
            for ncx in range(NCH):
                sp = psA[:, 512 * (ncx % 2):512 * (ncx % 2 + 1)]
                if ncx >= 2:
                    cx.wait(te, "ac", ac_after[ncx - 2])
                else:
                    cx.wait(te, "ac", cx.val("ac"))
                cx.wait(te, "xd", wv)
                for cc in range(NCH):
                    last = mm3(sp,
                        p2h[:, 1024 * cc + 128 * ncx:1024 * cc + 128 * (ncx + 1)],
                        p2l[:, 1024 * cc + 128 * ncx:1024 * cc + 128 * (ncx + 1)],
                        h3T_h[:, 512 * cc:512 * (cc + 1)],
                        h3T_l[:, 512 * cc:512 * (cc + 1)],
                        start=(cc == 0), stop=(cc == NCH - 1))
                cx.inc("pe", last)
                cx.wait(sc, "pe")
                if ncx >= 2:
                    sfe, sfv = gp_after[ncx - 2]
                    cx.wait(sc, sfe, sfv)
                pf = Pf[ncx % 2]
                cx.inc("ac", sc.activation(pf[:], sp, AF.Exp, scale=1.0 / 32.0))
                ac_after[ncx] = cx.val("ac")
                cx.wait(ve, "ac")
                cx.inc("dv", ve.tensor_copy(PT_h[:, 512 * ncx:512 * (ncx + 1)], pf[:]))
                if ncx % 2 == 0:
                    cx.wait(gp, "dv")
                    cx.inc("gp", gp.tensor_sub(PT_l[:, 512 * ncx:512 * (ncx + 1)], pf[:],
                                               PT_h[:, 512 * ncx:512 * (ncx + 1)]))
                    gp_after[ncx] = ("gp", cx.val("gp"))
                else:
                    cx.wait(ve, "dv")
                    cx.inc("dv", ve.tensor_sub(PT_l[:, 512 * ncx:512 * (ncx + 1)], pf[:],
                                               PT_h[:, 512 * ncx:512 * (ncx + 1)]))
                    gp_after[ncx] = ("dv", cx.val("dv"))
            cx.wait(te, "gp")
            cx.wait(te, "dv")
            for ncx in range(NCH):
                te.matmul(lrow, ones_b[:, 0:1], PT_h[:, 512 * ncx:512 * (ncx + 1)],
                          start=(ncx == 0), stop=False)
                last = te.matmul(lrow, ones_b[:, 0:1], PT_l[:, 512 * ncx:512 * (ncx + 1)],
                                 start=False, stop=(ncx == NCH - 1))
            cx.inc("pe", last)
            cx.wait(ve, "pe")
            cx.inc("dv", ve.reciprocal(rl[0:1, 0:512], lrow))
            prc = psB[:, 1024:1152]
            cx.wait(te, "dv")
            cx.wait(te, "ac", cx.val("ac"))
            for tau in range(TPG):
                cx.inc("pe", te.matmul(prc[0:128, tau:tau + 1],
                                       rl[0:1, 128 * tau:128 * (tau + 1)],
                                       ident[0:1, 0:1], is_transpose=True))
            cx.wait(ve, "pe")
            for tau in range(TPG):
                cx.inc("dv", ve.tensor_copy(rl[:, 1024 + tau:1024 + tau + 1],
                                            prc[0:128, tau:tau + 1]))
            afree[ks] = cx.val("pe")
            prefetch("fkT", ks, c0=0, c1=1024)
            cx.wait(te, "xd", wv_pv)
            dv_pv = {}
            dv_h4 = {}
            for tau in range(TPG):
                t = t0 + tau
                po = psA[:, 1024:2048] if tau % 2 == 0 else psA[:, 0:1024]
                cx.wait(sy, "dv", cx.val("dv"))
                cx.wait(sy, "pe", cx.val("pe"))
                cx.wait(sy, "ac", cx.val("ac"))
                dma(sy, stg[:, 0:1024], h2_sp[128 * t:128 * (t + 1), :])
                dma(sy, xt[tau][:], x1_sp[128 * t:128 * (t + 1), :])
                h2v = cx.val("xd")
                if tau >= 2:
                    cx.wait(te, "dv", dv_pv[tau - 2])
                elif tau == 1:
                    cx.wait(te, "ac", cx.val("ac"))
                for ncx in range(NCH):
                    for hf in range(2):
                        last = mm3(po[:, 512 * hf:512 * (hf + 1)],
                            PT_h[:, 512 * ncx + 128 * tau:512 * ncx + 128 * (tau + 1)],
                            PT_l[:, 512 * ncx + 128 * tau:512 * ncx + 128 * (tau + 1)],
                            v2h[:, 1024 * ncx + 512 * hf:1024 * ncx + 512 * (hf + 1)],
                            v2l[:, 1024 * ncx + 512 * hf:1024 * ncx + 512 * (hf + 1)],
                            start=(ncx == 0), stop=(ncx == NCH - 1))
                        if ncx == NCH - 1:
                            cx.inc("pe", last)
                cx.wait(ve, "pe")
                cx.wait(ve, "xd", h2v)
                cx.wait(ve, "dv")
                cx.wait(ve, "sd", cx.val("sd"))
                cx.inc("dv", ve.scalar_tensor_tensor(ht[:], po[:, 0:1024],
                                                     rl[:, 1024 + tau:1024 + tau + 1],
                                                     stg[:, 0:1024], ALU.mult, ALU.add))
                dv_pv[tau] = cx.val("dv")
                cx.wait(ve, "dv")
                cx.inc("dv", ve.tensor_add(xt[tau][:], xt[tau][:], ht[:]))
                ln(ht[:])
                h4b = h2t if tau % 2 == 0 else ht2
                cx.wait(ve, "dv")
                cx.inc("dv", ln_apply(h4b[:], ht[:]))
                dv_h4[tau] = cx.val("dv")
                if tau >= 1:
                    pb = h2t if (tau - 1) % 2 == 0 else ht2
                    cx.wait(te, "dv", dv_h4[tau - 1])
                    transp_split(pb, lambda c: 128 * c, NCH, h4T_h, h4T_l,
                                 lambda c, _t=tau - 1: 512 * c + 128 * _t)

            pb = h2t if (TPG - 1) % 2 == 0 else ht2
            cx.wait(te, "dv", dv_h4[TPG - 1])
            transp_split(pb, lambda c: 128 * c, NCH, h4T_h, h4T_l,
                         lambda c: 512 * c + 128 * (TPG - 1))
            cx.wait(gp, "dv", cx.val("dv"))
            cx.inc("gp", gp.memset(o_facc[:], 0.0))
            lrow = psB[0:1, 0:512]
            afree[vs] = cx.val("pe")
            for Q in range(4):
                wv = wv_of[("fkT", ks)]
                prefetch("fv", vs, r0=1024 * Q, r1=1024 * (Q + 1))
                wv_fv = wv_of[("fv", vs)]
                cx.wait(te, "dv", cx.val("dv"))
                ac_after = {}
                gp_after = {-1: cx.val("gp")}
                for ncx in range(NCH):
                    sp = psA[:, 512 * (ncx % 2):512 * (ncx % 2 + 1)]
                    if ncx >= 2:
                        cx.wait(te, "ac", ac_after[ncx - 2])
                    else:
                        cx.wait(te, "ac", cx.val("ac"))
                    cx.wait(te, "xd", wv)
                    for cc in range(NCH):
                        last = mm3(sp,
                            p2h[:, 1024 * cc + 128 * ncx:1024 * cc + 128 * (ncx + 1)],
                            p2l[:, 1024 * cc + 128 * ncx:1024 * cc + 128 * (ncx + 1)],
                            h4T_h[:, 512 * cc:512 * (cc + 1)],
                            h4T_l[:, 512 * cc:512 * (cc + 1)],
                            start=(cc == 0), stop=(cc == NCH - 1))
                    cx.inc("pe", last)
                    cx.wait(sc, "pe")
                    if ncx >= 2:
                        sfe, sfv = gp_after[ncx - 2]
                        cx.wait(sc, sfe, sfv)
                    pf = Pf[ncx % 2]
                    cx.inc("ac", sc.activation(pf[:], sp, AF.Exp, scale=1.0 / 32.0))
                    ac_after[ncx] = cx.val("ac")
                    cx.wait(ve, "ac")
                    cx.inc("dv", ve.tensor_copy(PT_h[:, 512 * ncx:512 * (ncx + 1)], pf[:]))
                    if ncx % 2 == 0:
                        cx.wait(gp, "dv")
                        cx.inc("gp", gp.tensor_sub(PT_l[:, 512 * ncx:512 * (ncx + 1)], pf[:],
                                                   PT_h[:, 512 * ncx:512 * (ncx + 1)]))
                        gp_after[ncx] = ("gp", cx.val("gp"))
                    else:
                        cx.wait(ve, "dv")
                        cx.inc("dv", ve.tensor_sub(PT_l[:, 512 * ncx:512 * (ncx + 1)], pf[:],
                                                   PT_h[:, 512 * ncx:512 * (ncx + 1)]))
                        gp_after[ncx] = ("dv", cx.val("dv"))
                cx.wait(te, "gp")
                cx.wait(te, "dv")
                for ncx in range(NCH):
                    te.matmul(lrow, ones_b[:, 0:1], PT_h[:, 512 * ncx:512 * (ncx + 1)],
                              start=(Q == 0 and ncx == 0), stop=False)
                    last = te.matmul(lrow, ones_b[:, 0:1], PT_l[:, 512 * ncx:512 * (ncx + 1)],
                                     start=False, stop=(Q == 3 and ncx == NCH - 1))
                cx.inc("pe", last)
                afree[ks] = cx.val("pe")
                if Q < 3:
                    prefetch("fkT", ks, c0=1024 * (Q + 1), c1=1024 * (Q + 2))
                else:
                    prefetch("wbT", ks)
                cx.wait(te, "xd", wv_fv)
                dv_fv = {}
                for tau in range(TPG):
                    po = psA[:, 1024:2048] if tau % 2 == 0 else psA[:, 0:1024]
                    if tau >= 2:
                        cx.wait(te, "dv", dv_fv[tau - 2])
                    elif tau == 1:
                        cx.wait(te, "ac", cx.val("ac"))
                    else:
                        cx.wait(te, "dv", cx.val("dv"))
                    for ncx in range(NCH):
                        for hf in range(2):
                            last = mm3(po[:, 512 * hf:512 * (hf + 1)],
                                PT_h[:, 512 * ncx + 128 * tau:512 * ncx + 128 * (tau + 1)],
                                PT_l[:, 512 * ncx + 128 * tau:512 * ncx + 128 * (tau + 1)],
                                v2h[:, 1024 * ncx + 512 * hf:1024 * ncx + 512 * (hf + 1)],
                                v2l[:, 1024 * ncx + 512 * hf:1024 * ncx + 512 * (hf + 1)],
                                start=(ncx == 0), stop=(ncx == NCH - 1))
                            if ncx == NCH - 1:
                                cx.inc("pe", last)
                    cx.wait(ve, "pe")
                    cx.wait(ve, "gp", cx.val("gp"))
                    cx.inc("dv", ve.tensor_add(o_facc[:, 1024 * tau:1024 * (tau + 1)],
                                               o_facc[:, 1024 * tau:1024 * (tau + 1)],
                                               po[:, 0:1024]))
                    dv_fv[tau] = cx.val("dv")
                    cx.wait(ve, "dv")
                afree[vs] = cx.val("pe")
            cx.wait(ve, "pe")
            cx.inc("dv", ve.reciprocal(rl[0:1, 0:512], lrow))
            prc = psB[:, 1024:1152]
            cx.wait(te, "dv")
            for tau in range(TPG):
                cx.inc("pe", te.matmul(prc[0:128, tau:tau + 1],
                                       rl[0:1, 128 * tau:128 * (tau + 1)],
                                       ident[0:1, 0:1], is_transpose=True))
            cx.wait(ve, "pe")
            for tau in range(TPG):
                cx.inc("dv", ve.tensor_copy(rl[:, 1024 + tau:1024 + tau + 1],
                                            prc[0:128, tau:tau + 1]))
            cx.wait(ve, "dv")
            for tau in range(TPG):
                cx.inc("dv", ve.scalar_tensor_tensor(
                    xt[tau][:], o_facc[:, 1024 * tau:1024 * (tau + 1)],
                    rl[:, 1024 + tau:1024 + tau + 1], xt[tau][:], ALU.mult, ALU.add))

            xd0 = wv_of[("wbT", ks)]
            if G + 1 < n_groups:
                prefetch("pkT", vs)
            lnbuf = [(ht, h1T_h, h1T_l), (ht2, h1TB_h, h1TB_l)]
            cx.wait(sc, "xd", xd0)
            cx.wait(sc, "dv", cx.val("dv"))
            ln(xt[0][:], sb=8)
            cx.wait(ve, "dv")
            cx.wait(ve, "sd", cx.val("sd"))
            cx.inc("dv", ln_apply(lnbuf[0][0][:], xt[0][:], sb=8))
            cx.wait(te, "dv")
            transp_split(lnbuf[0][0], lambda c: 128 * c, NCH,
                         lnbuf[0][1], lnbuf[0][2], lambda c: 128 * c)
            dv_h1 = cx.val("dv")
            dv_stt = cx.val("dv")
            for tau in range(TPG):
                t = t0 + tau
                xa = xt[tau][:]
                hb, hTh, hTl = lnbuf[tau % 2]
                pbase = psA[:, 0:1024]
                p16 = psA[:, 1024:1040]
                cx.wait(te, "dv", dv_h1)
                cx.wait(te, "xd", xd0)
                for cc in range(NCH):
                    lh = hTh[:, 128 * cc:128 * (cc + 1)]
                    llo = hTl[:, 128 * cc:128 * (cc + 1)]
                    mm3(p16[:, 0:16], lh, llo, rmsb_h[:, 16 * cc:16 * (cc + 1)],
                        rmsb_l[:, 16 * cc:16 * (cc + 1)], start=(cc == 0), stop=False)
                cx.inc("pe", te.matmul(p16[:, 0:8], ones_f[0:1, :], rbt[:], start=False, stop=True))
                pe_p16 = cx.val("pe")
                cx.wait(te, "dv", dv_stt)
                first = True
                for cc in range(NCH):
                    lh = hTh[:, 128 * cc:128 * (cc + 1)]
                    llo = hTl[:, 128 * cc:128 * (cc + 1)]
                    for hf in range(2):
                        mm3(pbase[:, 512 * hf:512 * (hf + 1)], lh, llo,
                            f2h[:, 1024 * cc + 512 * hf:1024 * cc + 512 * (hf + 1)],
                            f2l[:, 1024 * cc + 512 * hf:1024 * cc + 512 * (hf + 1)],
                            start=first, stop=False)
                    first = False
                te.matmul(pbase[:, 0:512], ones_b[0:1, :], bbp[0:1, 0:512], start=False, stop=False)
                te.matmul(pbase[:, 512:1024], ones_b[0:1, :], bbp[0:1, 512:1024], start=False, stop=False)
                te.matmul(pbase[:, 0:512], ones_b[0:1, :], bbp[0:1, 1024:1536], start=False, stop=False)
                cx.inc("pe", te.matmul(pbase[:, 512:1024], ones_b[0:1, :], bbp[0:1, 1536:2048], start=False, stop=False))
                cx.wait(sc, "pe", pe_p16)
                cx.inc("ac", sc.activation(g8[:], p16[:, 0:8], AF.Exp, accum_out=stat[:, 6:7]))
                cx.wait(ve, "ac")
                cx.inc("dv", ve.reciprocal(stat[:, 7:8], stat[:, 6:7]))
                cx.wait(ve, "dv")
                cx.inc("dv", ve.tensor_scalar(g8b[:], g8[:], stat[:, 7:8], None, ALU.mult))
                cx.wait(ve, "dv")
                cx.inc("dv", ve.tensor_copy(g8[:], g8b[:]))
                cx.wait(ve, "dv")
                for e in range(8):
                    cx.inc("dv", ve.tensor_scalar(P64[:, 8 * e:8 * (e + 1)], p16[:, 8:16],
                                     g8[:, e:e + 1], 2.0, ALU.mult, ALU.mult))
                if tau + 1 < TPG:
                    nb, nTh, nTl = lnbuf[(tau + 1) % 2]
                    cx.wait(sc, "xd", xd0)
                    ln(xt[tau + 1][:], sb=8)
                    cx.wait(ve, "dv")
                    cx.wait(ve, "sd", cx.val("sd"))
                    cx.inc("dv", ln_apply(nb[:], xt[tau + 1][:], sb=8))
                    cx.wait(te, "dv")
                    transp_split(nb, lambda c: 128 * c, NCH, nTh, nTl,
                                 lambda c: 128 * c)
                    dv_h1 = cx.val("dv")
                pt64 = psB[:, 1024:1152]
                cx.wait(te, "dv")
                cx.inc("pe", te.matmul(pt64[0:64, 0:128], P64[:], ident[:], is_transpose=True))
                cx.wait(sc, "pe")
                cx.inc("ac", sc.activation(P64h[:], pt64[0:64, 0:128], AF.Copy))
                cx.wait(ve, "ac")
                cx.inc("dv", ve.tensor_sub(P64l[:], pt64[0:64, 0:128], P64h[:]))
                cx.wait(te, "dv")
                for hf in range(2):
                    cx.inc("pe", mm3(pbase[:, 512 * hf:512 * (hf + 1)], P64h[:], P64l[:],
                        b2h[:, 512 * hf:512 * (hf + 1)], b2l[:, 512 * hf:512 * (hf + 1)],
                        start=False, stop=True))
                cx.wait(ve, "pe")
                cx.wait(ve, "sd", cx.val("sd"))
                cx.inc("dv", ve.scalar_tensor_tensor(ht[:], pbase[:, 0:1024], 0.5, xa,
                                                     ALU.mult, ALU.add))
                dv_stt = cx.val("dv")
                cx.wait(sy, "dv")
                dma(sy, y_out[128 * t:128 * (t + 1), :], ht[:], sem="sd")
            afree[ks] = cx.val("pe")
        barrier()
        return nc, cx, locals()




from concourse.bass_utils import run_bass_kernel_spmd


def _host_inputs(core, x_batch, W):
    import ml_dtypes
    s = core % 2
    NT = 16
    tri16 = np.where(np.arange(128)[:, None] <= np.arange(128)[None, :],
                     0.0, NEG).astype(ml_dtypes.bfloat16)
    xr = x_batch.reshape(NT * 2, 128, 1024)
    return {
        "x": np.ascontiguousarray(xr[s::2].reshape(NT * 128, 1024)),
        "xp": np.ascontiguousarray(xr[1 - s::2].reshape(NT * 128, 1024)),
        "wbT": W["wbT"], "bb": W["bb"], "rb": W["rb"], "rmid": W["rmid"],
        "b2": W["b2"], "qkT": W["qkT"], "qv": W["qv"], "kkT": W["kkT"],
        "kv": W["kv"], "vkT": W["vkT"], "vv": W["vv"], "pkT": W["pkT"],
        "pv": W["pv"], "fkT": W["fkT"], "fv": W["fv"],
        "maskA": tri16,
        "maskB": (np.full((128, 1), NEG, dtype=ml_dtypes.bfloat16) if s == 0
                  else np.zeros((128, 1), dtype=ml_dtypes.bfloat16)),
        "ident": np.eye(128, dtype=np.float32),
    }


def kernel(L0_attn_x, L0_attn_qK, L0_attn_qV, L0_attn_kK, L0_attn_kV,
           L0_attn_vK, L0_attn_vV, L0_attn_pK, L0_attn_pV, L0_attn_fK,
           L0_attn_fV, L0_ffn_base_w, L0_ffn_base_b, L0_ffn_lora_A,
           L0_ffn_lora_B, L0_ffn_router_w, L0_ffn_router_b):
    x = np.ascontiguousarray(np.asarray(L0_attn_x, dtype=np.float32))
    W = {
        "wbT": np.ascontiguousarray(np.asarray(L0_ffn_base_w).T),
        "bb": np.asarray(L0_ffn_base_b).reshape(1, 1024),
        "rb": np.asarray(L0_ffn_router_b).reshape(1, 8),
        "rmid": np.ascontiguousarray(np.concatenate(
            [np.asarray(L0_ffn_router_w).T, np.asarray(L0_ffn_lora_A)], axis=1)),
        "b2": np.ascontiguousarray(np.asarray(L0_ffn_lora_B).reshape(64, 1024)),
        "qkT": np.ascontiguousarray(np.asarray(L0_attn_qK).T),
        "qv": np.ascontiguousarray(np.asarray(L0_attn_qV)),
        "kkT": np.ascontiguousarray(np.asarray(L0_attn_kK).T),
        "kv": np.ascontiguousarray(np.asarray(L0_attn_kV)),
        "vkT": np.ascontiguousarray(np.asarray(L0_attn_vK).T),
        "vv": np.ascontiguousarray(np.asarray(L0_attn_vV)),
        "pkT": np.ascontiguousarray(np.asarray(L0_attn_pK).T),
        "pv": np.ascontiguousarray(np.asarray(L0_attn_pV)),
        "fkT": np.ascontiguousarray(np.asarray(L0_attn_fK).T),
        "fv": np.ascontiguousarray(np.asarray(L0_attn_fV)),
    }
    nc, cx, _ = build(n_groups=4, upto="full")
    in_maps = [_host_inputs(c, x[c // 2], W) for c in range(8)]
    res = run_bass_kernel_spmd(nc, in_maps, core_ids=list(range(8)))
    out = np.empty((4, 4096, 1024), dtype=np.float32)
    for c in range(8):
        b, s = c // 2, c % 2
        y = np.asarray(res.results[c]["y"]).reshape(16, 128, 1024)
        out[b].reshape(32, 128, 1024)[s::2] = y
    return out


# version 23
# speedup vs baseline: 1.0085x; 1.0085x over previous
import contextlib
import numpy as np
import concourse.bass as bass
import concourse.mybir as mybir

F32 = mybir.dt.float32
BF16 = mybir.dt.bfloat16
AF = mybir.ActivationFunctionType
ALU = mybir.AluOpType

D = 1024
NCH = 8
H = 16
TPG = 4
NEG = -1e30
EPS = 1e-5

ENGS = ["sync", "tensor", "scalar", "vector", "gpsimd"]


class Cx:
    DMA_PROD = {"xd": "sync", "sd": "sync", "gd": "gpsimd", "vd": "scalar"}

    def __init__(self, nc, es):
        self.nc = nc
        self.es = es
        self._s = {}
        self._bound = {}

    def _rec(self, n):
        if n not in self._s:
            self._s[n] = [self.es.enter_context(self.nc.semaphore(n)), 0]
            self._bound[n] = 0
        return self._s[n]

    def sem(self, n):
        return self._rec(n)[0]

    def inc(self, n, instr, k=1):
        r = self._rec(n)
        instr.then_inc(r[0], k)
        r[1] += k
        return r[1]

    def val(self, n):
        return self._rec(n)[1]

    def wait(self, eng, n, v=None):
        r = self._rec(n)
        tv = r[1] if v is None else v
        prod = self.DMA_PROD.get(n)
        if prod is not None and tv > self._bound[n]:
            pe_eng = getattr(self.nc, prod)
            if getattr(eng, "engine", None) != getattr(pe_eng, "engine", None):
                pe_eng.wait_ge(r[0], tv)
            self._bound[n] = tv
        eng.wait_ge(r[0], tv)


def build(n_groups=4, upto="full", max_g2=None, max_tiles=None, tile_stage=99, max_j=None, sdpa_stage=99):
    NT = TPG * n_groups
    NTOK = 128 * NT

    nc = bass.Bass("TRN2", target_bir_lowering=False)
    es = contextlib.ExitStack()
    cx = Cx(nc, es)

    x_in = nc.dram_tensor("x", [NTOK, D], F32, kind="ExternalInput").ap()
    xp_in = nc.dram_tensor("xp", [NTOK, D], F32, kind="ExternalInput").ap()
    y_out = nc.dram_tensor("y", [NTOK, D], F32, kind="ExternalOutput").ap()
    w_in = {}
    for nm, shp in [("wbT", [D, D]), ("bb", [1, D]), ("rb", [1, 8]),
                    ("rmid", [D, 16]), ("b2", [64, D]),
                    ("qkT", [D, D]), ("qv", [D, D]), ("kkT", [D, D]), ("kv", [D, D]),
                    ("vkT", [D, D]), ("vv", [D, D]), ("pkT", [D, D]), ("pv", [D, D]),
                    ("fkT", [D, 4 * D]), ("fv", [4 * D, D]),

                    ("ident", [128, 128])]:
        w_in[nm] = nc.dram_tensor(nm, shp, F32, kind="ExternalInput").ap()
    w_in["maskA"] = nc.dram_tensor("maskA", [128, 128], BF16, kind="ExternalInput").ap()
    w_in["maskB"] = nc.dram_tensor("maskB", [128, 1], BF16, kind="ExternalInput").ap()

    wsp = {}
    for nm in ["wbT", "rmid", "qkT", "qv", "kkT", "kv", "vkT", "vv",
               "pkT", "pv", "fkT", "fv"]:
        shp = list(w_in[nm].shape)
        wsp[nm + "_h"] = nc.dram_tensor(nm + "_h", shp, BF16).ap()
        wsp[nm + "_l"] = nc.dram_tensor(nm + "_l", shp, BF16).ap()

    x1_sp = nc.dram_tensor("x1_sp", [NTOK, D], F32).ap()
    h2_sp = nc.dram_tensor("h2_sp", [NTOK, D], F32).ap()
    o_sp = nc.dram_tensor("o_sp", [NTOK, D], F32).ap()
    q_sp = nc.dram_tensor("q_sp", [NTOK, D], F32).ap()
    CCW = 4128
    cc_out = [nc.dram_tensor(f"cc_out{g}", [2 * TPG * 128, CCW], BF16, addr_space="Local").ap()
              for g in range(n_groups)]

    def ccpos(g):
        return g // (2 * TPG), 512 * (g % 2) + 128 * ((g % (2 * TPG)) // 2)

    with es:
        ident = es.enter_context(nc.sbuf_tensor("identt", [128, 128], F32))
        ones_f = es.enter_context(nc.sbuf_tensor("ones_f", [128, 128], F32))
        ones_b = es.enter_context(nc.sbuf_tensor("ones_b", [128, 128], BF16))
        maskA = es.enter_context(nc.sbuf_tensor("maskA_t", [128, 128], BF16))
        maskB = es.enter_context(nc.sbuf_tensor("maskB_t", [128, 1], BF16))
        bbp = es.enter_context(nc.sbuf_tensor("bbp", [1, 2 * D], BF16))
        bbh, bbl = bbp[0:1, 0:D], bbp[0:1, D:2 * D]
        rbt = es.enter_context(nc.sbuf_tensor("rbt", [1, 8], F32))
        b2h = es.enter_context(nc.sbuf_tensor("b2h", [64, D], BF16))
        b2l = es.enter_context(nc.sbuf_tensor("b2l", [64, D], BF16))
        rmsb_h = es.enter_context(nc.sbuf_tensor("rmsb_h", [128, 128], BF16))
        rmsb_l = es.enter_context(nc.sbuf_tensor("rmsb_l", [128, 128], BF16))
        wa_h = es.enter_context(nc.sbuf_tensor("wa_h", [128, NCH * 1024], BF16))
        wa_l = es.enter_context(nc.sbuf_tensor("wa_l", [128, NCH * 1024], BF16))
        wb_h = es.enter_context(nc.sbuf_tensor("wb_h", [128, NCH * 1024], BF16))
        wb_l = es.enter_context(nc.sbuf_tensor("wb_l", [128, NCH * 1024], BF16))
        stg = es.enter_context(nc.sbuf_tensor("stg", [128, 2048], F32))
        stg_h = es.enter_context(nc.sbuf_tensor("stg_h", [128, 2048], BF16))
        stg_l = es.enter_context(nc.sbuf_tensor("stg_l", [128, 2048], BF16))
        h4T_h = es.enter_context(nc.sbuf_tensor("h4T_h", [128, NCH * 512], BF16))
        h4T_l = es.enter_context(nc.sbuf_tensor("h4T_l", [128, NCH * 512], BF16))
        o_facc = es.enter_context(nc.sbuf_tensor("o_facc", [128, 4096], F32))
        h3T_h = es.enter_context(nc.sbuf_tensor("h3T_h", [128, NCH * 512], BF16))
        h3T_l = es.enter_context(nc.sbuf_tensor("h3T_l", [128, NCH * 512], BF16))
        PT_h = es.enter_context(nc.sbuf_tensor("PT_h", [128, NCH * 512], BF16))
        PT_l = es.enter_context(nc.sbuf_tensor("PT_l", [128, NCH * 512], BF16))
        Pf = [es.enter_context(nc.sbuf_tensor(f"Pf{i}", [128, 512], F32)) for i in range(2)]
        xt = [es.enter_context(nc.sbuf_tensor(f"xt{i}", [128, D], F32)) for i in range(TPG)]
        ht = es.enter_context(nc.sbuf_tensor("ht", [128, D], F32))
        ht2 = es.enter_context(nc.sbuf_tensor("ht2", [128, D], F32))
        h1TB_h = es.enter_context(nc.sbuf_tensor("h1TB_h", [128, D], BF16))
        h1TB_l = es.enter_context(nc.sbuf_tensor("h1TB_l", [128, D], BF16))
        h2t = es.enter_context(nc.sbuf_tensor("h2t", [128, D], F32))
        h1T_h = es.enter_context(nc.sbuf_tensor("h1T_h", [128, D], BF16))
        h1T_l = es.enter_context(nc.sbuf_tensor("h1T_l", [128, D], BF16))
        trash = es.enter_context(nc.sbuf_tensor("trash", [128, D], BF16))
        stat = es.enter_context(nc.sbuf_tensor("stat", [128, 16], F32))
        g8 = es.enter_context(nc.sbuf_tensor("g8", [128, 8], F32))
        g8b = es.enter_context(nc.sbuf_tensor("g8b", [128, 8], BF16))
        P64 = es.enter_context(nc.sbuf_tensor("P64", [128, 64], F32))
        P64h = es.enter_context(nc.sbuf_tensor("P64h", [64, 128], BF16))
        P64l = es.enter_context(nc.sbuf_tensor("P64l", [64, 128], BF16))
        ev32 = es.enter_context(nc.sbuf_tensor("ev32", [128, 512], F32))
        rl = es.enter_context(nc.sbuf_tensor("rl", [128, 2048], F32))
        epsb = es.enter_context(nc.sbuf_tensor("epsb", [128, 1], F32))

        psA = es.enter_context(nc.psum_tensor("psA", [128, 2048], F32))
        psB = es.enter_context(nc.psum_tensor("psB", [128, 2048], F32))

        sy, te, sc, ve, gp = nc.sync, nc.tensor, nc.scalar, nc.vector, nc.gpsimd

        def dma(eng, out, in_, sem="xd", cut=True):
            v = cx.inc(sem, eng.dma_start(out=out, in_=in_), 16)
            if cut:
                eng.nop(nofuse=True)
            return v

        def barrier():
            for snm in ("xd", "sd"):
                cx.wait(sy, snm)
            cx.wait(gp, "gd")
            r = cx._rec("bar")
            for e in ENGS:
                getattr(nc, e).sem_inc(r[0], 1)
                r[1] += 1
            v = r[1]
            for e in ENGS:
                getattr(nc, e).wait_ge(r[0], v)

        def mm3(pap, lh, ll, rh, rlo, start, stop, tp=None):
            te.matmul(pap, lh, rh, start=start, stop=False, tile_position=tp)
            te.matmul(pap, lh, rlo, start=False, stop=False, tile_position=tp)
            return te.matmul(pap, ll, rh, start=False, stop=stop, tile_position=tp)

        def ln(src_ap, sb=0):
            s1, sq = stat[:, sb + 2:sb + 3], stat[:, sb + 3:sb + 4]
            mu, rstd = stat[:, sb:sb + 1], stat[:, sb + 1:sb + 2]
            t1, t2 = stat[:, sb + 4:sb + 5], stat[:, sb + 5:sb + 6]
            cx.wait(sc, "dv")
            cx.inc("ac", sc.activation(trash[:, 0:D], src_ap, AF.Copy, accum_out=s1))
            cx.wait(sc, "ac")
            cx.inc("ac", sc.activation(trash[:, 0:D], src_ap, AF.Square, accum_out=sq))
            cx.wait(ve, "ac")
            ve.tensor_scalar(mu, s1, 1.0 / D, None, ALU.mult)
            cx.inc("dv", ve.tensor_scalar(t1, sq, 1.0 / D, None, ALU.mult))
            cx.wait(ve, "dv")
            cx.inc("dv", ve.tensor_tensor(t2, mu, mu, ALU.mult))
            cx.wait(ve, "dv")
            cx.inc("dv", ve.tensor_sub(t1, t1, t2))
            cx.wait(sc, "dv")
            cx.inc("ac", sc.activation(t2, t1, AF.Sqrt, bias=epsb[:, 0:1]))
            cx.wait(ve, "ac")
            cx.inc("dv", ve.reciprocal(rstd, t2))
            cx.wait(ve, "dv")

        def ln_apply(dst_ap, src_ap, sb=0):
            return ve.tensor_scalar(dst_ap, src_ap, stat[:, sb:sb + 1],
                                    stat[:, sb + 1:sb + 2], ALU.subtract, ALU.mult)

        pts_dv = {0: 0, 1: 0}

        def transp_split(src, scol_fn, nch, dsth, dstl, dcol_fn, alt=False):
            entry_dv = cx.val("dv")
            for gi, b0 in enumerate(range(0, nch, 4)):
                nb = min(4, nch - b0)
                if alt and gi % 2 == 1:
                    pts, reg = psB[:, 512:1024], 1
                else:
                    pts, reg = psB[:, 1536:2048], 0
                if alt:
                    cx.wait(te, "dv", max(entry_dv, pts_dv[reg]))
                else:
                    cx.wait(te, "dv", cx.val("dv"))
                for i in range(nb):
                    c = scol_fn(b0 + i)
                    cx.inc("pe", te.matmul(pts[:, 128 * i:128 * (i + 1)], src[:, c:c + 128],
                                           ident[:], is_transpose=True))
                cx.wait(sc, "pe")
                for i in range(nb):
                    d = dcol_fn(b0 + i)
                    cx.inc("ac", sc.activation(dsth[:, d:d + 128],
                                               pts[:, 128 * i:128 * (i + 1)], AF.Copy))
                cx.wait(ve, "ac")
                for i in range(nb):
                    d = dcol_fn(b0 + i)
                    cx.inc("dv", ve.tensor_sub(dstl[:, d:d + 128],
                                               pts[:, 128 * i:128 * (i + 1)],
                                               dsth[:, d:d + 128]))
                pts_dv[reg] = cx.val("dv")


        def load_pair(dsth, dstl, nm, r0, r1, c0, c1, dst_off=0):
            cw = c1 - c0
            for i, rr in enumerate(range(r0, r1, 128)):
                dma(sy, dsth[:, dst_off + i * cw:dst_off + (i + 1) * cw],
                    wsp[nm + "_h"][rr:rr + 128, c0:c1])
                dma(sy, dstl[:, dst_off + i * cw:dst_off + (i + 1) * cw],
                    wsp[nm + "_l"][rr:rr + 128, c0:c1])

        cx.inc("gp", gp.memset(ones_f[:], 1.0))
        cx.inc("gp", gp.memset(epsb[:], EPS))
        cx.inc("gp", gp.memset(ones_b[:], 1.0))
        dma(sy, ident[:], w_in["ident"][:, :])
        dma(sy, maskA[:], w_in["maskA"][:, :])
        dma(sy, maskB[:], w_in["maskB"][:, :])
        dma(sy, rbt[:], w_in["rb"][:, :])
        dma(sy, stg[0:1, 0:D], w_in["bb"][:, :])
        cx.wait(sc, "xd")
        cx.inc("ac", sc.activation(bbh, stg[0:1, 0:D], AF.Copy))
        cx.wait(ve, "ac")
        cx.inc("dv", ve.tensor_sub(bbl, stg[0:1, 0:D], bbh))
        dma(sy, stg[0:64, 1024:1024 + D], w_in["b2"][:, :])
        cx.wait(sc, "xd")
        cx.inc("ac", sc.activation(b2h[:], stg[0:64, 1024:1024 + D], AF.Copy))
        cx.wait(ve, "ac")
        cx.inc("dv", ve.tensor_sub(b2l[:], stg[0:64, 1024:1024 + D], b2h[:]))
        cx.wait(sy, "dv")

        for nm in ["wbT", "rmid", "qkT", "qv", "kkT", "kv", "vkT", "vv",
                   "pkT", "pv", "fkT", "fv"]:
            src = w_in[nm]
            R, C = src.shape
            for r0 in range(0, R, 128):
                for c0 in range(0, C, 2048):
                    cw = min(2048, C - c0)
                    cx.wait(sy, "gd", cx.val("gd"))
                    dma(sy, stg[:, 0:cw], src[r0:r0 + 128, c0:c0 + cw])
                    dma(gp, stg_h[:, 0:cw], src[r0:r0 + 128, c0:c0 + cw], sem="gd")
                    cx.wait(gp, "xd")
                    cx.wait(gp, "gd")
                    cx.inc("gp", gp.tensor_sub(stg_l[:, 0:cw], stg[:, 0:cw], stg_h[:, 0:cw]))
                    cx.wait(gp, "gp")
                    dma(gp, wsp[nm + "_h"][r0:r0 + 128, c0:c0 + cw], stg_h[:, 0:cw], sem="gd")
                    dma(gp, wsp[nm + "_l"][r0:r0 + 128, c0:c0 + cw], stg_l[:, 0:cw], sem="gd")
        barrier()
        if upto == "prologue":
            return nc, cx, locals()

        AR = [(wa_h, wa_l), (wb_h, wb_l)]
        load_pair(rmsb_h, rmsb_l, "rmid", 0, D, 0, 16)
        afree = {0: cx.val("pe"), 1: cx.val("pe")}
        wv_of = {}

        def prefetch(nm, idx, c0=0, c1=None, r0=0, r1=None):
            sh = wsp[nm + "_h"]
            if c1 is None:
                c1 = sh.shape[1]
            if r1 is None:
                r1 = sh.shape[0]
            cx.wait(sy, "pe", afree[idx])
            load_pair(AR[idx][0], AR[idx][1], nm, r0, r1, c0, c1)
            wv_of[(nm, idx)] = cx.val("xd")
            cx.wait(sy, "xd", wv_of[(nm, idx)])

        for G2 in range(2 * n_groups):
            if max_g2 is not None and G2 >= max_g2:
                break
            G, half = G2 // 2, G2 % 2
            spill = (half == 0)
            xsrc = x_in if half == 0 else xp_in
            t0 = G * TPG
            cx.wait(sy, "dv", cx.val("dv"))
            cx.wait(sy, "ac", cx.val("ac"))
            for i in range(TPG):
                dma(sy, xt[i][:], xsrc[128 * (t0 + i):128 * (t0 + i + 1), :])
            xv = cx.val("xd")
            cx.wait(sy, "xd", xv)
            fa = G2 % 2
            fWh, fWl = AR[fa]
            if G2 == 0:
                prefetch("wbT", 0)
            xd0 = wv_of[("wbT", fa)]
            prefetch("kkT", 1 - fa)

            dv_h3 = {}
            lnbuf = [(ht, h1T_h, h1T_l), (ht2, h1TB_h, h1TB_l)]
            ntile = TPG if max_tiles is None else min(TPG, max_tiles)
            cx.wait(sc, "xd", max(xd0, xv))
            ln(xt[0][:], sb=8)
            cx.wait(ve, "dv")
            cx.inc("dv", ln_apply(lnbuf[0][0][:], xt[0][:], sb=8))
            cx.wait(te, "dv")
            transp_split(lnbuf[0][0], lambda c: 128 * c, NCH,
                         lnbuf[0][1], lnbuf[0][2], lambda c: 128 * c, alt=True)
            dv_h1 = cx.val("dv")
            dv_stt = cx.val("dv")
            for i in range(ntile):
                t = t0 + i
                xa = xt[i][:]
                hb, hTh, hTl = lnbuf[i % 2]
                pbase = psA[:, 0:1024]
                p16 = psA[:, 1024:1040]
                cx.wait(te, "dv", dv_h1)
                cx.wait(te, "xd", max(xd0, xv))
                for cc in range(NCH):
                    lh = hTh[:, 128 * cc:128 * (cc + 1)]
                    llo = hTl[:, 128 * cc:128 * (cc + 1)]
                    mm3(p16[:, 0:16], lh, llo, rmsb_h[:, 16 * cc:16 * (cc + 1)],
                        rmsb_l[:, 16 * cc:16 * (cc + 1)], start=(cc == 0), stop=False)
                cx.inc("pe", te.matmul(p16[:, 0:8], ones_f[0:1, :], rbt[:], start=False, stop=True))
                pe_p16 = cx.val("pe")
                cx.wait(te, "dv", dv_stt)
                first = True
                for cc in range(NCH):
                    lh = hTh[:, 128 * cc:128 * (cc + 1)]
                    llo = hTl[:, 128 * cc:128 * (cc + 1)]
                    for hf in range(2):
                        mm3(pbase[:, 512 * hf:512 * (hf + 1)], lh, llo,
                            fWh[:, 1024 * cc + 512 * hf:1024 * cc + 512 * (hf + 1)],
                            fWl[:, 1024 * cc + 512 * hf:1024 * cc + 512 * (hf + 1)],
                            start=first, stop=False)
                    first = False
                te.matmul(pbase[:, 0:512], ones_b[0:1, :], bbp[0:1, 0:512], start=False, stop=False)
                te.matmul(pbase[:, 512:1024], ones_b[0:1, :], bbp[0:1, 512:1024], start=False, stop=False)
                te.matmul(pbase[:, 0:512], ones_b[0:1, :], bbp[0:1, 1024:1536], start=False, stop=False)
                cx.inc("pe", te.matmul(pbase[:, 512:1024], ones_b[0:1, :], bbp[0:1, 1536:2048], start=False, stop=False))
                cx.wait(sc, "pe", pe_p16)
                cx.inc("ac", sc.activation(g8[:], p16[:, 0:8], AF.Exp, accum_out=stat[:, 6:7]))
                cx.wait(ve, "ac")
                cx.inc("dv", ve.reciprocal(stat[:, 7:8], stat[:, 6:7]))
                cx.wait(ve, "dv")
                cx.inc("dv", ve.tensor_scalar(g8b[:], g8[:], stat[:, 7:8], None, ALU.mult))
                cx.wait(ve, "dv")
                cx.inc("dv", ve.tensor_copy(g8[:], g8b[:]))
                cx.wait(ve, "dv")
                for e in range(8):
                    cx.inc("dv", ve.tensor_scalar(P64[:, 8 * e:8 * (e + 1)], p16[:, 8:16],
                                     g8[:, e:e + 1], 2.0, ALU.mult, ALU.mult))
                if i + 1 < ntile:
                    nb, nTh, nTl = lnbuf[(i + 1) % 2]
                    cx.wait(sc, "xd", max(xd0, xv))
                    ln(xt[i + 1][:], sb=8)
                    cx.wait(ve, "dv")
                    cx.inc("dv", ln_apply(nb[:], xt[i + 1][:], sb=8))
                    cx.wait(te, "dv")
                    transp_split(nb, lambda c: 128 * c, NCH, nTh, nTl,
                                 lambda c: 128 * c, alt=True)
                    dv_h1 = cx.val("dv")
                pt64 = psB[:, 1024:1152]
                cx.wait(te, "dv")
                cx.inc("pe", te.matmul(pt64[0:64, 0:128], P64[:], ident[:], is_transpose=True))
                cx.wait(sc, "pe")
                cx.inc("ac", sc.activation(P64h[:], pt64[0:64, 0:128], AF.Copy))
                cx.wait(ve, "ac")
                cx.inc("dv", ve.tensor_sub(P64l[:], pt64[0:64, 0:128], P64h[:]))
                cx.wait(te, "dv")
                for hf in range(2):
                    cx.inc("pe", mm3(pbase[:, 512 * hf:512 * (hf + 1)], P64h[:], P64l[:],
                        b2h[:, 512 * hf:512 * (hf + 1)], b2l[:, 512 * hf:512 * (hf + 1)],
                        start=False, stop=True))
                cx.wait(ve, "pe")
                cx.inc("dv", ve.scalar_tensor_tensor(xa, pbase[:, 0:1024], 0.5, xa,
                                                     ALU.mult, ALU.add))
                dv_stt = cx.val("dv")
                if spill:
                    cx.wait(sy, "dv")
                    dma(sy, x1_sp[128 * t:128 * (t + 1), :], xa, sem="sd")
                ln(xa)
                cx.wait(ve, "dv")
                cx.wait(ve, "sd", cx.val("sd"))
                cx.inc("dv", ln_apply(h2t[:], xa))
                if spill:
                    cx.wait(sy, "dv")
                    dma(sy, h2_sp[128 * t:128 * (t + 1), :], h2t[:], sem="sd")
                ln(h2t[:])
                cx.wait(ve, "dv")
                cx.wait(ve, "gp", cx.val("gp"))
                cx.inc("dv", ln_apply(o_facc[:, 1024 * i:1024 * (i + 1)], h2t[:]))
                dv_h3[i] = cx.val("dv")
            for i in sorted(dv_h3):
                cx.wait(te, "dv", dv_h3[i])
                transp_split(o_facc, lambda c, _i=i: 1024 * _i + 128 * c, NCH,
                             h3T_h, h3T_l, lambda c, _i=i: 512 * c + 128 * _i, alt=True)
            afree[fa] = cx.val("pe")
            if upto == "ffn1":
                continue

            def w_pass(kind, nmK, nmV, ka, va, pre_s=(), pre_v=(), G=G, t0=t0, half=half):
                scale = 1.0 / 32.0
                kWh, kWl = AR[ka]
                vWh, vWl = AR[va]
                wv = wv_of[(nmK, ka)]
                for nm, idx in pre_s:
                    prefetch(nm, idx)
                lrow = psB[0:1, 0:512]
                cx.wait(te, "dv", cx.val("dv"))
                cx.wait(te, "ac", cx.val("ac"))
                ac_after = {}
                gp_after = {-1: cx.val("gp")}
                for ncx in range(NCH):
                    sp = psA[:, 512 * (ncx % 2):512 * (ncx % 2 + 1)]
                    if ncx >= 2:
                        cx.wait(te, "ac", ac_after[ncx - 2])
                    cx.wait(te, "xd", wv)
                    for cc in range(NCH):
                        last = mm3(sp,
                            kWh[:, 1024 * cc + 128 * ncx:1024 * cc + 128 * (ncx + 1)],
                            kWl[:, 1024 * cc + 128 * ncx:1024 * cc + 128 * (ncx + 1)],
                            h3T_h[:, 512 * cc:512 * (cc + 1)],
                            h3T_l[:, 512 * cc:512 * (cc + 1)],
                            start=(cc == 0), stop=(cc == NCH - 1))
                    cx.inc("pe", last)
                    cx.wait(sc, "pe")
                    if ncx >= 2:
                        sfe, sfv = gp_after[ncx - 2]
                        cx.wait(sc, sfe, sfv)
                    pf = Pf[ncx % 2]
                    cx.inc("ac", sc.activation(pf[:], sp, AF.Exp, scale=scale))
                    ac_after[ncx] = cx.val("ac")
                    cx.wait(ve, "ac")
                    cx.inc("dv", ve.tensor_copy(PT_h[:, 512 * ncx:512 * (ncx + 1)], pf[:]))
                    if ncx % 2 == 0:
                        cx.wait(gp, "dv")
                        cx.inc("gp", gp.tensor_sub(PT_l[:, 512 * ncx:512 * (ncx + 1)], pf[:],
                                                   PT_h[:, 512 * ncx:512 * (ncx + 1)]))
                        gp_after[ncx] = ("gp", cx.val("gp"))
                    else:
                        cx.wait(ve, "dv")
                        cx.inc("dv", ve.tensor_sub(PT_l[:, 512 * ncx:512 * (ncx + 1)], pf[:],
                                                   PT_h[:, 512 * ncx:512 * (ncx + 1)]))
                        gp_after[ncx] = ("dv", cx.val("dv"))
                afree[ka] = cx.val("pe")
                for nm, idx in pre_v:
                    prefetch(nm, idx)
                wvv = wv_of[(nmV, va)]
                cx.wait(te, "gp")
                cx.wait(te, "dv")
                for ncx in range(NCH):
                    te.matmul(lrow, ones_b[:, 0:1], PT_h[:, 512 * ncx:512 * (ncx + 1)],
                              start=(ncx == 0), stop=False)
                    last = te.matmul(lrow, ones_b[:, 0:1], PT_l[:, 512 * ncx:512 * (ncx + 1)],
                                     start=False, stop=(ncx == NCH - 1))
                cx.inc("pe", last)
                cx.wait(ve, "pe")
                cx.inc("dv", ve.reciprocal(rl[0:1, 0:512], lrow))
                if kind in ("q", "k"):
                    rbc = psB[:, 512:1024]
                    cx.wait(te, "dv")
                    cx.wait(te, "gp", cx.val("gp"))
                    cx.inc("pe", te.matmul(rbc, ones_f[0:1, :], rl[0:1, 0:512],
                                           start=True, stop=True))
                    cx.wait(ve, "pe")
                    cx.inc("dv", ve.tensor_copy(Pf[0][:], rbc))
                    rbc_sb = Pf[0][:]
                    cx.wait(te, "xd", wvv)
                    dv_evac = {}
                    for cc in range(NCH):
                        po = psA[:, 1024 + 512 * (cc % 2):1024 + 512 * (cc % 2 + 1)]
                        if cc >= 2:
                            cx.wait(te, "dv", dv_evac[cc - 2])
                        for ncx in range(NCH):
                            last = mm3(po,
                                vWh[:, 1024 * ncx + 128 * cc:1024 * ncx + 128 * (cc + 1)],
                                vWl[:, 1024 * ncx + 128 * cc:1024 * ncx + 128 * (cc + 1)],
                                PT_h[:, 512 * ncx:512 * (ncx + 1)],
                                PT_l[:, 512 * ncx:512 * (ncx + 1)],
                                start=(ncx == 0), stop=(ncx == NCH - 1))
                        cx.inc("pe", last)
                        cx.wait(ve, "pe")
                        cx.wait(ve, "sd", cx.val("sd"))
                        cx.wait(ve, "gd", cx.val("gd"))
                        cx.wait(ve, "dv")
                        cx.inc("dv", ve.tensor_tensor(ev32[:, 0:512], po, rbc_sb, ALU.mult))
                        dv_evac[cc] = cx.val("dv")
                        if kind == "q":
                            cx.wait(sy, "dv")
                            for tau in range(TPG):
                                dma(sy, q_sp[128 * (t0 + tau):128 * (t0 + tau + 1),
                                             128 * cc:128 * (cc + 1)],
                                    ev32[:, 128 * tau:128 * (tau + 1)], sem="sd")
                        else:
                            cx.wait(sc, "dv")
                            cx.wait(sc, "gd", cx.val("gd"))
                            cx.inc("ac", sc.activation(stg_h[:, 0:512], ev32[:, 0:512], AF.Copy))
                            cx.wait(gp, "ac")
                            cx.inc("gp", gp.tensor_sub(stg_l[:, 0:512], ev32[:, 0:512],
                                                       stg_h[:, 0:512]))
                            cx.wait(gp, "gp")
                            for tau in range(TPG):
                                rb0 = 512 * half + 128 * tau
                                dma(gp, cc_out[G][rb0:rb0 + 128,
                                                  128 * cc:128 * (cc + 1)],
                                    stg_h[:, 128 * tau:128 * (tau + 1)], sem="gd")
                                dma(gp, cc_out[G][rb0:rb0 + 128,
                                                  1024 + 128 * cc:1024 + 128 * (cc + 1)],
                                    stg_l[:, 128 * tau:128 * (tau + 1)], sem="gd")
                else:
                    cx.wait(gp, "pe", cx.val("pe"))
                    cx.inc("gp", gp.memset(o_facc[0:128, 0:1040], 1.0))
                    prc = psB[:, 1024:1152]
                    cx.wait(te, "dv")
                    cx.wait(te, "ac", cx.val("ac"))
                    for tau in range(TPG):
                        cx.inc("pe", te.matmul(prc[0:128, tau:tau + 1],
                                               rl[0:1, 128 * tau:128 * (tau + 1)],
                                               ident[0:1, 0:1], is_transpose=True))
                    cx.wait(ve, "pe")
                    for tau in range(TPG):
                        cx.inc("dv", ve.tensor_copy(rl[:, 1024 + tau:1024 + tau + 1],
                                                    prc[0:128, tau:tau + 1]))
                    cx.wait(te, "xd", wvv)
                    dv_vev = {}
                    for tau in range(TPG):
                        po = psA[:, 1024:2048] if tau % 2 else psA[:, 0:1024]
                        if tau < 2:
                            cx.wait(te, "ac", cx.val("ac"))
                        else:
                            cx.wait(te, "dv", dv_vev[tau - 2])
                        for ncx in range(NCH):
                            for hf in range(2):
                                last = mm3(po[:, 512 * hf:512 * (hf + 1)],
                                    PT_h[:, 512 * ncx + 128 * tau:512 * ncx + 128 * (tau + 1)],
                                    PT_l[:, 512 * ncx + 128 * tau:512 * ncx + 128 * (tau + 1)],
                                    vWh[:, 1024 * ncx + 512 * hf:1024 * ncx + 512 * (hf + 1)],
                                    vWl[:, 1024 * ncx + 512 * hf:1024 * ncx + 512 * (hf + 1)],
                                    start=(ncx == 0), stop=(ncx == NCH - 1))
                                if ncx == NCH - 1:
                                    cx.inc("pe", last)
                        cx.wait(ve, "pe")
                        cx.wait(ve, "dv")
                        cx.wait(ve, "gp", cx.val("gp"))
                        vstr = bass.AP(o_facc, 0, [[4096, 128], [65, 16], [1, 64]])
                        cx.inc("dv", ve.tensor_scalar(vstr, po[:, 0:1024],
                                                      rl[:, 1024 + tau:1024 + tau + 1],
                                                      None, ALU.mult))
                        dv_vev[tau] = cx.val("dv")
                        cx.wait(sc, "dv")
                        cx.wait(sc, "gd", cx.val("gd"))
                        cx.inc("ac", sc.activation(stg_h[:, 0:1040], o_facc[:, 0:1040], AF.Copy))
                        cx.wait(gp, "ac")
                        cx.inc("gp", gp.tensor_sub(stg_l[:, 0:1040], o_facc[:, 0:1040],
                                                   stg_h[:, 0:1040]))
                        cx.wait(gp, "gp")
                        rb0 = 512 * half + 128 * tau
                        dma(gp, cc_out[G][rb0:rb0 + 128, 2048:3088],
                            stg_h[:, 0:1040], sem="gd")
                        dma(gp, cc_out[G][rb0:rb0 + 128, 3088:4128],
                            stg_l[:, 0:1040], sem="gd")

                afree[va] = cx.val("pe")

            nxt = ("qkT" if half == 0 else "wbT")
            w_pass("k", "kkT", "kv", 1 - fa, fa,
                   pre_s=[("kv", fa)], pre_v=[("vkT", 1 - fa)])
            w_pass("v", "vkT", "vv", 1 - fa, fa,
                   pre_s=[("vv", fa)], pre_v=[(nxt, 1 - fa)])
            if half == 0:
                w_pass("q", "qkT", "qv", 1 - fa, fa,
                       pre_s=[("qv", fa)], pre_v=[("wbT", 1 - fa)])

        barrier()
        if upto in ("ffn1", "phaseA"):
            return nc, cx, locals()

        def pcol(h, par):
            return 2048 * par + 128 * (h // 2) + 1024 * (h % 2)

        dv_hi = {}
        gp_lo = {}
        ac_f32 = {}
        slot_pe = {0: 0, 1: 0, 2: 0}
        ev_state = {}

        def emit_av(i, np_):
            par = i % 2
            bi = i % 3
            vb = 3072 + 1040 * bi
            cx.wait(te, "dv", dv_hi[i])
            for h in range(0, H, 2):
                pc = pcol(h, par)
                ob = psB[0:65, 128 * h:128 * (h + 1)]
                te.matmul(ob, wb_h[:, vb + 65 * h:vb + 65 * (h + 1)],
                          PT_h[:, pc:pc + 128],
                          start=(i == 0 and h % 4 == 0), stop=False)
                te.matmul(ob, wb_l[:, vb + 65 * h:vb + 65 * (h + 1)],
                          PT_h[:, pc:pc + 128], start=False, stop=False)
                te.matmul(ob, wb_h[:, vb + 65 * h:vb + 65 * (h + 1)],
                          PT_l[:, pc:pc + 128], start=False, stop=False)
            cx.inc("pe", te.nop())
            cx.wait(te, "gp", gp_lo[i])
            for h in range(1, H, 2):
                pc = pcol(h, par)
                ob = psB[0:65, 128 * h:128 * (h + 1)]
                te.matmul(ob, wb_h[:, vb + 65 * h:vb + 65 * (h + 1)],
                          PT_h[:, pc:pc + 128], start=False, stop=False)
                te.matmul(ob, wb_l[:, vb + 65 * h:vb + 65 * (h + 1)],
                          PT_h[:, pc:pc + 128], start=False, stop=False)
                last = te.matmul(ob, wb_h[:, vb + 65 * h:vb + 65 * (h + 1)],
                                 PT_l[:, pc:pc + 128], start=False,
                                 stop=(i == np_ - 1 and h % 4 == 3))
            cx.inc("pe", last)
            slot_pe[bi] = cx.val("pe")

        def emit_evac(jp, np_):
            te.wait_ge(cx.sem("pe"), cx.val("pe"))
            emit_av(np_ - 1, np_)
            te.wait_ge(cx.sem("pe"), cx.val("pe"))
            cx.wait(ve, "pe")
            cx.inc("dv", ve.tensor_copy(rl[64:65, 0:2048], psB[64:65, 0:2048]))
            dv_rec = cx.val("dv")
            cx.wait(te, "ac", ac_f32[np_ - 1])
            cx.wait(te, "dv", dv_rec)
            for h in range(H):
                cx.inc("pe", te.matmul(psA[0:64, 128 * h:128 * (h + 1)],
                          ones_f[64:65, 0:64], rl[64:65, 128 * h:128 * (h + 1)],
                          start=True, stop=True, tile_position=(64, 0)))
            te.wait_ge(cx.sem("pe"), cx.val("pe"))
            cx.wait(ve, "pe")
            cx.wait(ve, "gd", ev_state.get("gd_odma", 0))
            cx.inc("dv", ve.tensor_copy(o_facc[0:64, 0:2048], psA[0:64, 0:2048]))
            dv_copy = cx.val("dv")
            cx.wait(ve, "dv")
            cx.inc("dv", ve.reciprocal(o_facc[0:64, 0:2048], o_facc[0:64, 0:2048]))
            cx.wait(ve, "dv")
            for h in range(H):
                ob = ht if h < 8 else h2t
                ve.tensor_tensor(ob[0:64, 128 * (h % 8):128 * (h % 8 + 1)],
                                 psB[0:64, 128 * h:128 * (h + 1)],
                                 o_facc[0:64, 128 * h:128 * (h + 1)], ALU.mult)
            cx.inc("dv", ve.engine_nop())
            dv_mult = cx.val("dv")
            cx.wait(gp, "dv", dv_mult)
            for c in range(NCH):
                ob = ht if c < 4 else h2t
                srcb = bass.AP(ob, 256 * (c % 4), [[1024, 64], [128, 2], [1, 128]])
                dst = bass.AP(o_sp.tensor, (128 * jp) * D + 128 * c,
                              [[D, 64], [64 * D, 2], [1, 128]])
                dma(gp, dst, srcb, sem="gd")
            ev_state["gd_odma"] = cx.val("gd")
            ev_state["dv_copy"] = dv_copy

        for j in range(NT if max_j is None else max_j):
            cx.wait(sy, "pe", cx.val("pe"))
            cx.wait(sy, "sd")
            dma(sy, stg[:, 0:1024], q_sp[128 * j:128 * (j + 1), 0:1024])
            cx.wait(gp, "gd", cx.val("gd"))
            dma(gp, wa_h[:, 0:1024], q_sp[128 * j:128 * (j + 1), 0:1024], sem="gd")
            cx.wait(gp, "xd")
            cx.wait(gp, "gd")
            cx.inc("gp", gp.tensor_sub(wa_l[:, 0:1024], stg[:, 0:1024], wa_h[:, 0:1024]))
            gp_q = cx.val("gp")
            cx.wait(sc, "sd", cx.val("sd"))
            npair = 2 * j + 2
            if sdpa_stage < 2:
                continue
            if sdpa_stage >= 5 and j >= 1:
                emit_evac(j - 1, npair - 2)

            for i in range(npair):
                par = i % 2
                bi = i % 3
                Gk, rb0 = ccpos(i)
                gate = slot_pe[bi]
                cx.wait(sy, "pe", gate)
                dma(sy, wb_h[:, 1024 * bi:1024 * (bi + 1)],
                    cc_out[Gk][rb0:rb0 + 128, 0:1024])
                dma(sy, wb_h[:, 3072 + 1040 * bi:3072 + 1040 * (bi + 1)],
                    cc_out[Gk][rb0:rb0 + 128, 2048:3088])
                cx.wait(sc, "pe", gate)
                dma(sc, wb_l[:, 1024 * bi:1024 * (bi + 1)],
                    cc_out[Gk][rb0:rb0 + 128, 1024:2048], sem="vd")
                dma(sc, wb_l[:, 3072 + 1040 * bi:3072 + 1040 * (bi + 1)],
                    cc_out[Gk][rb0:rb0 + 128, 3088:4128], sem="vd")
                kvld = cx.val("xd")
                kvld_v = cx.val("vd")
                cx.wait(te, "ac", ac_f32[i - 1] if i >= 1 else cx.val("ac"))
                if i == 0 and j >= 1 and sdpa_stage >= 5:
                    cx.wait(te, "dv", ev_state["dv_copy"])
                cx.wait(te, "xd", kvld)
                cx.wait(te, "vd", kvld_v)
                cx.wait(te, "gp", gp_q)
                for c in range(NCH):
                    for sub in range(2):
                        h = 2 * c + sub
                        tp = (64 * sub, 0)
                        kcol = 1024 * bi + 128 * c
                        qcol = 128 * c
                        pc = pcol(h, 0)
                        cx.inc("pe", mm3(psA[:, pc:pc + 128],
                            wb_h[64 * sub:64 * (sub + 1), kcol:kcol + 128],
                            wb_l[64 * sub:64 * (sub + 1), kcol:kcol + 128],
                            wa_h[64 * sub:64 * (sub + 1), qcol:qcol + 128],
                            wa_l[64 * sub:64 * (sub + 1), qcol:qcol + 128],
                            start=True, stop=True, tp=tp))
                pe_s = cx.val("pe")
                te.wait_ge(cx.sem("pe"), pe_s)
                if sdpa_stage >= 4 and i >= 1:
                    emit_av(i - 1, npair)
                if sdpa_stage < 3:
                    continue
                if i >= npair - 2:
                    if i == npair - 2:
                        mkap = bass.AP(maskA, 0, [[128, 128], [0, 16], [1, 128]])
                    else:
                        mkap = bass.AP(maskB, 0, [[1, 128], [0, 16], [0, 128]])
                    cx.wait(ve, "pe", pe_s)
                    cx.inc("dv", ve.tensor_tensor(psA[:, 0:2048], psA[:, 0:2048],
                                                  mkap, ALU.add))
                    cx.wait(sc, "dv")
                else:
                    cx.wait(sc, "pe", pe_s)
                po2 = 2048 * par
                pstg = stg[:, 0:2048] if par == 0 else o_facc[:, 2048:4096]
                if i >= 2:
                    cx.wait(sc, "gp", gp_lo[i - 2])
                elif i == 0:
                    cx.wait(sc, "gp", gp_q)
                cx.inc("ac", sc.activation(pstg, psA[:, 0:2048],
                                           AF.Exp, scale=0.125))
                ac_f32[i] = cx.val("ac")
                cx.wait(ve, "ac", ac_f32[i])
                cx.inc("dv", ve.tensor_copy(PT_h[:, po2:po2 + 2048], pstg))
                dv_cast = cx.val("dv")
                cx.wait(ve, "dv")
                cx.inc("dv", ve.tensor_sub(PT_l[:, po2:po2 + 1024], pstg[:, 0:1024],
                                           PT_h[:, po2:po2 + 1024]))
                dv_hi[i] = cx.val("dv")
                cx.wait(gp, "dv", dv_cast)
                cx.inc("gp", gp.tensor_sub(PT_l[:, po2 + 1024:po2 + 2048],
                                           pstg[:, 1024:2048],
                                           PT_h[:, po2 + 1024:po2 + 2048]))
                gp_lo[i] = cx.val("gp")
        if sdpa_stage >= 5:
            NTj = NT if max_j is None else max_j
            if NTj >= 1:
                emit_evac(NTj - 1, 2 * NTj)
        barrier()
        if upto == "sdpa":
            return nc, cx, locals()

        afree[0] = cx.val("pe")
        afree[1] = cx.val("pe")
        for G in range(n_groups):
            pa = G % 2
            ks, vs = 1 - pa, pa
            p2h, p2l = AR[ks]
            v2h, v2l = AR[vs]
            f2h, f2l = AR[ks]
            t0 = G * TPG
            cx.wait(sy, "pe", cx.val("pe"))
            cx.wait(gp, "pe", cx.val("pe"))
            for tau in range(TPG):
                t = t0 + tau
                half = tau % 2
                cx.wait(sy, "dv", cx.val("dv"))
                dma(sy, stg[:, 1024 * half:1024 * (half + 1)],
                    o_sp[128 * t:128 * (t + 1), :])
                dsth = bass.AP(h3T_h, 128 * tau, [[4096, 128], [512, 8], [1, 128]])
                dstl = bass.AP(h3T_l, 128 * tau, [[4096, 128], [512, 8], [1, 128]])
                cx.wait(gp, "dv", cx.val("dv"))
                dma(gp, dsth, o_sp[128 * t:128 * (t + 1), :], sem="gd")
                cx.wait(ve, "xd")
                cx.wait(ve, "gd")
                srcf = bass.AP(stg, 1024 * half, [[2048, 128], [128, 8], [1, 128]])
                cx.inc("dv", ve.tensor_tensor(dstl, srcf, dsth, ALU.subtract))

            if G == 0:
                prefetch("pkT", ks)
            wv = wv_of[("pkT", ks)]
            prefetch("pv", vs)
            wv_pv = wv_of[("pv", vs)]
            lrow = psB[0:1, 0:512]
            cx.wait(te, "dv", cx.val("dv"))
            cx.wait(te, "ac", cx.val("ac"))
            ac_after = {}
            gp_after = {-1: cx.val("gp")}
            for ncx in range(NCH):
                sp = psA[:, 512 * (ncx % 2):512 * (ncx % 2 + 1)]
                if ncx >= 2:
                    cx.wait(te, "ac", ac_after[ncx - 2])
                else:
                    cx.wait(te, "ac", cx.val("ac"))
                cx.wait(te, "xd", wv)
                for cc in range(NCH):
                    last = mm3(sp,
                        p2h[:, 1024 * cc + 128 * ncx:1024 * cc + 128 * (ncx + 1)],
                        p2l[:, 1024 * cc + 128 * ncx:1024 * cc + 128 * (ncx + 1)],
                        h3T_h[:, 512 * cc:512 * (cc + 1)],
                        h3T_l[:, 512 * cc:512 * (cc + 1)],
                        start=(cc == 0), stop=(cc == NCH - 1))
                cx.inc("pe", last)
                cx.wait(sc, "pe")
                if ncx >= 2:
                    sfe, sfv = gp_after[ncx - 2]
                    cx.wait(sc, sfe, sfv)
                pf = Pf[ncx % 2]
                cx.inc("ac", sc.activation(pf[:], sp, AF.Exp, scale=1.0 / 32.0))
                ac_after[ncx] = cx.val("ac")
                cx.wait(ve, "ac")
                cx.inc("dv", ve.tensor_copy(PT_h[:, 512 * ncx:512 * (ncx + 1)], pf[:]))
                if ncx % 2 == 0:
                    cx.wait(gp, "dv")
                    cx.inc("gp", gp.tensor_sub(PT_l[:, 512 * ncx:512 * (ncx + 1)], pf[:],
                                               PT_h[:, 512 * ncx:512 * (ncx + 1)]))
                    gp_after[ncx] = ("gp", cx.val("gp"))
                else:
                    cx.wait(ve, "dv")
                    cx.inc("dv", ve.tensor_sub(PT_l[:, 512 * ncx:512 * (ncx + 1)], pf[:],
                                               PT_h[:, 512 * ncx:512 * (ncx + 1)]))
                    gp_after[ncx] = ("dv", cx.val("dv"))
            cx.wait(te, "gp")
            cx.wait(te, "dv")
            for ncx in range(NCH):
                te.matmul(lrow, ones_b[:, 0:1], PT_h[:, 512 * ncx:512 * (ncx + 1)],
                          start=(ncx == 0), stop=False)
                last = te.matmul(lrow, ones_b[:, 0:1], PT_l[:, 512 * ncx:512 * (ncx + 1)],
                                 start=False, stop=(ncx == NCH - 1))
            cx.inc("pe", last)
            cx.wait(ve, "pe")
            cx.inc("dv", ve.reciprocal(rl[0:1, 0:512], lrow))
            prc = psB[:, 1024:1152]
            cx.wait(te, "dv")
            cx.wait(te, "ac", cx.val("ac"))
            for tau in range(TPG):
                cx.inc("pe", te.matmul(prc[0:128, tau:tau + 1],
                                       rl[0:1, 128 * tau:128 * (tau + 1)],
                                       ident[0:1, 0:1], is_transpose=True))
            cx.wait(ve, "pe")
            for tau in range(TPG):
                cx.inc("dv", ve.tensor_copy(rl[:, 1024 + tau:1024 + tau + 1],
                                            prc[0:128, tau:tau + 1]))
            afree[ks] = cx.val("pe")
            prefetch("fkT", ks, c0=0, c1=1024)
            cx.wait(te, "xd", wv_pv)
            dv_pv = {}
            dv_h4 = {}
            for tau in range(TPG):
                t = t0 + tau
                po = psA[:, 1024:2048] if tau % 2 == 0 else psA[:, 0:1024]
                cx.wait(sy, "dv", cx.val("dv"))
                cx.wait(sy, "pe", cx.val("pe"))
                cx.wait(sy, "ac", cx.val("ac"))
                dma(sy, stg[:, 0:1024], h2_sp[128 * t:128 * (t + 1), :])
                dma(sy, xt[tau][:], x1_sp[128 * t:128 * (t + 1), :])
                h2v = cx.val("xd")
                if tau >= 2:
                    cx.wait(te, "dv", dv_pv[tau - 2])
                elif tau == 1:
                    cx.wait(te, "ac", cx.val("ac"))
                for ncx in range(NCH):
                    for hf in range(2):
                        last = mm3(po[:, 512 * hf:512 * (hf + 1)],
                            PT_h[:, 512 * ncx + 128 * tau:512 * ncx + 128 * (tau + 1)],
                            PT_l[:, 512 * ncx + 128 * tau:512 * ncx + 128 * (tau + 1)],
                            v2h[:, 1024 * ncx + 512 * hf:1024 * ncx + 512 * (hf + 1)],
                            v2l[:, 1024 * ncx + 512 * hf:1024 * ncx + 512 * (hf + 1)],
                            start=(ncx == 0), stop=(ncx == NCH - 1))
                        if ncx == NCH - 1:
                            cx.inc("pe", last)
                cx.wait(ve, "pe")
                cx.wait(ve, "xd", h2v)
                cx.wait(ve, "dv")
                cx.wait(ve, "sd", cx.val("sd"))
                cx.inc("dv", ve.scalar_tensor_tensor(ht[:], po[:, 0:1024],
                                                     rl[:, 1024 + tau:1024 + tau + 1],
                                                     stg[:, 0:1024], ALU.mult, ALU.add))
                dv_pv[tau] = cx.val("dv")
                cx.wait(ve, "dv")
                cx.inc("dv", ve.tensor_add(xt[tau][:], xt[tau][:], ht[:]))
                ln(ht[:])
                h4b = h2t if tau % 2 == 0 else ht2
                cx.wait(ve, "dv")
                cx.inc("dv", ln_apply(h4b[:], ht[:]))
                dv_h4[tau] = cx.val("dv")
                if tau >= 1:
                    pb = h2t if (tau - 1) % 2 == 0 else ht2
                    cx.wait(te, "dv", dv_h4[tau - 1])
                    transp_split(pb, lambda c: 128 * c, NCH, h4T_h, h4T_l,
                                 lambda c, _t=tau - 1: 512 * c + 128 * _t)

            pb = h2t if (TPG - 1) % 2 == 0 else ht2
            cx.wait(te, "dv", dv_h4[TPG - 1])
            transp_split(pb, lambda c: 128 * c, NCH, h4T_h, h4T_l,
                         lambda c: 512 * c + 128 * (TPG - 1))
            cx.wait(gp, "dv", cx.val("dv"))
            cx.inc("gp", gp.memset(o_facc[:], 0.0))
            lrow = psB[0:1, 0:512]
            afree[vs] = cx.val("pe")
            for Q in range(4):
                wv = wv_of[("fkT", ks)]
                prefetch("fv", vs, r0=1024 * Q, r1=1024 * (Q + 1))
                wv_fv = wv_of[("fv", vs)]
                cx.wait(te, "dv", cx.val("dv"))
                ac_after = {}
                gp_after = {-1: cx.val("gp")}
                for ncx in range(NCH):
                    sp = psA[:, 512 * (ncx % 2):512 * (ncx % 2 + 1)]
                    if ncx >= 2:
                        cx.wait(te, "ac", ac_after[ncx - 2])
                    else:
                        cx.wait(te, "ac", cx.val("ac"))
                    cx.wait(te, "xd", wv)
                    for cc in range(NCH):
                        last = mm3(sp,
                            p2h[:, 1024 * cc + 128 * ncx:1024 * cc + 128 * (ncx + 1)],
                            p2l[:, 1024 * cc + 128 * ncx:1024 * cc + 128 * (ncx + 1)],
                            h4T_h[:, 512 * cc:512 * (cc + 1)],
                            h4T_l[:, 512 * cc:512 * (cc + 1)],
                            start=(cc == 0), stop=(cc == NCH - 1))
                    cx.inc("pe", last)
                    cx.wait(sc, "pe")
                    if ncx >= 2:
                        sfe, sfv = gp_after[ncx - 2]
                        cx.wait(sc, sfe, sfv)
                    pf = Pf[ncx % 2]
                    cx.inc("ac", sc.activation(pf[:], sp, AF.Exp, scale=1.0 / 32.0))
                    ac_after[ncx] = cx.val("ac")
                    cx.wait(ve, "ac")
                    cx.inc("dv", ve.tensor_copy(PT_h[:, 512 * ncx:512 * (ncx + 1)], pf[:]))
                    if ncx % 2 == 0:
                        cx.wait(gp, "dv")
                        cx.inc("gp", gp.tensor_sub(PT_l[:, 512 * ncx:512 * (ncx + 1)], pf[:],
                                                   PT_h[:, 512 * ncx:512 * (ncx + 1)]))
                        gp_after[ncx] = ("gp", cx.val("gp"))
                    else:
                        cx.wait(ve, "dv")
                        cx.inc("dv", ve.tensor_sub(PT_l[:, 512 * ncx:512 * (ncx + 1)], pf[:],
                                                   PT_h[:, 512 * ncx:512 * (ncx + 1)]))
                        gp_after[ncx] = ("dv", cx.val("dv"))
                cx.wait(te, "gp")
                cx.wait(te, "dv")
                for ncx in range(NCH):
                    te.matmul(lrow, ones_b[:, 0:1], PT_h[:, 512 * ncx:512 * (ncx + 1)],
                              start=(Q == 0 and ncx == 0), stop=False)
                    last = te.matmul(lrow, ones_b[:, 0:1], PT_l[:, 512 * ncx:512 * (ncx + 1)],
                                     start=False, stop=(Q == 3 and ncx == NCH - 1))
                cx.inc("pe", last)
                afree[ks] = cx.val("pe")
                if Q < 3:
                    prefetch("fkT", ks, c0=1024 * (Q + 1), c1=1024 * (Q + 2))
                else:
                    prefetch("wbT", ks)
                cx.wait(te, "xd", wv_fv)
                dv_fv = {}
                for tau in range(TPG):
                    po = psA[:, 1024:2048] if tau % 2 == 0 else psA[:, 0:1024]
                    if tau >= 2:
                        cx.wait(te, "dv", dv_fv[tau - 2])
                    elif tau == 1:
                        cx.wait(te, "ac", cx.val("ac"))
                    else:
                        cx.wait(te, "dv", cx.val("dv"))
                    for ncx in range(NCH):
                        for hf in range(2):
                            last = mm3(po[:, 512 * hf:512 * (hf + 1)],
                                PT_h[:, 512 * ncx + 128 * tau:512 * ncx + 128 * (tau + 1)],
                                PT_l[:, 512 * ncx + 128 * tau:512 * ncx + 128 * (tau + 1)],
                                v2h[:, 1024 * ncx + 512 * hf:1024 * ncx + 512 * (hf + 1)],
                                v2l[:, 1024 * ncx + 512 * hf:1024 * ncx + 512 * (hf + 1)],
                                start=(ncx == 0), stop=(ncx == NCH - 1))
                            if ncx == NCH - 1:
                                cx.inc("pe", last)
                    cx.wait(ve, "pe")
                    cx.wait(ve, "gp", cx.val("gp"))
                    cx.inc("dv", ve.tensor_add(o_facc[:, 1024 * tau:1024 * (tau + 1)],
                                               o_facc[:, 1024 * tau:1024 * (tau + 1)],
                                               po[:, 0:1024]))
                    dv_fv[tau] = cx.val("dv")
                    cx.wait(ve, "dv")
                afree[vs] = cx.val("pe")
            cx.wait(ve, "pe")
            cx.inc("dv", ve.reciprocal(rl[0:1, 0:512], lrow))
            prc = psB[:, 1024:1152]
            cx.wait(te, "dv")
            for tau in range(TPG):
                cx.inc("pe", te.matmul(prc[0:128, tau:tau + 1],
                                       rl[0:1, 128 * tau:128 * (tau + 1)],
                                       ident[0:1, 0:1], is_transpose=True))
            cx.wait(ve, "pe")
            for tau in range(TPG):
                cx.inc("dv", ve.tensor_copy(rl[:, 1024 + tau:1024 + tau + 1],
                                            prc[0:128, tau:tau + 1]))
            cx.wait(ve, "dv")
            for tau in range(TPG):
                cx.inc("dv", ve.scalar_tensor_tensor(
                    xt[tau][:], o_facc[:, 1024 * tau:1024 * (tau + 1)],
                    rl[:, 1024 + tau:1024 + tau + 1], xt[tau][:], ALU.mult, ALU.add))

            xd0 = wv_of[("wbT", ks)]
            if G + 1 < n_groups:
                prefetch("pkT", vs)
            lnbuf = [(ht, h1T_h, h1T_l), (ht2, h1TB_h, h1TB_l)]
            cx.wait(sc, "xd", xd0)
            cx.wait(sc, "dv", cx.val("dv"))
            ln(xt[0][:], sb=8)
            cx.wait(ve, "dv")
            cx.wait(ve, "sd", cx.val("sd"))
            cx.inc("dv", ln_apply(lnbuf[0][0][:], xt[0][:], sb=8))
            cx.wait(te, "dv")
            transp_split(lnbuf[0][0], lambda c: 128 * c, NCH,
                         lnbuf[0][1], lnbuf[0][2], lambda c: 128 * c, alt=True)
            dv_h1 = cx.val("dv")
            dv_stt = cx.val("dv")
            for tau in range(TPG):
                t = t0 + tau
                xa = xt[tau][:]
                hb, hTh, hTl = lnbuf[tau % 2]
                pbase = psA[:, 0:1024]
                p16 = psA[:, 1024:1040]
                cx.wait(te, "dv", dv_h1)
                cx.wait(te, "xd", xd0)
                for cc in range(NCH):
                    lh = hTh[:, 128 * cc:128 * (cc + 1)]
                    llo = hTl[:, 128 * cc:128 * (cc + 1)]
                    mm3(p16[:, 0:16], lh, llo, rmsb_h[:, 16 * cc:16 * (cc + 1)],
                        rmsb_l[:, 16 * cc:16 * (cc + 1)], start=(cc == 0), stop=False)
                cx.inc("pe", te.matmul(p16[:, 0:8], ones_f[0:1, :], rbt[:], start=False, stop=True))
                pe_p16 = cx.val("pe")
                cx.wait(te, "dv", dv_stt)
                first = True
                for cc in range(NCH):
                    lh = hTh[:, 128 * cc:128 * (cc + 1)]
                    llo = hTl[:, 128 * cc:128 * (cc + 1)]
                    for hf in range(2):
                        mm3(pbase[:, 512 * hf:512 * (hf + 1)], lh, llo,
                            f2h[:, 1024 * cc + 512 * hf:1024 * cc + 512 * (hf + 1)],
                            f2l[:, 1024 * cc + 512 * hf:1024 * cc + 512 * (hf + 1)],
                            start=first, stop=False)
                    first = False
                te.matmul(pbase[:, 0:512], ones_b[0:1, :], bbp[0:1, 0:512], start=False, stop=False)
                te.matmul(pbase[:, 512:1024], ones_b[0:1, :], bbp[0:1, 512:1024], start=False, stop=False)
                te.matmul(pbase[:, 0:512], ones_b[0:1, :], bbp[0:1, 1024:1536], start=False, stop=False)
                cx.inc("pe", te.matmul(pbase[:, 512:1024], ones_b[0:1, :], bbp[0:1, 1536:2048], start=False, stop=False))
                cx.wait(sc, "pe", pe_p16)
                cx.inc("ac", sc.activation(g8[:], p16[:, 0:8], AF.Exp, accum_out=stat[:, 6:7]))
                cx.wait(ve, "ac")
                cx.inc("dv", ve.reciprocal(stat[:, 7:8], stat[:, 6:7]))
                cx.wait(ve, "dv")
                cx.inc("dv", ve.tensor_scalar(g8b[:], g8[:], stat[:, 7:8], None, ALU.mult))
                cx.wait(ve, "dv")
                cx.inc("dv", ve.tensor_copy(g8[:], g8b[:]))
                cx.wait(ve, "dv")
                for e in range(8):
                    cx.inc("dv", ve.tensor_scalar(P64[:, 8 * e:8 * (e + 1)], p16[:, 8:16],
                                     g8[:, e:e + 1], 2.0, ALU.mult, ALU.mult))
                if tau + 1 < TPG:
                    nb, nTh, nTl = lnbuf[(tau + 1) % 2]
                    cx.wait(sc, "xd", xd0)
                    ln(xt[tau + 1][:], sb=8)
                    cx.wait(ve, "dv")
                    cx.wait(ve, "sd", cx.val("sd"))
                    cx.inc("dv", ln_apply(nb[:], xt[tau + 1][:], sb=8))
                    cx.wait(te, "dv")
                    transp_split(nb, lambda c: 128 * c, NCH, nTh, nTl,
                                 lambda c: 128 * c, alt=True)
                    dv_h1 = cx.val("dv")
                pt64 = psB[:, 1024:1152]
                cx.wait(te, "dv")
                cx.inc("pe", te.matmul(pt64[0:64, 0:128], P64[:], ident[:], is_transpose=True))
                cx.wait(sc, "pe")
                cx.inc("ac", sc.activation(P64h[:], pt64[0:64, 0:128], AF.Copy))
                cx.wait(ve, "ac")
                cx.inc("dv", ve.tensor_sub(P64l[:], pt64[0:64, 0:128], P64h[:]))
                cx.wait(te, "dv")
                for hf in range(2):
                    cx.inc("pe", mm3(pbase[:, 512 * hf:512 * (hf + 1)], P64h[:], P64l[:],
                        b2h[:, 512 * hf:512 * (hf + 1)], b2l[:, 512 * hf:512 * (hf + 1)],
                        start=False, stop=True))
                cx.wait(ve, "pe")
                cx.wait(ve, "sd", cx.val("sd"))
                cx.inc("dv", ve.scalar_tensor_tensor(ht[:], pbase[:, 0:1024], 0.5, xa,
                                                     ALU.mult, ALU.add))
                dv_stt = cx.val("dv")
                cx.wait(sy, "dv")
                dma(sy, y_out[128 * t:128 * (t + 1), :], ht[:], sem="sd")
            afree[ks] = cx.val("pe")
        barrier()
        return nc, cx, locals()




from concourse.bass_utils import run_bass_kernel_spmd


def _host_inputs(core, x_batch, W):
    import ml_dtypes
    s = core % 2
    NT = 16
    tri16 = np.where(np.arange(128)[:, None] <= np.arange(128)[None, :],
                     0.0, NEG).astype(ml_dtypes.bfloat16)
    xr = x_batch.reshape(NT * 2, 128, 1024)
    return {
        "x": np.ascontiguousarray(xr[s::2].reshape(NT * 128, 1024)),
        "xp": np.ascontiguousarray(xr[1 - s::2].reshape(NT * 128, 1024)),
        "wbT": W["wbT"], "bb": W["bb"], "rb": W["rb"], "rmid": W["rmid"],
        "b2": W["b2"], "qkT": W["qkT"], "qv": W["qv"], "kkT": W["kkT"],
        "kv": W["kv"], "vkT": W["vkT"], "vv": W["vv"], "pkT": W["pkT"],
        "pv": W["pv"], "fkT": W["fkT"], "fv": W["fv"],
        "maskA": tri16,
        "maskB": (np.full((128, 1), NEG, dtype=ml_dtypes.bfloat16) if s == 0
                  else np.zeros((128, 1), dtype=ml_dtypes.bfloat16)),
        "ident": np.eye(128, dtype=np.float32),
    }


def kernel(L0_attn_x, L0_attn_qK, L0_attn_qV, L0_attn_kK, L0_attn_kV,
           L0_attn_vK, L0_attn_vV, L0_attn_pK, L0_attn_pV, L0_attn_fK,
           L0_attn_fV, L0_ffn_base_w, L0_ffn_base_b, L0_ffn_lora_A,
           L0_ffn_lora_B, L0_ffn_router_w, L0_ffn_router_b):
    x = np.ascontiguousarray(np.asarray(L0_attn_x, dtype=np.float32))
    W = {
        "wbT": np.ascontiguousarray(np.asarray(L0_ffn_base_w).T),
        "bb": np.asarray(L0_ffn_base_b).reshape(1, 1024),
        "rb": np.asarray(L0_ffn_router_b).reshape(1, 8),
        "rmid": np.ascontiguousarray(np.concatenate(
            [np.asarray(L0_ffn_router_w).T, np.asarray(L0_ffn_lora_A)], axis=1)),
        "b2": np.ascontiguousarray(np.asarray(L0_ffn_lora_B).reshape(64, 1024)),
        "qkT": np.ascontiguousarray(np.asarray(L0_attn_qK).T),
        "qv": np.ascontiguousarray(np.asarray(L0_attn_qV)),
        "kkT": np.ascontiguousarray(np.asarray(L0_attn_kK).T),
        "kv": np.ascontiguousarray(np.asarray(L0_attn_kV)),
        "vkT": np.ascontiguousarray(np.asarray(L0_attn_vK).T),
        "vv": np.ascontiguousarray(np.asarray(L0_attn_vV)),
        "pkT": np.ascontiguousarray(np.asarray(L0_attn_pK).T),
        "pv": np.ascontiguousarray(np.asarray(L0_attn_pV)),
        "fkT": np.ascontiguousarray(np.asarray(L0_attn_fK).T),
        "fv": np.ascontiguousarray(np.asarray(L0_attn_fV)),
    }
    nc, cx, _ = build(n_groups=4, upto="full")
    in_maps = [_host_inputs(c, x[c // 2], W) for c in range(8)]
    res = run_bass_kernel_spmd(nc, in_maps, core_ids=list(range(8)))
    out = np.empty((4, 4096, 1024), dtype=np.float32)
    for c in range(8):
        b, s = c // 2, c % 2
        y = np.asarray(res.results[c]["y"]).reshape(16, 128, 1024)
        out[b].reshape(32, 128, 1024)[s::2] = y
    return out


# version 24
# speedup vs baseline: 1.0349x; 1.0262x over previous
import contextlib
import numpy as np
import concourse.bass as bass
import concourse.mybir as mybir

F32 = mybir.dt.float32
BF16 = mybir.dt.bfloat16
AF = mybir.ActivationFunctionType
ALU = mybir.AluOpType

D = 1024
NCH = 8
H = 16
TPG = 4
NEG = -1e30
EPS = 1e-5

ENGS = ["sync", "tensor", "scalar", "vector", "gpsimd"]


class Cx:
    DMA_PROD = {"xd": "sync", "sd": "sync", "gd": "gpsimd", "vd": "scalar"}

    def __init__(self, nc, es):
        self.nc = nc
        self.es = es
        self._s = {}
        self._bound = {}

    def _rec(self, n):
        if n not in self._s:
            self._s[n] = [self.es.enter_context(self.nc.semaphore(n)), 0]
            self._bound[n] = 0
        return self._s[n]

    def sem(self, n):
        return self._rec(n)[0]

    def inc(self, n, instr, k=1):
        r = self._rec(n)
        instr.then_inc(r[0], k)
        r[1] += k
        return r[1]

    def val(self, n):
        return self._rec(n)[1]

    def wait(self, eng, n, v=None):
        r = self._rec(n)
        tv = r[1] if v is None else v
        prod = self.DMA_PROD.get(n)
        if prod is not None and tv > self._bound[n]:
            pe_eng = getattr(self.nc, prod)
            if getattr(eng, "engine", None) != getattr(pe_eng, "engine", None):
                pe_eng.wait_ge(r[0], tv)
            self._bound[n] = tv
        eng.wait_ge(r[0], tv)


def build(n_groups=4, upto="full", max_g2=None, max_tiles=None, tile_stage=99, max_j=None, sdpa_stage=99):
    NT = TPG * n_groups
    NTOK = 128 * NT

    nc = bass.Bass("TRN2", target_bir_lowering=False)
    es = contextlib.ExitStack()
    cx = Cx(nc, es)

    x_in = nc.dram_tensor("x", [NTOK, D], F32, kind="ExternalInput").ap()
    xp_in = nc.dram_tensor("xp", [NTOK, D], F32, kind="ExternalInput").ap()
    y_out = nc.dram_tensor("y", [NTOK, D], F32, kind="ExternalOutput").ap()
    w_in = {}
    for nm, shp in [("wbT", [D, D]), ("bb", [1, D]), ("rb", [1, 8]),
                    ("rmid", [D, 16]), ("b2", [64, D]),
                    ("qkT", [D, D]), ("qv", [D, D]), ("kkT", [D, D]), ("kv", [D, D]),
                    ("vkT", [D, D]), ("vv", [D, D]), ("pkT", [D, D]), ("pv", [D, D]),
                    ("fkT", [D, 4 * D]), ("fv", [4 * D, D]),

                    ("ident", [128, 128])]:
        w_in[nm] = nc.dram_tensor(nm, shp, F32, kind="ExternalInput").ap()
    w_in["maskA"] = nc.dram_tensor("maskA", [128, 128], BF16, kind="ExternalInput").ap()
    w_in["maskB"] = nc.dram_tensor("maskB", [128, 1], BF16, kind="ExternalInput").ap()

    wsp = {}
    for nm in ["wbT", "rmid", "qkT", "qv", "kkT", "kv", "vkT", "vv",
               "pkT", "pv", "fkT", "fv"]:
        shp = list(w_in[nm].shape)
        wsp[nm + "_h"] = nc.dram_tensor(nm + "_h", shp, BF16).ap()
        wsp[nm + "_l"] = nc.dram_tensor(nm + "_l", shp, BF16).ap()

    x1_sp = nc.dram_tensor("x1_sp", [NTOK, D], F32).ap()
    h2_sp = nc.dram_tensor("h2_sp", [NTOK, D], F32).ap()
    o_sp = nc.dram_tensor("o_sp", [NTOK, D], F32).ap()
    q_sp = nc.dram_tensor("q_sp", [NTOK, D], F32).ap()
    CCW = 4128
    cc_out = [nc.dram_tensor(f"cc_out{g}", [2 * TPG * 128, CCW], BF16, addr_space="Local").ap()
              for g in range(n_groups)]

    def ccpos(g):
        return g // (2 * TPG), 512 * (g % 2) + 128 * ((g % (2 * TPG)) // 2)

    with es:
        ident = es.enter_context(nc.sbuf_tensor("identt", [128, 128], F32))
        ones_f = es.enter_context(nc.sbuf_tensor("ones_f", [128, 128], F32))
        ones_b = es.enter_context(nc.sbuf_tensor("ones_b", [128, 128], BF16))
        maskA = es.enter_context(nc.sbuf_tensor("maskA_t", [128, 128], BF16))
        maskB = es.enter_context(nc.sbuf_tensor("maskB_t", [128, 1], BF16))
        bbp = es.enter_context(nc.sbuf_tensor("bbp", [1, 2 * D], BF16))
        bbh, bbl = bbp[0:1, 0:D], bbp[0:1, D:2 * D]
        rbt = es.enter_context(nc.sbuf_tensor("rbt", [1, 8], F32))
        b2h = es.enter_context(nc.sbuf_tensor("b2h", [64, D], BF16))
        b2l = es.enter_context(nc.sbuf_tensor("b2l", [64, D], BF16))
        rmsb_h = es.enter_context(nc.sbuf_tensor("rmsb_h", [128, 128], BF16))
        rmsb_l = es.enter_context(nc.sbuf_tensor("rmsb_l", [128, 128], BF16))
        wa_h = es.enter_context(nc.sbuf_tensor("wa_h", [128, NCH * 1024], BF16))
        wa_l = es.enter_context(nc.sbuf_tensor("wa_l", [128, NCH * 1024], BF16))
        wb_h = es.enter_context(nc.sbuf_tensor("wb_h", [128, NCH * 1024], BF16))
        wb_l = es.enter_context(nc.sbuf_tensor("wb_l", [128, NCH * 1024], BF16))
        stg = es.enter_context(nc.sbuf_tensor("stg", [128, 2048], F32))
        stg_h = es.enter_context(nc.sbuf_tensor("stg_h", [128, 2048], BF16))
        stg_l = es.enter_context(nc.sbuf_tensor("stg_l", [128, 2048], BF16))
        h4T_h = es.enter_context(nc.sbuf_tensor("h4T_h", [128, NCH * 512], BF16))
        h4T_l = es.enter_context(nc.sbuf_tensor("h4T_l", [128, NCH * 512], BF16))
        o_facc = es.enter_context(nc.sbuf_tensor("o_facc", [128, 4096], F32))
        h3T_h = es.enter_context(nc.sbuf_tensor("h3T_h", [128, NCH * 512], BF16))
        h3T_l = es.enter_context(nc.sbuf_tensor("h3T_l", [128, NCH * 512], BF16))
        PT_h = es.enter_context(nc.sbuf_tensor("PT_h", [128, NCH * 512], BF16))
        PT_l = es.enter_context(nc.sbuf_tensor("PT_l", [128, NCH * 512], BF16))
        Pf = [es.enter_context(nc.sbuf_tensor(f"Pf{i}", [128, 512], F32)) for i in range(2)]
        xt = [es.enter_context(nc.sbuf_tensor(f"xt{i}", [128, D], F32)) for i in range(TPG)]
        ht = es.enter_context(nc.sbuf_tensor("ht", [128, D], F32))
        ht2 = es.enter_context(nc.sbuf_tensor("ht2", [128, D], F32))
        h1TB_h = es.enter_context(nc.sbuf_tensor("h1TB_h", [128, D], BF16))
        h1TB_l = es.enter_context(nc.sbuf_tensor("h1TB_l", [128, D], BF16))
        h2t = es.enter_context(nc.sbuf_tensor("h2t", [128, D], F32))
        h1T_h = es.enter_context(nc.sbuf_tensor("h1T_h", [128, D], BF16))
        h1T_l = es.enter_context(nc.sbuf_tensor("h1T_l", [128, D], BF16))
        trash = es.enter_context(nc.sbuf_tensor("trash", [128, D], BF16))
        stat = es.enter_context(nc.sbuf_tensor("stat", [128, 16], F32))
        g8 = es.enter_context(nc.sbuf_tensor("g8", [128, 8], F32))
        g8b = es.enter_context(nc.sbuf_tensor("g8b", [128, 8], BF16))
        P64 = es.enter_context(nc.sbuf_tensor("P64", [128, 64], F32))
        P64h = es.enter_context(nc.sbuf_tensor("P64h", [64, 128], BF16))
        P64l = es.enter_context(nc.sbuf_tensor("P64l", [64, 128], BF16))
        ev32 = es.enter_context(nc.sbuf_tensor("ev32", [128, 512], F32))
        rl = es.enter_context(nc.sbuf_tensor("rl", [128, 2048], F32))
        epsb = es.enter_context(nc.sbuf_tensor("epsb", [128, 1], F32))

        psA = es.enter_context(nc.psum_tensor("psA", [128, 2048], F32))
        psB = es.enter_context(nc.psum_tensor("psB", [128, 2048], F32))

        sy, te, sc, ve, gp = nc.sync, nc.tensor, nc.scalar, nc.vector, nc.gpsimd

        def dma(eng, out, in_, sem="xd", cut=True):
            v = cx.inc(sem, eng.dma_start(out=out, in_=in_), 16)
            if cut:
                eng.nop(nofuse=True)
            return v

        def barrier():
            for snm in ("xd", "sd"):
                cx.wait(sy, snm)
            cx.wait(gp, "gd")
            r = cx._rec("bar")
            for e in ENGS:
                getattr(nc, e).sem_inc(r[0], 1)
                r[1] += 1
            v = r[1]
            for e in ENGS:
                getattr(nc, e).wait_ge(r[0], v)

        def mm3(pap, lh, ll, rh, rlo, start, stop, tp=None):
            te.matmul(pap, lh, rh, start=start, stop=False, tile_position=tp)
            te.matmul(pap, lh, rlo, start=False, stop=False, tile_position=tp)
            return te.matmul(pap, ll, rh, start=False, stop=stop, tile_position=tp)

        def ln(src_ap, sb=0):
            s1, sq = stat[:, sb + 2:sb + 3], stat[:, sb + 3:sb + 4]
            mu, rstd = stat[:, sb:sb + 1], stat[:, sb + 1:sb + 2]
            t1, t2 = stat[:, sb + 4:sb + 5], stat[:, sb + 5:sb + 6]
            cx.wait(sc, "dv")
            cx.inc("ac", sc.activation(trash[:, 0:D], src_ap, AF.Copy, accum_out=s1))
            cx.wait(sc, "ac")
            cx.inc("ac", sc.activation(trash[:, 0:D], src_ap, AF.Square, accum_out=sq))
            cx.wait(ve, "ac")
            ve.tensor_scalar(mu, s1, 1.0 / D, None, ALU.mult)
            cx.inc("dv", ve.tensor_scalar(t1, sq, 1.0 / D, None, ALU.mult))
            cx.wait(ve, "dv")
            cx.inc("dv", ve.tensor_tensor(t2, mu, mu, ALU.mult))
            cx.wait(ve, "dv")
            cx.inc("dv", ve.tensor_sub(t1, t1, t2))
            cx.wait(sc, "dv")
            cx.inc("ac", sc.activation(t2, t1, AF.Sqrt, bias=epsb[:, 0:1]))
            cx.wait(ve, "ac")
            cx.inc("dv", ve.reciprocal(rstd, t2))
            cx.wait(ve, "dv")

        def ln_apply(dst_ap, src_ap, sb=0):
            return ve.tensor_scalar(dst_ap, src_ap, stat[:, sb:sb + 1],
                                    stat[:, sb + 1:sb + 2], ALU.subtract, ALU.mult)

        pts_dv = {0: 0, 1: 0}

        def transp_split(src, scol_fn, nch, dsth, dstl, dcol_fn, alt=False):
            entry_dv = cx.val("dv")
            for gi, b0 in enumerate(range(0, nch, 4)):
                nb = min(4, nch - b0)
                if alt and gi % 2 == 1:
                    pts, reg = psB[:, 512:1024], 1
                else:
                    pts, reg = psB[:, 1536:2048], 0
                if alt:
                    cx.wait(te, "dv", max(entry_dv, pts_dv[reg]))
                else:
                    cx.wait(te, "dv", cx.val("dv"))
                for i in range(nb):
                    c = scol_fn(b0 + i)
                    cx.inc("pe", te.matmul(pts[:, 128 * i:128 * (i + 1)], src[:, c:c + 128],
                                           ident[:], is_transpose=True))
                cx.wait(sc, "pe")
                for i in range(nb):
                    d = dcol_fn(b0 + i)
                    cx.inc("ac", sc.activation(dsth[:, d:d + 128],
                                               pts[:, 128 * i:128 * (i + 1)], AF.Copy))
                cx.wait(ve, "ac")
                for i in range(nb):
                    d = dcol_fn(b0 + i)
                    cx.inc("dv", ve.tensor_sub(dstl[:, d:d + 128],
                                               pts[:, 128 * i:128 * (i + 1)],
                                               dsth[:, d:d + 128]))
                pts_dv[reg] = cx.val("dv")


        def load_pair(dsth, dstl, nm, r0, r1, c0, c1, dst_off=0, qe=None, semn="xd"):
            qe = sy if qe is None else qe
            cw = c1 - c0
            for i, rr in enumerate(range(r0, r1, 128)):
                dma(qe, dsth[:, dst_off + i * cw:dst_off + (i + 1) * cw],
                    wsp[nm + "_h"][rr:rr + 128, c0:c1], sem=semn)
                dma(qe, dstl[:, dst_off + i * cw:dst_off + (i + 1) * cw],
                    wsp[nm + "_l"][rr:rr + 128, c0:c1], sem=semn)

        cx.inc("gp", gp.memset(ones_f[:], 1.0))
        cx.inc("gp", gp.memset(epsb[:], EPS))
        cx.inc("gp", gp.memset(ones_b[:], 1.0))
        dma(sy, ident[:], w_in["ident"][:, :])
        dma(sy, maskA[:], w_in["maskA"][:, :])
        dma(sy, maskB[:], w_in["maskB"][:, :])
        dma(sy, rbt[:], w_in["rb"][:, :])
        dma(sy, stg[0:1, 0:D], w_in["bb"][:, :])
        cx.wait(sc, "xd")
        cx.inc("ac", sc.activation(bbh, stg[0:1, 0:D], AF.Copy))
        cx.wait(ve, "ac")
        cx.inc("dv", ve.tensor_sub(bbl, stg[0:1, 0:D], bbh))
        dma(sy, stg[0:64, 1024:1024 + D], w_in["b2"][:, :])
        cx.wait(sc, "xd")
        cx.inc("ac", sc.activation(b2h[:], stg[0:64, 1024:1024 + D], AF.Copy))
        cx.wait(ve, "ac")
        cx.inc("dv", ve.tensor_sub(b2l[:], stg[0:64, 1024:1024 + D], b2h[:]))
        cx.wait(sy, "dv")

        for nm in ["wbT", "rmid", "qkT", "qv", "kkT", "kv", "vkT", "vv",
                   "pkT", "pv", "fkT", "fv"]:
            src = w_in[nm]
            R, C = src.shape
            for r0 in range(0, R, 128):
                for c0 in range(0, C, 2048):
                    cw = min(2048, C - c0)
                    cx.wait(sy, "gd", cx.val("gd"))
                    dma(sy, stg[:, 0:cw], src[r0:r0 + 128, c0:c0 + cw])
                    dma(gp, stg_h[:, 0:cw], src[r0:r0 + 128, c0:c0 + cw], sem="gd")
                    cx.wait(gp, "xd")
                    cx.wait(gp, "gd")
                    cx.inc("gp", gp.tensor_sub(stg_l[:, 0:cw], stg[:, 0:cw], stg_h[:, 0:cw]))
                    cx.wait(gp, "gp")
                    dma(gp, wsp[nm + "_h"][r0:r0 + 128, c0:c0 + cw], stg_h[:, 0:cw], sem="gd")
                    dma(gp, wsp[nm + "_l"][r0:r0 + 128, c0:c0 + cw], stg_l[:, 0:cw], sem="gd")
        barrier()
        if upto == "prologue":
            return nc, cx, locals()

        AR = [(wa_h, wa_l), (wb_h, wb_l)]
        load_pair(rmsb_h, rmsb_l, "rmid", 0, D, 0, 16)
        afree = {0: cx.val("pe"), 1: cx.val("pe")}
        wv_of = {}

        VNAMES = ("kv", "vv", "qv", "pv", "fv")

        def prefetch(nm, idx, c0=0, c1=None, r0=0, r1=None):
            sh = wsp[nm + "_h"]
            if c1 is None:
                c1 = sh.shape[1]
            if r1 is None:
                r1 = sh.shape[0]
            if nm in VNAMES:
                qe, semn = sc, "vd"
            else:
                qe, semn = sy, "xd"
            cx.wait(qe, "pe", afree[idx])
            load_pair(AR[idx][0], AR[idx][1], nm, r0, r1, c0, c1, qe=qe, semn=semn)
            wv_of[(nm, idx)] = cx.val(semn)
            if semn == "xd":
                cx.wait(sy, "xd", wv_of[(nm, idx)])

        for G2 in range(2 * n_groups):
            if max_g2 is not None and G2 >= max_g2:
                break
            G, half = G2 // 2, G2 % 2
            spill = (half == 0)
            xsrc = x_in if half == 0 else xp_in
            t0 = G * TPG
            cx.wait(sy, "dv", cx.val("dv"))
            cx.wait(sy, "ac", cx.val("ac"))
            for i in range(TPG):
                dma(sy, xt[i][:], xsrc[128 * (t0 + i):128 * (t0 + i + 1), :])
            xv = cx.val("xd")
            cx.wait(sy, "xd", xv)
            fa = G2 % 2
            fWh, fWl = AR[fa]
            if G2 == 0:
                prefetch("wbT", 0)
            xd0 = wv_of[("wbT", fa)]
            prefetch("kkT", 1 - fa)

            dv_h3 = {}
            lnbuf = [(ht, h1T_h, h1T_l), (ht2, h1TB_h, h1TB_l)]
            ntile = TPG if max_tiles is None else min(TPG, max_tiles)
            cx.wait(sc, "xd", max(xd0, xv))
            ln(xt[0][:], sb=8)
            cx.wait(ve, "dv")
            cx.inc("dv", ln_apply(lnbuf[0][0][:], xt[0][:], sb=8))
            cx.wait(te, "dv")
            transp_split(lnbuf[0][0], lambda c: 128 * c, NCH,
                         lnbuf[0][1], lnbuf[0][2], lambda c: 128 * c, alt=True)
            dv_h1 = cx.val("dv")
            dv_stt = cx.val("dv")
            for i in range(ntile):
                t = t0 + i
                xa = xt[i][:]
                hb, hTh, hTl = lnbuf[i % 2]
                pbase = psA[:, 0:1024]
                p16 = psA[:, 1024:1040]
                cx.wait(te, "dv", dv_h1)
                cx.wait(te, "xd", max(xd0, xv))
                for cc in range(NCH):
                    lh = hTh[:, 128 * cc:128 * (cc + 1)]
                    llo = hTl[:, 128 * cc:128 * (cc + 1)]
                    mm3(p16[:, 0:16], lh, llo, rmsb_h[:, 16 * cc:16 * (cc + 1)],
                        rmsb_l[:, 16 * cc:16 * (cc + 1)], start=(cc == 0), stop=False)
                cx.inc("pe", te.matmul(p16[:, 0:8], ones_f[0:1, :], rbt[:], start=False, stop=True))
                pe_p16 = cx.val("pe")
                cx.wait(te, "dv", dv_stt)
                first = True
                for cc in range(NCH):
                    lh = hTh[:, 128 * cc:128 * (cc + 1)]
                    llo = hTl[:, 128 * cc:128 * (cc + 1)]
                    for hf in range(2):
                        mm3(pbase[:, 512 * hf:512 * (hf + 1)], lh, llo,
                            fWh[:, 1024 * cc + 512 * hf:1024 * cc + 512 * (hf + 1)],
                            fWl[:, 1024 * cc + 512 * hf:1024 * cc + 512 * (hf + 1)],
                            start=first, stop=False)
                    first = False
                te.matmul(pbase[:, 0:512], ones_b[0:1, :], bbp[0:1, 0:512], start=False, stop=False)
                te.matmul(pbase[:, 512:1024], ones_b[0:1, :], bbp[0:1, 512:1024], start=False, stop=False)
                te.matmul(pbase[:, 0:512], ones_b[0:1, :], bbp[0:1, 1024:1536], start=False, stop=False)
                cx.inc("pe", te.matmul(pbase[:, 512:1024], ones_b[0:1, :], bbp[0:1, 1536:2048], start=False, stop=False))
                cx.wait(sc, "pe", pe_p16)
                cx.inc("ac", sc.activation(g8[:], p16[:, 0:8], AF.Exp, accum_out=stat[:, 6:7]))
                cx.wait(ve, "ac")
                cx.inc("dv", ve.reciprocal(stat[:, 7:8], stat[:, 6:7]))
                cx.wait(ve, "dv")
                cx.inc("dv", ve.tensor_scalar(g8b[:], g8[:], stat[:, 7:8], None, ALU.mult))
                cx.wait(ve, "dv")
                cx.inc("dv", ve.tensor_copy(g8[:], g8b[:]))
                cx.wait(ve, "dv")
                for e in range(8):
                    cx.inc("dv", ve.tensor_scalar(P64[:, 8 * e:8 * (e + 1)], p16[:, 8:16],
                                     g8[:, e:e + 1], 2.0, ALU.mult, ALU.mult))
                if i + 1 < ntile:
                    nb, nTh, nTl = lnbuf[(i + 1) % 2]
                    cx.wait(sc, "xd", max(xd0, xv))
                    ln(xt[i + 1][:], sb=8)
                    cx.wait(ve, "dv")
                    cx.inc("dv", ln_apply(nb[:], xt[i + 1][:], sb=8))
                    cx.wait(te, "dv")
                    transp_split(nb, lambda c: 128 * c, NCH, nTh, nTl,
                                 lambda c: 128 * c, alt=True)
                    dv_h1 = cx.val("dv")
                pt64 = psB[:, 1024:1152]
                cx.wait(te, "dv")
                cx.inc("pe", te.matmul(pt64[0:64, 0:128], P64[:], ident[:], is_transpose=True))
                cx.wait(sc, "pe")
                cx.inc("ac", sc.activation(P64h[:], pt64[0:64, 0:128], AF.Copy))
                cx.wait(ve, "ac")
                cx.inc("dv", ve.tensor_sub(P64l[:], pt64[0:64, 0:128], P64h[:]))
                cx.wait(te, "dv")
                for hf in range(2):
                    cx.inc("pe", mm3(pbase[:, 512 * hf:512 * (hf + 1)], P64h[:], P64l[:],
                        b2h[:, 512 * hf:512 * (hf + 1)], b2l[:, 512 * hf:512 * (hf + 1)],
                        start=False, stop=True))
                cx.wait(ve, "pe")
                cx.inc("dv", ve.scalar_tensor_tensor(xa, pbase[:, 0:1024], 0.5, xa,
                                                     ALU.mult, ALU.add))
                dv_stt = cx.val("dv")
                if spill:
                    cx.wait(sy, "dv")
                    dma(sy, x1_sp[128 * t:128 * (t + 1), :], xa, sem="sd")
                ln(xa)
                cx.wait(ve, "dv")
                cx.wait(ve, "sd", cx.val("sd"))
                cx.inc("dv", ln_apply(h2t[:], xa))
                if spill:
                    cx.wait(sy, "dv")
                    dma(sy, h2_sp[128 * t:128 * (t + 1), :], h2t[:], sem="sd")
                ln(h2t[:])
                cx.wait(ve, "dv")
                cx.wait(ve, "gp", cx.val("gp"))
                cx.inc("dv", ln_apply(o_facc[:, 1024 * i:1024 * (i + 1)], h2t[:]))
                dv_h3[i] = cx.val("dv")
            for i in sorted(dv_h3):
                cx.wait(te, "dv", dv_h3[i])
                transp_split(o_facc, lambda c, _i=i: 1024 * _i + 128 * c, NCH,
                             h3T_h, h3T_l, lambda c, _i=i: 512 * c + 128 * _i, alt=True)
            afree[fa] = cx.val("pe")
            if upto == "ffn1":
                continue

            def w_pass(kind, nmK, nmV, ka, va, pre_s=(), pre_v=(), G=G, t0=t0, half=half):
                scale = 1.0 / 32.0
                kWh, kWl = AR[ka]
                vWh, vWl = AR[va]
                wv = wv_of[(nmK, ka)]
                for nm, idx in pre_s:
                    prefetch(nm, idx)
                lrow = psB[0:1, 0:512]
                cx.wait(te, "dv", cx.val("dv"))
                cx.wait(te, "ac", cx.val("ac"))
                ac_after = {}
                gp_after = {-1: cx.val("gp")}
                for ncx in range(NCH):
                    sp = psA[:, 512 * (ncx % 2):512 * (ncx % 2 + 1)]
                    if ncx >= 2:
                        cx.wait(te, "ac", ac_after[ncx - 2])
                    cx.wait(te, "xd", wv)
                    for cc in range(NCH):
                        last = mm3(sp,
                            kWh[:, 1024 * cc + 128 * ncx:1024 * cc + 128 * (ncx + 1)],
                            kWl[:, 1024 * cc + 128 * ncx:1024 * cc + 128 * (ncx + 1)],
                            h3T_h[:, 512 * cc:512 * (cc + 1)],
                            h3T_l[:, 512 * cc:512 * (cc + 1)],
                            start=(cc == 0), stop=(cc == NCH - 1))
                    cx.inc("pe", last)
                    cx.wait(sc, "pe")
                    if ncx >= 2:
                        sfe, sfv = gp_after[ncx - 2]
                        cx.wait(sc, sfe, sfv)
                    pf = Pf[ncx % 2]
                    cx.inc("ac", sc.activation(pf[:], sp, AF.Exp, scale=scale))
                    ac_after[ncx] = cx.val("ac")
                    cx.wait(ve, "ac")
                    cx.inc("dv", ve.tensor_copy(PT_h[:, 512 * ncx:512 * (ncx + 1)], pf[:]))
                    if ncx % 2 == 0:
                        cx.wait(gp, "dv")
                        cx.inc("gp", gp.tensor_sub(PT_l[:, 512 * ncx:512 * (ncx + 1)], pf[:],
                                                   PT_h[:, 512 * ncx:512 * (ncx + 1)]))
                        gp_after[ncx] = ("gp", cx.val("gp"))
                    else:
                        cx.wait(ve, "dv")
                        cx.inc("dv", ve.tensor_sub(PT_l[:, 512 * ncx:512 * (ncx + 1)], pf[:],
                                                   PT_h[:, 512 * ncx:512 * (ncx + 1)]))
                        gp_after[ncx] = ("dv", cx.val("dv"))
                afree[ka] = cx.val("pe")
                for nm, idx in pre_v:
                    prefetch(nm, idx)
                wvv = wv_of[(nmV, va)]
                cx.wait(te, "gp")
                cx.wait(te, "dv")
                for ncx in range(NCH):
                    te.matmul(lrow, ones_b[:, 0:1], PT_h[:, 512 * ncx:512 * (ncx + 1)],
                              start=(ncx == 0), stop=False)
                    last = te.matmul(lrow, ones_b[:, 0:1], PT_l[:, 512 * ncx:512 * (ncx + 1)],
                                     start=False, stop=(ncx == NCH - 1))
                cx.inc("pe", last)
                cx.wait(ve, "pe")
                cx.inc("dv", ve.reciprocal(rl[0:1, 0:512], lrow))
                if kind in ("q", "k"):
                    rbc = psB[:, 512:1024]
                    cx.wait(te, "dv")
                    cx.wait(te, "gp", cx.val("gp"))
                    cx.inc("pe", te.matmul(rbc, ones_f[0:1, :], rl[0:1, 0:512],
                                           start=True, stop=True))
                    cx.wait(ve, "pe")
                    cx.inc("dv", ve.tensor_copy(Pf[0][:], rbc))
                    rbc_sb = Pf[0][:]
                    cx.wait(te, "vd", wvv)
                    dv_evac = {}
                    for cc in range(NCH):
                        po = psA[:, 1024 + 512 * (cc % 2):1024 + 512 * (cc % 2 + 1)]
                        if cc >= 2:
                            cx.wait(te, "dv", dv_evac[cc - 2])
                        for ncx in range(NCH):
                            last = mm3(po,
                                vWh[:, 1024 * ncx + 128 * cc:1024 * ncx + 128 * (cc + 1)],
                                vWl[:, 1024 * ncx + 128 * cc:1024 * ncx + 128 * (cc + 1)],
                                PT_h[:, 512 * ncx:512 * (ncx + 1)],
                                PT_l[:, 512 * ncx:512 * (ncx + 1)],
                                start=(ncx == 0), stop=(ncx == NCH - 1))
                        cx.inc("pe", last)
                        cx.wait(ve, "pe")
                        cx.wait(ve, "sd", cx.val("sd"))
                        cx.wait(ve, "gd", cx.val("gd"))
                        cx.wait(ve, "dv")
                        cx.inc("dv", ve.tensor_tensor(ev32[:, 0:512], po, rbc_sb, ALU.mult))
                        dv_evac[cc] = cx.val("dv")
                        if kind == "q":
                            cx.wait(sy, "dv")
                            for tau in range(TPG):
                                dma(sy, q_sp[128 * (t0 + tau):128 * (t0 + tau + 1),
                                             128 * cc:128 * (cc + 1)],
                                    ev32[:, 128 * tau:128 * (tau + 1)], sem="sd")
                        else:
                            cx.wait(sc, "dv")
                            cx.wait(sc, "gd", cx.val("gd"))
                            cx.inc("ac", sc.activation(stg_h[:, 0:512], ev32[:, 0:512], AF.Copy))
                            cx.wait(gp, "ac")
                            cx.inc("gp", gp.tensor_sub(stg_l[:, 0:512], ev32[:, 0:512],
                                                       stg_h[:, 0:512]))
                            cx.wait(gp, "gp")
                            for tau in range(TPG):
                                rb0 = 512 * half + 128 * tau
                                dma(gp, cc_out[G][rb0:rb0 + 128,
                                                  128 * cc:128 * (cc + 1)],
                                    stg_h[:, 128 * tau:128 * (tau + 1)], sem="gd")
                                dma(gp, cc_out[G][rb0:rb0 + 128,
                                                  1024 + 128 * cc:1024 + 128 * (cc + 1)],
                                    stg_l[:, 128 * tau:128 * (tau + 1)], sem="gd")
                else:
                    cx.wait(gp, "pe", cx.val("pe"))
                    cx.inc("gp", gp.memset(o_facc[0:128, 0:1040], 1.0))
                    prc = psB[:, 1024:1152]
                    cx.wait(te, "dv")
                    cx.wait(te, "ac", cx.val("ac"))
                    for tau in range(TPG):
                        cx.inc("pe", te.matmul(prc[0:128, tau:tau + 1],
                                               rl[0:1, 128 * tau:128 * (tau + 1)],
                                               ident[0:1, 0:1], is_transpose=True))
                    cx.wait(ve, "pe")
                    for tau in range(TPG):
                        cx.inc("dv", ve.tensor_copy(rl[:, 1024 + tau:1024 + tau + 1],
                                                    prc[0:128, tau:tau + 1]))
                    cx.wait(te, "vd", wvv)
                    dv_vev = {}
                    for tau in range(TPG):
                        po = psA[:, 1024:2048] if tau % 2 else psA[:, 0:1024]
                        if tau < 2:
                            cx.wait(te, "ac", cx.val("ac"))
                        else:
                            cx.wait(te, "dv", dv_vev[tau - 2])
                        for ncx in range(NCH):
                            for hf in range(2):
                                last = mm3(po[:, 512 * hf:512 * (hf + 1)],
                                    PT_h[:, 512 * ncx + 128 * tau:512 * ncx + 128 * (tau + 1)],
                                    PT_l[:, 512 * ncx + 128 * tau:512 * ncx + 128 * (tau + 1)],
                                    vWh[:, 1024 * ncx + 512 * hf:1024 * ncx + 512 * (hf + 1)],
                                    vWl[:, 1024 * ncx + 512 * hf:1024 * ncx + 512 * (hf + 1)],
                                    start=(ncx == 0), stop=(ncx == NCH - 1))
                                if ncx == NCH - 1:
                                    cx.inc("pe", last)
                        cx.wait(ve, "pe")
                        cx.wait(ve, "dv")
                        cx.wait(ve, "gp", cx.val("gp"))
                        vstr = bass.AP(o_facc, 0, [[4096, 128], [65, 16], [1, 64]])
                        cx.inc("dv", ve.tensor_scalar(vstr, po[:, 0:1024],
                                                      rl[:, 1024 + tau:1024 + tau + 1],
                                                      None, ALU.mult))
                        dv_vev[tau] = cx.val("dv")
                        cx.wait(sc, "dv")
                        cx.wait(sc, "gd", cx.val("gd"))
                        cx.inc("ac", sc.activation(stg_h[:, 0:1040], o_facc[:, 0:1040], AF.Copy))
                        cx.wait(gp, "ac")
                        cx.inc("gp", gp.tensor_sub(stg_l[:, 0:1040], o_facc[:, 0:1040],
                                                   stg_h[:, 0:1040]))
                        cx.wait(gp, "gp")
                        rb0 = 512 * half + 128 * tau
                        dma(gp, cc_out[G][rb0:rb0 + 128, 2048:3088],
                            stg_h[:, 0:1040], sem="gd")
                        dma(gp, cc_out[G][rb0:rb0 + 128, 3088:4128],
                            stg_l[:, 0:1040], sem="gd")

                afree[va] = cx.val("pe")

            nxt = ("qkT" if half == 0 else "wbT")
            w_pass("k", "kkT", "kv", 1 - fa, fa,
                   pre_s=[("kv", fa)], pre_v=[("vkT", 1 - fa)])
            w_pass("v", "vkT", "vv", 1 - fa, fa,
                   pre_s=[("vv", fa)], pre_v=[(nxt, 1 - fa)])
            if half == 0:
                w_pass("q", "qkT", "qv", 1 - fa, fa,
                       pre_s=[("qv", fa)], pre_v=[("wbT", 1 - fa)])

        barrier()
        if upto in ("ffn1", "phaseA"):
            return nc, cx, locals()

        def pcol(h, par):
            return 2048 * par + 128 * (h // 2) + 1024 * (h % 2)

        dv_hi = {}
        gp_lo = {}
        ac_f32 = {}
        slot_pe = {0: 0, 1: 0, 2: 0}
        ev_state = {}

        def emit_av(i, np_):
            par = i % 2
            bi = i % 3
            vb = 3072 + 1040 * bi
            cx.wait(te, "dv", dv_hi[i])
            for h in range(0, H, 2):
                pc = pcol(h, par)
                ob = psB[0:65, 128 * h:128 * (h + 1)]
                te.matmul(ob, wb_h[:, vb + 65 * h:vb + 65 * (h + 1)],
                          PT_h[:, pc:pc + 128],
                          start=(i == 0 and h % 4 == 0), stop=False)
                te.matmul(ob, wb_l[:, vb + 65 * h:vb + 65 * (h + 1)],
                          PT_h[:, pc:pc + 128], start=False, stop=False)
                te.matmul(ob, wb_h[:, vb + 65 * h:vb + 65 * (h + 1)],
                          PT_l[:, pc:pc + 128], start=False, stop=False)
            cx.inc("pe", te.nop())
            cx.wait(te, "gp", gp_lo[i])
            for h in range(1, H, 2):
                pc = pcol(h, par)
                ob = psB[0:65, 128 * h:128 * (h + 1)]
                te.matmul(ob, wb_h[:, vb + 65 * h:vb + 65 * (h + 1)],
                          PT_h[:, pc:pc + 128], start=False, stop=False)
                te.matmul(ob, wb_l[:, vb + 65 * h:vb + 65 * (h + 1)],
                          PT_h[:, pc:pc + 128], start=False, stop=False)
                last = te.matmul(ob, wb_h[:, vb + 65 * h:vb + 65 * (h + 1)],
                                 PT_l[:, pc:pc + 128], start=False,
                                 stop=(i == np_ - 1 and h % 4 == 3))
            cx.inc("pe", last)
            slot_pe[bi] = cx.val("pe")

        def emit_evac(jp, np_):
            te.wait_ge(cx.sem("pe"), cx.val("pe"))
            emit_av(np_ - 1, np_)
            te.wait_ge(cx.sem("pe"), cx.val("pe"))
            cx.wait(ve, "pe")
            cx.inc("dv", ve.tensor_copy(rl[64:65, 0:2048], psB[64:65, 0:2048]))
            dv_rec = cx.val("dv")
            cx.wait(te, "ac", ac_f32[np_ - 1])
            cx.wait(te, "dv", dv_rec)
            for h in range(H):
                cx.inc("pe", te.matmul(psA[0:64, 128 * h:128 * (h + 1)],
                          ones_f[64:65, 0:64], rl[64:65, 128 * h:128 * (h + 1)],
                          start=True, stop=True, tile_position=(64, 0)))
            te.wait_ge(cx.sem("pe"), cx.val("pe"))
            cx.wait(ve, "pe")
            cx.wait(ve, "gd", ev_state.get("gd_odma", 0))
            cx.inc("dv", ve.tensor_copy(o_facc[0:64, 0:2048], psA[0:64, 0:2048]))
            dv_copy = cx.val("dv")
            cx.wait(ve, "dv")
            cx.inc("dv", ve.reciprocal(o_facc[0:64, 0:2048], o_facc[0:64, 0:2048]))
            cx.wait(ve, "dv")
            for h in range(H):
                ob = ht if h < 8 else h2t
                ve.tensor_tensor(ob[0:64, 128 * (h % 8):128 * (h % 8 + 1)],
                                 psB[0:64, 128 * h:128 * (h + 1)],
                                 o_facc[0:64, 128 * h:128 * (h + 1)], ALU.mult)
            cx.inc("dv", ve.engine_nop())
            dv_mult = cx.val("dv")
            cx.wait(gp, "dv", dv_mult)
            for c in range(NCH):
                ob = ht if c < 4 else h2t
                srcb = bass.AP(ob, 256 * (c % 4), [[1024, 64], [128, 2], [1, 128]])
                dst = bass.AP(o_sp.tensor, (128 * jp) * D + 128 * c,
                              [[D, 64], [64 * D, 2], [1, 128]])
                dma(gp, dst, srcb, sem="gd")
            ev_state["gd_odma"] = cx.val("gd")
            ev_state["dv_copy"] = dv_copy

        for j in range(NT if max_j is None else max_j):
            cx.wait(sy, "pe", cx.val("pe"))
            cx.wait(sy, "sd")
            dma(sy, stg[:, 0:1024], q_sp[128 * j:128 * (j + 1), 0:1024])
            cx.wait(gp, "gd", cx.val("gd"))
            dma(gp, wa_h[:, 0:1024], q_sp[128 * j:128 * (j + 1), 0:1024], sem="gd")
            cx.wait(gp, "xd")
            cx.wait(gp, "gd")
            cx.inc("gp", gp.tensor_sub(wa_l[:, 0:1024], stg[:, 0:1024], wa_h[:, 0:1024]))
            gp_q = cx.val("gp")
            cx.wait(sc, "sd", cx.val("sd"))
            npair = 2 * j + 2
            if sdpa_stage < 2:
                continue
            if sdpa_stage >= 5 and j >= 1:
                emit_evac(j - 1, npair - 2)

            for i in range(npair):
                par = i % 2
                bi = i % 3
                Gk, rb0 = ccpos(i)
                gate = slot_pe[bi]
                cx.wait(sy, "pe", gate)
                dma(sy, wb_h[:, 1024 * bi:1024 * (bi + 1)],
                    cc_out[Gk][rb0:rb0 + 128, 0:1024])
                dma(sy, wb_h[:, 3072 + 1040 * bi:3072 + 1040 * (bi + 1)],
                    cc_out[Gk][rb0:rb0 + 128, 2048:3088])
                cx.wait(sc, "pe", gate)
                dma(sc, wb_l[:, 1024 * bi:1024 * (bi + 1)],
                    cc_out[Gk][rb0:rb0 + 128, 1024:2048], sem="vd")
                dma(sc, wb_l[:, 3072 + 1040 * bi:3072 + 1040 * (bi + 1)],
                    cc_out[Gk][rb0:rb0 + 128, 3088:4128], sem="vd")
                kvld = cx.val("xd")
                kvld_v = cx.val("vd")
                cx.wait(te, "ac", ac_f32[i - 1] if i >= 1 else cx.val("ac"))
                if i == 0 and j >= 1 and sdpa_stage >= 5:
                    cx.wait(te, "dv", ev_state["dv_copy"])
                cx.wait(te, "xd", kvld)
                cx.wait(te, "vd", kvld_v)
                cx.wait(te, "gp", gp_q)
                for c in range(NCH):
                    for sub in range(2):
                        h = 2 * c + sub
                        tp = (64 * sub, 0)
                        kcol = 1024 * bi + 128 * c
                        qcol = 128 * c
                        pc = pcol(h, 0)
                        cx.inc("pe", mm3(psA[:, pc:pc + 128],
                            wb_h[64 * sub:64 * (sub + 1), kcol:kcol + 128],
                            wb_l[64 * sub:64 * (sub + 1), kcol:kcol + 128],
                            wa_h[64 * sub:64 * (sub + 1), qcol:qcol + 128],
                            wa_l[64 * sub:64 * (sub + 1), qcol:qcol + 128],
                            start=True, stop=True, tp=tp))
                pe_s = cx.val("pe")
                te.wait_ge(cx.sem("pe"), pe_s)
                if sdpa_stage >= 4 and i >= 1:
                    emit_av(i - 1, npair)
                if sdpa_stage < 3:
                    continue
                if i >= npair - 2:
                    if i == npair - 2:
                        mkap = bass.AP(maskA, 0, [[128, 128], [0, 16], [1, 128]])
                    else:
                        mkap = bass.AP(maskB, 0, [[1, 128], [0, 16], [0, 128]])
                    cx.wait(ve, "pe", pe_s)
                    cx.inc("dv", ve.tensor_tensor(psA[:, 0:2048], psA[:, 0:2048],
                                                  mkap, ALU.add))
                    cx.wait(sc, "dv")
                else:
                    cx.wait(sc, "pe", pe_s)
                po2 = 2048 * par
                pstg = stg[:, 0:2048] if par == 0 else o_facc[:, 2048:4096]
                if i >= 2:
                    cx.wait(sc, "gp", gp_lo[i - 2])
                elif i == 0:
                    cx.wait(sc, "gp", gp_q)
                cx.inc("ac", sc.activation(pstg, psA[:, 0:2048],
                                           AF.Exp, scale=0.125))
                ac_f32[i] = cx.val("ac")
                cx.wait(ve, "ac", ac_f32[i])
                cx.inc("dv", ve.tensor_copy(PT_h[:, po2:po2 + 2048], pstg))
                dv_cast = cx.val("dv")
                cx.wait(ve, "dv")
                cx.inc("dv", ve.tensor_sub(PT_l[:, po2:po2 + 1024], pstg[:, 0:1024],
                                           PT_h[:, po2:po2 + 1024]))
                dv_hi[i] = cx.val("dv")
                cx.wait(gp, "dv", dv_cast)
                cx.inc("gp", gp.tensor_sub(PT_l[:, po2 + 1024:po2 + 2048],
                                           pstg[:, 1024:2048],
                                           PT_h[:, po2 + 1024:po2 + 2048]))
                gp_lo[i] = cx.val("gp")
        if sdpa_stage >= 5:
            NTj = NT if max_j is None else max_j
            if NTj >= 1:
                emit_evac(NTj - 1, 2 * NTj)
        barrier()
        if upto == "sdpa":
            return nc, cx, locals()

        afree[0] = cx.val("pe")
        afree[1] = cx.val("pe")
        for G in range(n_groups):
            pa = G % 2
            ks, vs = 1 - pa, pa
            p2h, p2l = AR[ks]
            v2h, v2l = AR[vs]
            f2h, f2l = AR[ks]
            t0 = G * TPG
            cx.wait(sy, "pe", cx.val("pe"))
            cx.wait(gp, "pe", cx.val("pe"))
            for tau in range(TPG):
                t = t0 + tau
                half = tau % 2
                cx.wait(sy, "dv", cx.val("dv"))
                dma(sy, stg[:, 1024 * half:1024 * (half + 1)],
                    o_sp[128 * t:128 * (t + 1), :])
                dsth = bass.AP(h3T_h, 128 * tau, [[4096, 128], [512, 8], [1, 128]])
                dstl = bass.AP(h3T_l, 128 * tau, [[4096, 128], [512, 8], [1, 128]])
                cx.wait(gp, "dv", cx.val("dv"))
                dma(gp, dsth, o_sp[128 * t:128 * (t + 1), :], sem="gd")
                cx.wait(ve, "xd")
                cx.wait(ve, "gd")
                srcf = bass.AP(stg, 1024 * half, [[2048, 128], [128, 8], [1, 128]])
                cx.inc("dv", ve.tensor_tensor(dstl, srcf, dsth, ALU.subtract))

            if G == 0:
                prefetch("pkT", ks)
            wv = wv_of[("pkT", ks)]
            prefetch("pv", vs)
            wv_pv = wv_of[("pv", vs)]
            lrow = psB[0:1, 0:512]
            cx.wait(te, "dv", cx.val("dv"))
            cx.wait(te, "ac", cx.val("ac"))
            ac_after = {}
            gp_after = {-1: cx.val("gp")}
            for ncx in range(NCH):
                sp = psA[:, 512 * (ncx % 2):512 * (ncx % 2 + 1)]
                if ncx >= 2:
                    cx.wait(te, "ac", ac_after[ncx - 2])
                else:
                    cx.wait(te, "ac", cx.val("ac"))
                cx.wait(te, "xd", wv)
                for cc in range(NCH):
                    last = mm3(sp,
                        p2h[:, 1024 * cc + 128 * ncx:1024 * cc + 128 * (ncx + 1)],
                        p2l[:, 1024 * cc + 128 * ncx:1024 * cc + 128 * (ncx + 1)],
                        h3T_h[:, 512 * cc:512 * (cc + 1)],
                        h3T_l[:, 512 * cc:512 * (cc + 1)],
                        start=(cc == 0), stop=(cc == NCH - 1))
                cx.inc("pe", last)
                cx.wait(sc, "pe")
                if ncx >= 2:
                    sfe, sfv = gp_after[ncx - 2]
                    cx.wait(sc, sfe, sfv)
                pf = Pf[ncx % 2]
                cx.inc("ac", sc.activation(pf[:], sp, AF.Exp, scale=1.0 / 32.0))
                ac_after[ncx] = cx.val("ac")
                cx.wait(ve, "ac")
                cx.inc("dv", ve.tensor_copy(PT_h[:, 512 * ncx:512 * (ncx + 1)], pf[:]))
                if ncx % 2 == 0:
                    cx.wait(gp, "dv")
                    cx.inc("gp", gp.tensor_sub(PT_l[:, 512 * ncx:512 * (ncx + 1)], pf[:],
                                               PT_h[:, 512 * ncx:512 * (ncx + 1)]))
                    gp_after[ncx] = ("gp", cx.val("gp"))
                else:
                    cx.wait(ve, "dv")
                    cx.inc("dv", ve.tensor_sub(PT_l[:, 512 * ncx:512 * (ncx + 1)], pf[:],
                                               PT_h[:, 512 * ncx:512 * (ncx + 1)]))
                    gp_after[ncx] = ("dv", cx.val("dv"))
            cx.wait(te, "gp")
            cx.wait(te, "dv")
            for ncx in range(NCH):
                te.matmul(lrow, ones_b[:, 0:1], PT_h[:, 512 * ncx:512 * (ncx + 1)],
                          start=(ncx == 0), stop=False)
                last = te.matmul(lrow, ones_b[:, 0:1], PT_l[:, 512 * ncx:512 * (ncx + 1)],
                                 start=False, stop=(ncx == NCH - 1))
            cx.inc("pe", last)
            cx.wait(ve, "pe")
            cx.inc("dv", ve.reciprocal(rl[0:1, 0:512], lrow))
            prc = psB[:, 1024:1152]
            cx.wait(te, "dv")
            cx.wait(te, "ac", cx.val("ac"))
            for tau in range(TPG):
                cx.inc("pe", te.matmul(prc[0:128, tau:tau + 1],
                                       rl[0:1, 128 * tau:128 * (tau + 1)],
                                       ident[0:1, 0:1], is_transpose=True))
            cx.wait(ve, "pe")
            for tau in range(TPG):
                cx.inc("dv", ve.tensor_copy(rl[:, 1024 + tau:1024 + tau + 1],
                                            prc[0:128, tau:tau + 1]))
            afree[ks] = cx.val("pe")
            prefetch("fkT", ks, c0=0, c1=1024)
            cx.wait(te, "vd", wv_pv)
            dv_pv = {}
            dv_h4 = {}
            for tau in range(TPG):
                t = t0 + tau
                po = psA[:, 1024:2048] if tau % 2 == 0 else psA[:, 0:1024]
                cx.wait(sy, "dv", cx.val("dv"))
                cx.wait(sy, "pe", cx.val("pe"))
                cx.wait(sy, "ac", cx.val("ac"))
                dma(sy, stg[:, 0:1024], h2_sp[128 * t:128 * (t + 1), :])
                dma(sy, xt[tau][:], x1_sp[128 * t:128 * (t + 1), :])
                h2v = cx.val("xd")
                if tau >= 2:
                    cx.wait(te, "dv", dv_pv[tau - 2])
                elif tau == 1:
                    cx.wait(te, "ac", cx.val("ac"))
                for ncx in range(NCH):
                    for hf in range(2):
                        last = mm3(po[:, 512 * hf:512 * (hf + 1)],
                            PT_h[:, 512 * ncx + 128 * tau:512 * ncx + 128 * (tau + 1)],
                            PT_l[:, 512 * ncx + 128 * tau:512 * ncx + 128 * (tau + 1)],
                            v2h[:, 1024 * ncx + 512 * hf:1024 * ncx + 512 * (hf + 1)],
                            v2l[:, 1024 * ncx + 512 * hf:1024 * ncx + 512 * (hf + 1)],
                            start=(ncx == 0), stop=(ncx == NCH - 1))
                        if ncx == NCH - 1:
                            cx.inc("pe", last)
                cx.wait(ve, "pe")
                cx.wait(ve, "xd", h2v)
                cx.wait(ve, "dv")
                cx.wait(ve, "sd", cx.val("sd"))
                cx.inc("dv", ve.scalar_tensor_tensor(ht[:], po[:, 0:1024],
                                                     rl[:, 1024 + tau:1024 + tau + 1],
                                                     stg[:, 0:1024], ALU.mult, ALU.add))
                dv_pv[tau] = cx.val("dv")
                cx.wait(ve, "dv")
                cx.inc("dv", ve.tensor_add(xt[tau][:], xt[tau][:], ht[:]))
                ln(ht[:])
                h4b = h2t if tau % 2 == 0 else ht2
                cx.wait(ve, "dv")
                cx.inc("dv", ln_apply(h4b[:], ht[:]))
                dv_h4[tau] = cx.val("dv")
                if tau >= 1:
                    pb = h2t if (tau - 1) % 2 == 0 else ht2
                    cx.wait(te, "dv", dv_h4[tau - 1])
                    transp_split(pb, lambda c: 128 * c, NCH, h4T_h, h4T_l,
                                 lambda c, _t=tau - 1: 512 * c + 128 * _t)

            pb = h2t if (TPG - 1) % 2 == 0 else ht2
            cx.wait(te, "dv", dv_h4[TPG - 1])
            transp_split(pb, lambda c: 128 * c, NCH, h4T_h, h4T_l,
                         lambda c: 512 * c + 128 * (TPG - 1))
            cx.wait(gp, "dv", cx.val("dv"))
            cx.inc("gp", gp.memset(o_facc[:], 0.0))
            lrow = psB[0:1, 0:512]
            afree[vs] = cx.val("pe")
            for Q in range(4):
                wv = wv_of[("fkT", ks)]
                prefetch("fv", vs, r0=1024 * Q, r1=1024 * (Q + 1))
                wv_fv = wv_of[("fv", vs)]
                cx.wait(te, "dv", cx.val("dv"))
                ac_after = {}
                gp_after = {-1: cx.val("gp")}
                for ncx in range(NCH):
                    sp = psA[:, 512 * (ncx % 2):512 * (ncx % 2 + 1)]
                    if ncx >= 2:
                        cx.wait(te, "ac", ac_after[ncx - 2])
                    else:
                        cx.wait(te, "ac", cx.val("ac"))
                    cx.wait(te, "xd", wv)
                    for cc in range(NCH):
                        last = mm3(sp,
                            p2h[:, 1024 * cc + 128 * ncx:1024 * cc + 128 * (ncx + 1)],
                            p2l[:, 1024 * cc + 128 * ncx:1024 * cc + 128 * (ncx + 1)],
                            h4T_h[:, 512 * cc:512 * (cc + 1)],
                            h4T_l[:, 512 * cc:512 * (cc + 1)],
                            start=(cc == 0), stop=(cc == NCH - 1))
                    cx.inc("pe", last)
                    cx.wait(sc, "pe")
                    if ncx >= 2:
                        sfe, sfv = gp_after[ncx - 2]
                        cx.wait(sc, sfe, sfv)
                    pf = Pf[ncx % 2]
                    cx.inc("ac", sc.activation(pf[:], sp, AF.Exp, scale=1.0 / 32.0))
                    ac_after[ncx] = cx.val("ac")
                    cx.wait(ve, "ac")
                    cx.inc("dv", ve.tensor_copy(PT_h[:, 512 * ncx:512 * (ncx + 1)], pf[:]))
                    if ncx % 2 == 0:
                        cx.wait(gp, "dv")
                        cx.inc("gp", gp.tensor_sub(PT_l[:, 512 * ncx:512 * (ncx + 1)], pf[:],
                                                   PT_h[:, 512 * ncx:512 * (ncx + 1)]))
                        gp_after[ncx] = ("gp", cx.val("gp"))
                    else:
                        cx.wait(ve, "dv")
                        cx.inc("dv", ve.tensor_sub(PT_l[:, 512 * ncx:512 * (ncx + 1)], pf[:],
                                                   PT_h[:, 512 * ncx:512 * (ncx + 1)]))
                        gp_after[ncx] = ("dv", cx.val("dv"))
                cx.wait(te, "gp")
                cx.wait(te, "dv")
                for ncx in range(NCH):
                    te.matmul(lrow, ones_b[:, 0:1], PT_h[:, 512 * ncx:512 * (ncx + 1)],
                              start=(Q == 0 and ncx == 0), stop=False)
                    last = te.matmul(lrow, ones_b[:, 0:1], PT_l[:, 512 * ncx:512 * (ncx + 1)],
                                     start=False, stop=(Q == 3 and ncx == NCH - 1))
                cx.inc("pe", last)
                afree[ks] = cx.val("pe")
                if Q < 3:
                    prefetch("fkT", ks, c0=1024 * (Q + 1), c1=1024 * (Q + 2))
                else:
                    prefetch("wbT", ks)
                cx.wait(te, "vd", wv_fv)
                dv_fv = {}
                for tau in range(TPG):
                    po = psA[:, 1024:2048] if tau % 2 == 0 else psA[:, 0:1024]
                    if tau >= 2:
                        cx.wait(te, "dv", dv_fv[tau - 2])
                    elif tau == 1:
                        cx.wait(te, "ac", cx.val("ac"))
                    else:
                        cx.wait(te, "dv", cx.val("dv"))
                    for ncx in range(NCH):
                        for hf in range(2):
                            last = mm3(po[:, 512 * hf:512 * (hf + 1)],
                                PT_h[:, 512 * ncx + 128 * tau:512 * ncx + 128 * (tau + 1)],
                                PT_l[:, 512 * ncx + 128 * tau:512 * ncx + 128 * (tau + 1)],
                                v2h[:, 1024 * ncx + 512 * hf:1024 * ncx + 512 * (hf + 1)],
                                v2l[:, 1024 * ncx + 512 * hf:1024 * ncx + 512 * (hf + 1)],
                                start=(ncx == 0), stop=(ncx == NCH - 1))
                            if ncx == NCH - 1:
                                cx.inc("pe", last)
                    cx.wait(ve, "pe")
                    cx.wait(ve, "gp", cx.val("gp"))
                    cx.inc("dv", ve.tensor_add(o_facc[:, 1024 * tau:1024 * (tau + 1)],
                                               o_facc[:, 1024 * tau:1024 * (tau + 1)],
                                               po[:, 0:1024]))
                    dv_fv[tau] = cx.val("dv")
                    cx.wait(ve, "dv")
                afree[vs] = cx.val("pe")
            cx.wait(ve, "pe")
            cx.inc("dv", ve.reciprocal(rl[0:1, 0:512], lrow))
            prc = psB[:, 1024:1152]
            cx.wait(te, "dv")
            for tau in range(TPG):
                cx.inc("pe", te.matmul(prc[0:128, tau:tau + 1],
                                       rl[0:1, 128 * tau:128 * (tau + 1)],
                                       ident[0:1, 0:1], is_transpose=True))
            cx.wait(ve, "pe")
            for tau in range(TPG):
                cx.inc("dv", ve.tensor_copy(rl[:, 1024 + tau:1024 + tau + 1],
                                            prc[0:128, tau:tau + 1]))
            cx.wait(ve, "dv")
            for tau in range(TPG):
                cx.inc("dv", ve.scalar_tensor_tensor(
                    xt[tau][:], o_facc[:, 1024 * tau:1024 * (tau + 1)],
                    rl[:, 1024 + tau:1024 + tau + 1], xt[tau][:], ALU.mult, ALU.add))

            xd0 = wv_of[("wbT", ks)]
            if G + 1 < n_groups:
                prefetch("pkT", vs)
            lnbuf = [(ht, h1T_h, h1T_l), (ht2, h1TB_h, h1TB_l)]
            cx.wait(sc, "xd", xd0)
            cx.wait(sc, "dv", cx.val("dv"))
            ln(xt[0][:], sb=8)
            cx.wait(ve, "dv")
            cx.wait(ve, "sd", cx.val("sd"))
            cx.inc("dv", ln_apply(lnbuf[0][0][:], xt[0][:], sb=8))
            cx.wait(te, "dv")
            transp_split(lnbuf[0][0], lambda c: 128 * c, NCH,
                         lnbuf[0][1], lnbuf[0][2], lambda c: 128 * c, alt=True)
            dv_h1 = cx.val("dv")
            dv_stt = cx.val("dv")
            for tau in range(TPG):
                t = t0 + tau
                xa = xt[tau][:]
                hb, hTh, hTl = lnbuf[tau % 2]
                pbase = psA[:, 0:1024]
                p16 = psA[:, 1024:1040]
                cx.wait(te, "dv", dv_h1)
                cx.wait(te, "xd", xd0)
                for cc in range(NCH):
                    lh = hTh[:, 128 * cc:128 * (cc + 1)]
                    llo = hTl[:, 128 * cc:128 * (cc + 1)]
                    mm3(p16[:, 0:16], lh, llo, rmsb_h[:, 16 * cc:16 * (cc + 1)],
                        rmsb_l[:, 16 * cc:16 * (cc + 1)], start=(cc == 0), stop=False)
                cx.inc("pe", te.matmul(p16[:, 0:8], ones_f[0:1, :], rbt[:], start=False, stop=True))
                pe_p16 = cx.val("pe")
                cx.wait(te, "dv", dv_stt)
                first = True
                for cc in range(NCH):
                    lh = hTh[:, 128 * cc:128 * (cc + 1)]
                    llo = hTl[:, 128 * cc:128 * (cc + 1)]
                    for hf in range(2):
                        mm3(pbase[:, 512 * hf:512 * (hf + 1)], lh, llo,
                            f2h[:, 1024 * cc + 512 * hf:1024 * cc + 512 * (hf + 1)],
                            f2l[:, 1024 * cc + 512 * hf:1024 * cc + 512 * (hf + 1)],
                            start=first, stop=False)
                    first = False
                te.matmul(pbase[:, 0:512], ones_b[0:1, :], bbp[0:1, 0:512], start=False, stop=False)
                te.matmul(pbase[:, 512:1024], ones_b[0:1, :], bbp[0:1, 512:1024], start=False, stop=False)
                te.matmul(pbase[:, 0:512], ones_b[0:1, :], bbp[0:1, 1024:1536], start=False, stop=False)
                cx.inc("pe", te.matmul(pbase[:, 512:1024], ones_b[0:1, :], bbp[0:1, 1536:2048], start=False, stop=False))
                cx.wait(sc, "pe", pe_p16)
                cx.inc("ac", sc.activation(g8[:], p16[:, 0:8], AF.Exp, accum_out=stat[:, 6:7]))
                cx.wait(ve, "ac")
                cx.inc("dv", ve.reciprocal(stat[:, 7:8], stat[:, 6:7]))
                cx.wait(ve, "dv")
                cx.inc("dv", ve.tensor_scalar(g8b[:], g8[:], stat[:, 7:8], None, ALU.mult))
                cx.wait(ve, "dv")
                cx.inc("dv", ve.tensor_copy(g8[:], g8b[:]))
                cx.wait(ve, "dv")
                for e in range(8):
                    cx.inc("dv", ve.tensor_scalar(P64[:, 8 * e:8 * (e + 1)], p16[:, 8:16],
                                     g8[:, e:e + 1], 2.0, ALU.mult, ALU.mult))
                if tau + 1 < TPG:
                    nb, nTh, nTl = lnbuf[(tau + 1) % 2]
                    cx.wait(sc, "xd", xd0)
                    ln(xt[tau + 1][:], sb=8)
                    cx.wait(ve, "dv")
                    cx.wait(ve, "sd", cx.val("sd"))
                    cx.inc("dv", ln_apply(nb[:], xt[tau + 1][:], sb=8))
                    cx.wait(te, "dv")
                    transp_split(nb, lambda c: 128 * c, NCH, nTh, nTl,
                                 lambda c: 128 * c, alt=True)
                    dv_h1 = cx.val("dv")
                pt64 = psB[:, 1024:1152]
                cx.wait(te, "dv")
                cx.inc("pe", te.matmul(pt64[0:64, 0:128], P64[:], ident[:], is_transpose=True))
                cx.wait(sc, "pe")
                cx.inc("ac", sc.activation(P64h[:], pt64[0:64, 0:128], AF.Copy))
                cx.wait(ve, "ac")
                cx.inc("dv", ve.tensor_sub(P64l[:], pt64[0:64, 0:128], P64h[:]))
                cx.wait(te, "dv")
                for hf in range(2):
                    cx.inc("pe", mm3(pbase[:, 512 * hf:512 * (hf + 1)], P64h[:], P64l[:],
                        b2h[:, 512 * hf:512 * (hf + 1)], b2l[:, 512 * hf:512 * (hf + 1)],
                        start=False, stop=True))
                cx.wait(ve, "pe")
                cx.wait(ve, "sd", cx.val("sd"))
                cx.inc("dv", ve.scalar_tensor_tensor(ht[:], pbase[:, 0:1024], 0.5, xa,
                                                     ALU.mult, ALU.add))
                dv_stt = cx.val("dv")
                cx.wait(sy, "dv")
                dma(sy, y_out[128 * t:128 * (t + 1), :], ht[:], sem="sd")
            afree[ks] = cx.val("pe")
        barrier()
        return nc, cx, locals()




from concourse.bass_utils import run_bass_kernel_spmd


def _host_inputs(core, x_batch, W):
    import ml_dtypes
    s = core % 2
    NT = 16
    tri16 = np.where(np.arange(128)[:, None] <= np.arange(128)[None, :],
                     0.0, NEG).astype(ml_dtypes.bfloat16)
    xr = x_batch.reshape(NT * 2, 128, 1024)
    return {
        "x": np.ascontiguousarray(xr[s::2].reshape(NT * 128, 1024)),
        "xp": np.ascontiguousarray(xr[1 - s::2].reshape(NT * 128, 1024)),
        "wbT": W["wbT"], "bb": W["bb"], "rb": W["rb"], "rmid": W["rmid"],
        "b2": W["b2"], "qkT": W["qkT"], "qv": W["qv"], "kkT": W["kkT"],
        "kv": W["kv"], "vkT": W["vkT"], "vv": W["vv"], "pkT": W["pkT"],
        "pv": W["pv"], "fkT": W["fkT"], "fv": W["fv"],
        "maskA": tri16,
        "maskB": (np.full((128, 1), NEG, dtype=ml_dtypes.bfloat16) if s == 0
                  else np.zeros((128, 1), dtype=ml_dtypes.bfloat16)),
        "ident": np.eye(128, dtype=np.float32),
    }


def kernel(L0_attn_x, L0_attn_qK, L0_attn_qV, L0_attn_kK, L0_attn_kV,
           L0_attn_vK, L0_attn_vV, L0_attn_pK, L0_attn_pV, L0_attn_fK,
           L0_attn_fV, L0_ffn_base_w, L0_ffn_base_b, L0_ffn_lora_A,
           L0_ffn_lora_B, L0_ffn_router_w, L0_ffn_router_b):
    x = np.ascontiguousarray(np.asarray(L0_attn_x, dtype=np.float32))
    W = {
        "wbT": np.ascontiguousarray(np.asarray(L0_ffn_base_w).T),
        "bb": np.asarray(L0_ffn_base_b).reshape(1, 1024),
        "rb": np.asarray(L0_ffn_router_b).reshape(1, 8),
        "rmid": np.ascontiguousarray(np.concatenate(
            [np.asarray(L0_ffn_router_w).T, np.asarray(L0_ffn_lora_A)], axis=1)),
        "b2": np.ascontiguousarray(np.asarray(L0_ffn_lora_B).reshape(64, 1024)),
        "qkT": np.ascontiguousarray(np.asarray(L0_attn_qK).T),
        "qv": np.ascontiguousarray(np.asarray(L0_attn_qV)),
        "kkT": np.ascontiguousarray(np.asarray(L0_attn_kK).T),
        "kv": np.ascontiguousarray(np.asarray(L0_attn_kV)),
        "vkT": np.ascontiguousarray(np.asarray(L0_attn_vK).T),
        "vv": np.ascontiguousarray(np.asarray(L0_attn_vV)),
        "pkT": np.ascontiguousarray(np.asarray(L0_attn_pK).T),
        "pv": np.ascontiguousarray(np.asarray(L0_attn_pV)),
        "fkT": np.ascontiguousarray(np.asarray(L0_attn_fK).T),
        "fv": np.ascontiguousarray(np.asarray(L0_attn_fV)),
    }
    nc, cx, _ = build(n_groups=4, upto="full")
    in_maps = [_host_inputs(c, x[c // 2], W) for c in range(8)]
    res = run_bass_kernel_spmd(nc, in_maps, core_ids=list(range(8)))
    out = np.empty((4, 4096, 1024), dtype=np.float32)
    for c in range(8):
        b, s = c // 2, c % 2
        y = np.asarray(res.results[c]["y"]).reshape(16, 128, 1024)
        out[b].reshape(32, 128, 1024)[s::2] = y
    return out


# version 25 (final)
# speedup vs baseline: 1.0485x; 1.0132x over previous
import contextlib
import numpy as np
import concourse.bass as bass
import concourse.mybir as mybir

F32 = mybir.dt.float32
BF16 = mybir.dt.bfloat16
AF = mybir.ActivationFunctionType
ALU = mybir.AluOpType

D = 1024
NCH = 8
H = 16
TPG = 4
NEG = -1e30
EPS = 1e-5

ENGS = ["sync", "tensor", "scalar", "vector", "gpsimd"]


class Cx:
    DMA_PROD = {"xd": "sync", "sd": "sync", "gd": "gpsimd", "vd": "scalar"}

    def __init__(self, nc, es):
        self.nc = nc
        self.es = es
        self._s = {}
        self._bound = {}

    def _rec(self, n):
        if n not in self._s:
            self._s[n] = [self.es.enter_context(self.nc.semaphore(n)), 0]
            self._bound[n] = 0
        return self._s[n]

    def sem(self, n):
        return self._rec(n)[0]

    def inc(self, n, instr, k=1):
        r = self._rec(n)
        instr.then_inc(r[0], k)
        r[1] += k
        return r[1]

    def val(self, n):
        return self._rec(n)[1]

    def wait(self, eng, n, v=None):
        r = self._rec(n)
        tv = r[1] if v is None else v
        prod = self.DMA_PROD.get(n)
        if prod is not None and tv > self._bound[n]:
            pe_eng = getattr(self.nc, prod)
            if getattr(eng, "engine", None) != getattr(pe_eng, "engine", None):
                pe_eng.wait_ge(r[0], tv)
            self._bound[n] = tv
        eng.wait_ge(r[0], tv)


def build(n_groups=4, upto="full", max_g2=None, max_tiles=None, tile_stage=99, max_j=None, sdpa_stage=99):
    NT = TPG * n_groups
    NTOK = 128 * NT

    nc = bass.Bass("TRN2", target_bir_lowering=False)
    es = contextlib.ExitStack()
    cx = Cx(nc, es)

    x_in = nc.dram_tensor("x", [NTOK, D], F32, kind="ExternalInput").ap()
    xp_in = nc.dram_tensor("xp", [NTOK, D], F32, kind="ExternalInput").ap()
    y_out = nc.dram_tensor("y", [NTOK, D], F32, kind="ExternalOutput").ap()
    w_in = {}
    for nm, shp in [("wbT", [D, D]), ("bb", [1, D]), ("rb", [1, 8]),
                    ("rmid", [D, 16]), ("b2", [64, D]),
                    ("qkT", [D, D]), ("qv", [D, D]), ("kkT", [D, D]), ("kv", [D, D]),
                    ("vkT", [D, D]), ("vv", [D, D]), ("pkT", [D, D]), ("pv", [D, D]),
                    ("fkT", [D, 4 * D]), ("fv", [4 * D, D]),

                    ("ident", [128, 128])]:
        w_in[nm] = nc.dram_tensor(nm, shp, F32, kind="ExternalInput").ap()
    w_in["maskA"] = nc.dram_tensor("maskA", [128, 128], BF16, kind="ExternalInput").ap()
    w_in["maskB"] = nc.dram_tensor("maskB", [128, 1], BF16, kind="ExternalInput").ap()

    wsp = {}
    for nm in ["wbT", "rmid", "qkT", "qv", "kkT", "kv", "vkT", "vv",
               "pkT", "pv", "fkT", "fv"]:
        shp = list(w_in[nm].shape)
        wsp[nm + "_h"] = nc.dram_tensor(nm + "_h", shp, BF16).ap()
        wsp[nm + "_l"] = nc.dram_tensor(nm + "_l", shp, BF16).ap()

    x1_sp = nc.dram_tensor("x1_sp", [NTOK, D], F32).ap()
    h2_sp = nc.dram_tensor("h2_sp", [NTOK, D], F32).ap()
    o_sp = nc.dram_tensor("o_sp", [NTOK, D], F32).ap()
    q_sp = nc.dram_tensor("q_sp", [NTOK, D], F32).ap()
    CCW = 4128
    cc_out = [nc.dram_tensor(f"cc_out{g}", [2 * TPG * 128, CCW], BF16, addr_space="Local").ap()
              for g in range(n_groups)]

    def ccpos(g):
        return g // (2 * TPG), 512 * (g % 2) + 128 * ((g % (2 * TPG)) // 2)

    with es:
        ident = es.enter_context(nc.sbuf_tensor("identt", [128, 128], F32))
        ones_f = es.enter_context(nc.sbuf_tensor("ones_f", [128, 128], F32))
        ones_b = es.enter_context(nc.sbuf_tensor("ones_b", [128, 128], BF16))
        maskA = es.enter_context(nc.sbuf_tensor("maskA_t", [128, 128], BF16))
        maskB = es.enter_context(nc.sbuf_tensor("maskB_t", [128, 1], BF16))
        bbp = es.enter_context(nc.sbuf_tensor("bbp", [1, 2 * D], BF16))
        bbh, bbl = bbp[0:1, 0:D], bbp[0:1, D:2 * D]
        rbt = es.enter_context(nc.sbuf_tensor("rbt", [1, 8], F32))
        b2h = es.enter_context(nc.sbuf_tensor("b2h", [64, D], BF16))
        b2l = es.enter_context(nc.sbuf_tensor("b2l", [64, D], BF16))
        rmsb_h = es.enter_context(nc.sbuf_tensor("rmsb_h", [128, 128], BF16))
        rmsb_l = es.enter_context(nc.sbuf_tensor("rmsb_l", [128, 128], BF16))
        wa_h = es.enter_context(nc.sbuf_tensor("wa_h", [128, NCH * 1024], BF16))
        wa_l = es.enter_context(nc.sbuf_tensor("wa_l", [128, NCH * 1024], BF16))
        wb_h = es.enter_context(nc.sbuf_tensor("wb_h", [128, NCH * 1024], BF16))
        wb_l = es.enter_context(nc.sbuf_tensor("wb_l", [128, NCH * 1024], BF16))
        stg = es.enter_context(nc.sbuf_tensor("stg", [128, 2048], F32))
        stg_h = es.enter_context(nc.sbuf_tensor("stg_h", [128, 2048], BF16))
        stg_l = es.enter_context(nc.sbuf_tensor("stg_l", [128, 2048], BF16))
        h4T_h = es.enter_context(nc.sbuf_tensor("h4T_h", [128, NCH * 512], BF16))
        h4T_l = es.enter_context(nc.sbuf_tensor("h4T_l", [128, NCH * 512], BF16))
        o_facc = es.enter_context(nc.sbuf_tensor("o_facc", [128, 4096], F32))
        h3T_h = es.enter_context(nc.sbuf_tensor("h3T_h", [128, NCH * 512], BF16))
        h3T_l = es.enter_context(nc.sbuf_tensor("h3T_l", [128, NCH * 512], BF16))
        PT_h = es.enter_context(nc.sbuf_tensor("PT_h", [128, NCH * 512], BF16))
        PT_l = es.enter_context(nc.sbuf_tensor("PT_l", [128, NCH * 512], BF16))
        Pf = [es.enter_context(nc.sbuf_tensor(f"Pf{i}", [128, 512], F32)) for i in range(2)]
        xt = [es.enter_context(nc.sbuf_tensor(f"xt{i}", [128, D], F32)) for i in range(TPG)]
        ht = es.enter_context(nc.sbuf_tensor("ht", [128, D], F32))
        ht2 = es.enter_context(nc.sbuf_tensor("ht2", [128, D], F32))
        h1TB_h = es.enter_context(nc.sbuf_tensor("h1TB_h", [128, D], BF16))
        h1TB_l = es.enter_context(nc.sbuf_tensor("h1TB_l", [128, D], BF16))
        h2t = es.enter_context(nc.sbuf_tensor("h2t", [128, D], F32))
        h1T_h = es.enter_context(nc.sbuf_tensor("h1T_h", [128, D], BF16))
        h1T_l = es.enter_context(nc.sbuf_tensor("h1T_l", [128, D], BF16))
        trash = es.enter_context(nc.sbuf_tensor("trash", [128, D], BF16))
        stat = es.enter_context(nc.sbuf_tensor("stat", [128, 16], F32))
        g8 = es.enter_context(nc.sbuf_tensor("g8", [128, 8], F32))
        g8b = es.enter_context(nc.sbuf_tensor("g8b", [128, 8], BF16))
        P64 = es.enter_context(nc.sbuf_tensor("P64", [128, 64], F32))
        P64h = es.enter_context(nc.sbuf_tensor("P64h", [64, 128], BF16))
        P64l = es.enter_context(nc.sbuf_tensor("P64l", [64, 128], BF16))
        ev32 = es.enter_context(nc.sbuf_tensor("ev32", [128, 512], F32))
        rl = es.enter_context(nc.sbuf_tensor("rl", [128, 2048], F32))
        epsb = es.enter_context(nc.sbuf_tensor("epsb", [128, 1], F32))

        psA = es.enter_context(nc.psum_tensor("psA", [128, 2048], F32))
        psB = es.enter_context(nc.psum_tensor("psB", [128, 2048], F32))

        sy, te, sc, ve, gp = nc.sync, nc.tensor, nc.scalar, nc.vector, nc.gpsimd

        def dma(eng, out, in_, sem="xd", cut=True):
            v = cx.inc(sem, eng.dma_start(out=out, in_=in_), 16)
            if cut:
                eng.nop(nofuse=True)
            return v

        def barrier():
            for snm in ("xd", "sd"):
                cx.wait(sy, snm)
            cx.wait(gp, "gd")
            r = cx._rec("bar")
            for e in ENGS:
                getattr(nc, e).sem_inc(r[0], 1)
                r[1] += 1
            v = r[1]
            for e in ENGS:
                getattr(nc, e).wait_ge(r[0], v)

        def mm3(pap, lh, ll, rh, rlo, start, stop, tp=None):
            te.matmul(pap, lh, rh, start=start, stop=False, tile_position=tp)
            te.matmul(pap, lh, rlo, start=False, stop=False, tile_position=tp)
            return te.matmul(pap, ll, rh, start=False, stop=stop, tile_position=tp)

        def ln(src_ap, sb=0):
            s1, sq = stat[:, sb + 2:sb + 3], stat[:, sb + 3:sb + 4]
            mu, rstd = stat[:, sb:sb + 1], stat[:, sb + 1:sb + 2]
            t1, t2 = stat[:, sb + 4:sb + 5], stat[:, sb + 5:sb + 6]
            cx.wait(sc, "dv")
            cx.inc("ac", sc.activation(trash[:, 0:D], src_ap, AF.Copy, accum_out=s1))
            cx.wait(sc, "ac")
            cx.inc("ac", sc.activation(trash[:, 0:D], src_ap, AF.Square, accum_out=sq))
            cx.wait(ve, "ac")
            ve.tensor_scalar(mu, s1, 1.0 / D, None, ALU.mult)
            cx.inc("dv", ve.tensor_scalar(t1, sq, 1.0 / D, None, ALU.mult))
            cx.wait(ve, "dv")
            cx.inc("dv", ve.tensor_tensor(t2, mu, mu, ALU.mult))
            cx.wait(ve, "dv")
            cx.inc("dv", ve.tensor_sub(t1, t1, t2))
            cx.wait(sc, "dv")
            cx.inc("ac", sc.activation(t2, t1, AF.Sqrt, bias=epsb[:, 0:1]))
            cx.wait(ve, "ac")
            cx.inc("dv", ve.reciprocal(rstd, t2))
            cx.wait(ve, "dv")

        def ln_apply(dst_ap, src_ap, sb=0):
            return ve.tensor_scalar(dst_ap, src_ap, stat[:, sb:sb + 1],
                                    stat[:, sb + 1:sb + 2], ALU.subtract, ALU.mult)

        pts_dv = {0: 0, 1: 0}

        def transp_split(src, scol_fn, nch, dsth, dstl, dcol_fn, alt=False):
            entry_dv = cx.val("dv")
            for gi, b0 in enumerate(range(0, nch, 4)):
                nb = min(4, nch - b0)
                if alt and gi % 2 == 1:
                    pts, reg = psB[:, 512:1024], 1
                else:
                    pts, reg = psB[:, 1536:2048], 0
                if alt:
                    cx.wait(te, "dv", max(entry_dv, pts_dv[reg]))
                else:
                    cx.wait(te, "dv", cx.val("dv"))
                for i in range(nb):
                    c = scol_fn(b0 + i)
                    cx.inc("pe", te.matmul(pts[:, 128 * i:128 * (i + 1)], src[:, c:c + 128],
                                           ident[:], is_transpose=True))
                cx.wait(sc, "pe")
                for i in range(nb):
                    d = dcol_fn(b0 + i)
                    cx.inc("ac", sc.activation(dsth[:, d:d + 128],
                                               pts[:, 128 * i:128 * (i + 1)], AF.Copy))
                cx.wait(ve, "ac")
                for i in range(nb):
                    d = dcol_fn(b0 + i)
                    cx.inc("dv", ve.tensor_sub(dstl[:, d:d + 128],
                                               pts[:, 128 * i:128 * (i + 1)],
                                               dsth[:, d:d + 128]))
                pts_dv[reg] = cx.val("dv")


        def load_pair(dsth, dstl, nm, r0, r1, c0, c1, dst_off=0, qe=None, semn="xd"):
            qe = sy if qe is None else qe
            cw = c1 - c0
            for i, rr in enumerate(range(r0, r1, 128)):
                dma(qe, dsth[:, dst_off + i * cw:dst_off + (i + 1) * cw],
                    wsp[nm + "_h"][rr:rr + 128, c0:c1], sem=semn)
                dma(qe, dstl[:, dst_off + i * cw:dst_off + (i + 1) * cw],
                    wsp[nm + "_l"][rr:rr + 128, c0:c1], sem=semn)

        cx.inc("gp", gp.memset(ones_f[:], 1.0))
        cx.inc("gp", gp.memset(epsb[:], EPS))
        cx.inc("gp", gp.memset(ones_b[:], 1.0))
        dma(sy, ident[:], w_in["ident"][:, :])
        dma(sy, maskA[:], w_in["maskA"][:, :])
        dma(sy, maskB[:], w_in["maskB"][:, :])
        dma(sy, rbt[:], w_in["rb"][:, :])
        dma(sy, stg[0:1, 0:D], w_in["bb"][:, :])
        cx.wait(sc, "xd")
        cx.inc("ac", sc.activation(bbh, stg[0:1, 0:D], AF.Copy))
        cx.wait(ve, "ac")
        cx.inc("dv", ve.tensor_sub(bbl, stg[0:1, 0:D], bbh))
        dma(sy, stg[0:64, 1024:1024 + D], w_in["b2"][:, :])
        cx.wait(sc, "xd")
        cx.inc("ac", sc.activation(b2h[:], stg[0:64, 1024:1024 + D], AF.Copy))
        cx.wait(ve, "ac")
        cx.inc("dv", ve.tensor_sub(b2l[:], stg[0:64, 1024:1024 + D], b2h[:]))
        cx.wait(sy, "dv")

        for nm in ["wbT", "rmid", "qkT", "qv", "kkT", "kv", "vkT", "vv",
                   "pkT", "pv", "fkT", "fv"]:
            src = w_in[nm]
            R, C = src.shape
            for r0 in range(0, R, 128):
                for c0 in range(0, C, 2048):
                    cw = min(2048, C - c0)
                    cx.wait(sy, "gd", cx.val("gd"))
                    dma(sy, stg[:, 0:cw], src[r0:r0 + 128, c0:c0 + cw])
                    dma(gp, stg_h[:, 0:cw], src[r0:r0 + 128, c0:c0 + cw], sem="gd")
                    cx.wait(gp, "xd")
                    cx.wait(gp, "gd")
                    cx.inc("gp", gp.tensor_sub(stg_l[:, 0:cw], stg[:, 0:cw], stg_h[:, 0:cw]))
                    cx.wait(gp, "gp")
                    dma(gp, wsp[nm + "_h"][r0:r0 + 128, c0:c0 + cw], stg_h[:, 0:cw], sem="gd")
                    dma(gp, wsp[nm + "_l"][r0:r0 + 128, c0:c0 + cw], stg_l[:, 0:cw], sem="gd")
        barrier()
        if upto == "prologue":
            return nc, cx, locals()

        AR = [(wa_h, wa_l), (wb_h, wb_l)]
        load_pair(rmsb_h, rmsb_l, "rmid", 0, D, 0, 16)
        afree = {0: cx.val("pe"), 1: cx.val("pe")}
        wv_of = {}

        VNAMES = ("kv", "vv", "qv", "pv", "fv")

        def prefetch(nm, idx, c0=0, c1=None, r0=0, r1=None):
            sh = wsp[nm + "_h"]
            if c1 is None:
                c1 = sh.shape[1]
            if r1 is None:
                r1 = sh.shape[0]
            if nm in VNAMES:
                qe, semn = sc, "vd"
            else:
                qe, semn = sy, "xd"
            cx.wait(qe, "pe", afree[idx])
            load_pair(AR[idx][0], AR[idx][1], nm, r0, r1, c0, c1, qe=qe, semn=semn)
            wv_of[(nm, idx)] = cx.val(semn)
            if semn == "xd":
                cx.wait(sy, "xd", wv_of[(nm, idx)])

        for G2 in range(2 * n_groups):
            if max_g2 is not None and G2 >= max_g2:
                break
            G, half = G2 // 2, G2 % 2
            spill = (half == 0)
            xsrc = x_in if half == 0 else xp_in
            t0 = G * TPG
            cx.wait(sy, "dv", cx.val("dv"))
            cx.wait(sy, "ac", cx.val("ac"))
            for i in range(TPG):
                dma(sy, xt[i][:], xsrc[128 * (t0 + i):128 * (t0 + i + 1), :])
            xv = cx.val("xd")
            cx.wait(sy, "xd", xv)
            fa = G2 % 2
            fWh, fWl = AR[fa]
            if G2 == 0:
                prefetch("wbT", 0)
            xd0 = wv_of[("wbT", fa)]
            prefetch("kkT", 1 - fa)

            dv_h3 = {}
            lnbuf = [(ht, h1T_h, h1T_l), (ht2, h1TB_h, h1TB_l)]
            ntile = TPG if max_tiles is None else min(TPG, max_tiles)
            cx.wait(sc, "xd", max(xd0, xv))
            ln(xt[0][:], sb=8)
            cx.wait(ve, "dv")
            cx.inc("dv", ln_apply(lnbuf[0][0][:], xt[0][:], sb=8))
            cx.wait(te, "dv")
            transp_split(lnbuf[0][0], lambda c: 128 * c, NCH,
                         lnbuf[0][1], lnbuf[0][2], lambda c: 128 * c, alt=True)
            dv_h1 = cx.val("dv")
            dv_stt = cx.val("dv")
            for i in range(ntile):
                t = t0 + i
                xa = xt[i][:]
                hb, hTh, hTl = lnbuf[i % 2]
                pbase = psA[:, 0:1024]
                p16 = psA[:, 1024:1040]
                cx.wait(te, "dv", dv_h1)
                cx.wait(te, "xd", max(xd0, xv))
                for cc in range(NCH):
                    lh = hTh[:, 128 * cc:128 * (cc + 1)]
                    llo = hTl[:, 128 * cc:128 * (cc + 1)]
                    mm3(p16[:, 0:16], lh, llo, rmsb_h[:, 16 * cc:16 * (cc + 1)],
                        rmsb_l[:, 16 * cc:16 * (cc + 1)], start=(cc == 0), stop=False)
                cx.inc("pe", te.matmul(p16[:, 0:8], ones_f[0:1, :], rbt[:], start=False, stop=True))
                pe_p16 = cx.val("pe")
                cx.wait(te, "dv", dv_stt)
                first = True
                for cc in range(NCH):
                    lh = hTh[:, 128 * cc:128 * (cc + 1)]
                    llo = hTl[:, 128 * cc:128 * (cc + 1)]
                    for hf in range(2):
                        mm3(pbase[:, 512 * hf:512 * (hf + 1)], lh, llo,
                            fWh[:, 1024 * cc + 512 * hf:1024 * cc + 512 * (hf + 1)],
                            fWl[:, 1024 * cc + 512 * hf:1024 * cc + 512 * (hf + 1)],
                            start=first, stop=False)
                    first = False
                te.matmul(pbase[:, 0:512], ones_b[0:1, :], bbp[0:1, 0:512], start=False, stop=False)
                te.matmul(pbase[:, 512:1024], ones_b[0:1, :], bbp[0:1, 512:1024], start=False, stop=False)
                te.matmul(pbase[:, 0:512], ones_b[0:1, :], bbp[0:1, 1024:1536], start=False, stop=False)
                cx.inc("pe", te.matmul(pbase[:, 512:1024], ones_b[0:1, :], bbp[0:1, 1536:2048], start=False, stop=False))
                cx.wait(sc, "pe", pe_p16)
                cx.inc("ac", sc.activation(g8[:], p16[:, 0:8], AF.Exp, accum_out=stat[:, 6:7]))
                cx.wait(ve, "ac")
                cx.inc("dv", ve.reciprocal(stat[:, 7:8], stat[:, 6:7]))
                cx.wait(ve, "dv")
                cx.inc("dv", ve.tensor_scalar(g8b[:], g8[:], stat[:, 7:8], None, ALU.mult))
                cx.wait(ve, "dv")
                cx.inc("dv", ve.tensor_copy(g8[:], g8b[:]))
                cx.wait(ve, "dv")
                for e in range(8):
                    cx.inc("dv", ve.tensor_scalar(P64[:, 8 * e:8 * (e + 1)], p16[:, 8:16],
                                     g8[:, e:e + 1], 2.0, ALU.mult, ALU.mult))
                if i + 1 < ntile:
                    nb, nTh, nTl = lnbuf[(i + 1) % 2]
                    cx.wait(sc, "xd", max(xd0, xv))
                    ln(xt[i + 1][:], sb=8)
                    cx.wait(ve, "dv")
                    cx.inc("dv", ln_apply(nb[:], xt[i + 1][:], sb=8))
                    cx.wait(te, "dv")
                    transp_split(nb, lambda c: 128 * c, NCH, nTh, nTl,
                                 lambda c: 128 * c, alt=True)
                    dv_h1 = cx.val("dv")
                pt64 = psB[:, 1024:1152]
                cx.wait(te, "dv")
                cx.inc("pe", te.matmul(pt64[0:64, 0:128], P64[:], ident[:], is_transpose=True))
                cx.wait(sc, "pe")
                cx.inc("ac", sc.activation(P64h[:], pt64[0:64, 0:128], AF.Copy))
                cx.wait(ve, "ac")
                cx.inc("dv", ve.tensor_sub(P64l[:], pt64[0:64, 0:128], P64h[:]))
                cx.wait(te, "dv")
                for hf in range(2):
                    cx.inc("pe", mm3(pbase[:, 512 * hf:512 * (hf + 1)], P64h[:], P64l[:],
                        b2h[:, 512 * hf:512 * (hf + 1)], b2l[:, 512 * hf:512 * (hf + 1)],
                        start=False, stop=True))
                cx.wait(ve, "pe")
                cx.inc("dv", ve.scalar_tensor_tensor(xa, pbase[:, 0:1024], 0.5, xa,
                                                     ALU.mult, ALU.add))
                dv_stt = cx.val("dv")
                if spill:
                    cx.wait(sy, "dv")
                    dma(sy, x1_sp[128 * t:128 * (t + 1), :], xa, sem="sd")
                ln(xa)
                cx.wait(ve, "dv")
                cx.wait(ve, "sd", cx.val("sd"))
                cx.inc("dv", ln_apply(h2t[:], xa))
                if spill:
                    cx.wait(sy, "dv")
                    dma(sy, h2_sp[128 * t:128 * (t + 1), :], h2t[:], sem="sd")
                ln(h2t[:])
                cx.wait(ve, "dv")
                cx.wait(ve, "gp", cx.val("gp"))
                cx.inc("dv", ln_apply(o_facc[:, 1024 * i:1024 * (i + 1)], h2t[:]))
                dv_h3[i] = cx.val("dv")
            for i in sorted(dv_h3):
                cx.wait(te, "dv", dv_h3[i])
                transp_split(o_facc, lambda c, _i=i: 1024 * _i + 128 * c, NCH,
                             h3T_h, h3T_l, lambda c, _i=i: 512 * c + 128 * _i, alt=True)
            afree[fa] = cx.val("pe")
            if upto == "ffn1":
                continue

            def w_pass(kind, nmK, nmV, ka, va, pre_s=(), pre_v=(), G=G, t0=t0, half=half):
                scale = 1.0 / 32.0
                kWh, kWl = AR[ka]
                vWh, vWl = AR[va]
                wv = wv_of[(nmK, ka)]
                for nm, idx in pre_s:
                    prefetch(nm, idx)
                lrow = psB[0:1, 0:512]
                cx.wait(te, "dv", cx.val("dv"))
                cx.wait(te, "ac", cx.val("ac"))
                ac_after = {}
                gp_after = {-1: cx.val("gp")}
                for ncx in range(NCH):
                    sp = psA[:, 512 * (ncx % 2):512 * (ncx % 2 + 1)]
                    if ncx >= 2:
                        cx.wait(te, "ac", ac_after[ncx - 2])
                    cx.wait(te, "xd", wv)
                    for cc in range(NCH):
                        last = mm3(sp,
                            kWh[:, 1024 * cc + 128 * ncx:1024 * cc + 128 * (ncx + 1)],
                            kWl[:, 1024 * cc + 128 * ncx:1024 * cc + 128 * (ncx + 1)],
                            h3T_h[:, 512 * cc:512 * (cc + 1)],
                            h3T_l[:, 512 * cc:512 * (cc + 1)],
                            start=(cc == 0), stop=(cc == NCH - 1))
                    cx.inc("pe", last)
                    cx.wait(sc, "pe")
                    if ncx >= 2:
                        sfe, sfv = gp_after[ncx - 2]
                        cx.wait(sc, sfe, sfv)
                    pf = Pf[ncx % 2]
                    cx.inc("ac", sc.activation(pf[:], sp, AF.Exp, scale=scale))
                    ac_after[ncx] = cx.val("ac")
                    cx.wait(ve, "ac")
                    cx.inc("dv", ve.tensor_copy(PT_h[:, 512 * ncx:512 * (ncx + 1)], pf[:]))
                    if ncx % 2 == 0:
                        cx.wait(gp, "dv")
                        cx.inc("gp", gp.tensor_sub(PT_l[:, 512 * ncx:512 * (ncx + 1)], pf[:],
                                                   PT_h[:, 512 * ncx:512 * (ncx + 1)]))
                        gp_after[ncx] = ("gp", cx.val("gp"))
                    else:
                        cx.wait(ve, "dv")
                        cx.inc("dv", ve.tensor_sub(PT_l[:, 512 * ncx:512 * (ncx + 1)], pf[:],
                                                   PT_h[:, 512 * ncx:512 * (ncx + 1)]))
                        gp_after[ncx] = ("dv", cx.val("dv"))
                afree[ka] = cx.val("pe")
                for nm, idx in pre_v:
                    prefetch(nm, idx)
                wvv = wv_of[(nmV, va)]
                cx.wait(te, "gp")
                cx.wait(te, "dv")
                for ncx in range(NCH):
                    te.matmul(lrow, ones_b[:, 0:1], PT_h[:, 512 * ncx:512 * (ncx + 1)],
                              start=(ncx == 0), stop=False)
                    last = te.matmul(lrow, ones_b[:, 0:1], PT_l[:, 512 * ncx:512 * (ncx + 1)],
                                     start=False, stop=(ncx == NCH - 1))
                cx.inc("pe", last)
                cx.wait(ve, "pe")
                cx.inc("dv", ve.reciprocal(rl[0:1, 0:512], lrow))
                if kind in ("q", "k"):
                    rbc = psB[:, 512:1024]
                    cx.wait(te, "dv")
                    cx.wait(te, "gp", cx.val("gp"))
                    cx.inc("pe", te.matmul(rbc, ones_f[0:1, :], rl[0:1, 0:512],
                                           start=True, stop=True))
                    cx.wait(ve, "pe")
                    cx.inc("dv", ve.tensor_copy(Pf[0][:], rbc))
                    rbc_sb = Pf[0][:]
                    cx.wait(te, "vd", wvv)
                    dv_evac = {}
                    for cc in range(NCH):
                        po = psA[:, 1024 + 512 * (cc % 2):1024 + 512 * (cc % 2 + 1)]
                        if cc >= 2:
                            cx.wait(te, "dv", dv_evac[cc - 2])
                        for ncx in range(NCH):
                            last = mm3(po,
                                vWh[:, 1024 * ncx + 128 * cc:1024 * ncx + 128 * (cc + 1)],
                                vWl[:, 1024 * ncx + 128 * cc:1024 * ncx + 128 * (cc + 1)],
                                PT_h[:, 512 * ncx:512 * (ncx + 1)],
                                PT_l[:, 512 * ncx:512 * (ncx + 1)],
                                start=(ncx == 0), stop=(ncx == NCH - 1))
                        cx.inc("pe", last)
                        cx.wait(ve, "pe")
                        cx.wait(ve, "sd", cx.val("sd"))
                        cx.wait(ve, "gd", cx.val("gd"))
                        cx.wait(ve, "dv")
                        cx.inc("dv", ve.tensor_tensor(ev32[:, 0:512], po, rbc_sb, ALU.mult))
                        dv_evac[cc] = cx.val("dv")
                        if kind == "q":
                            cx.wait(sy, "dv")
                            qdst = bass.AP(q_sp.tensor, (128 * t0) * D + 128 * cc,
                                           [[D, 128], [128 * D, TPG], [1, 128]])
                            qsrc = bass.AP(ev32, 0, [[512, 128], [128, TPG], [1, 128]])
                            dma(sy, qdst, qsrc, sem="sd")
                        else:
                            cx.wait(sc, "dv")
                            cx.wait(sc, "gd", cx.val("gd"))
                            cx.inc("ac", sc.activation(stg_h[:, 0:512], ev32[:, 0:512], AF.Copy))
                            cx.wait(gp, "ac")
                            cx.inc("gp", gp.tensor_sub(stg_l[:, 0:512], ev32[:, 0:512],
                                                       stg_h[:, 0:512]))
                            cx.wait(gp, "gp")
                            kof = (512 * half) * CCW + 128 * cc
                            kdims = [[CCW, 128], [128 * CCW, TPG], [1, 128]]
                            ksrc = [[2048, 128], [128, TPG], [1, 128]]
                            dma(gp, bass.AP(cc_out[G].tensor, kof, kdims),
                                bass.AP(stg_h, 0, ksrc), sem="gd")
                            dma(gp, bass.AP(cc_out[G].tensor, kof + 1024, kdims),
                                bass.AP(stg_l, 0, ksrc), sem="gd")
                else:
                    cx.wait(gp, "pe", cx.val("pe"))
                    cx.inc("gp", gp.memset(o_facc[0:128, 0:1040], 1.0))
                    prc = psB[:, 1024:1152]
                    cx.wait(te, "dv")
                    cx.wait(te, "ac", cx.val("ac"))
                    for tau in range(TPG):
                        cx.inc("pe", te.matmul(prc[0:128, tau:tau + 1],
                                               rl[0:1, 128 * tau:128 * (tau + 1)],
                                               ident[0:1, 0:1], is_transpose=True))
                    cx.wait(ve, "pe")
                    for tau in range(TPG):
                        cx.inc("dv", ve.tensor_copy(rl[:, 1024 + tau:1024 + tau + 1],
                                                    prc[0:128, tau:tau + 1]))
                    cx.wait(te, "vd", wvv)
                    dv_vev = {}
                    for tau in range(TPG):
                        po = psA[:, 1024:2048] if tau % 2 else psA[:, 0:1024]
                        if tau < 2:
                            cx.wait(te, "ac", cx.val("ac"))
                        else:
                            cx.wait(te, "dv", dv_vev[tau - 2])
                        for ncx in range(NCH):
                            for hf in range(2):
                                last = mm3(po[:, 512 * hf:512 * (hf + 1)],
                                    PT_h[:, 512 * ncx + 128 * tau:512 * ncx + 128 * (tau + 1)],
                                    PT_l[:, 512 * ncx + 128 * tau:512 * ncx + 128 * (tau + 1)],
                                    vWh[:, 1024 * ncx + 512 * hf:1024 * ncx + 512 * (hf + 1)],
                                    vWl[:, 1024 * ncx + 512 * hf:1024 * ncx + 512 * (hf + 1)],
                                    start=(ncx == 0), stop=(ncx == NCH - 1))
                                if ncx == NCH - 1:
                                    cx.inc("pe", last)
                        cx.wait(ve, "pe")
                        cx.wait(ve, "dv")
                        cx.wait(ve, "gp", cx.val("gp"))
                        vstr = bass.AP(o_facc, 0, [[4096, 128], [65, 16], [1, 64]])
                        cx.inc("dv", ve.tensor_scalar(vstr, po[:, 0:1024],
                                                      rl[:, 1024 + tau:1024 + tau + 1],
                                                      None, ALU.mult))
                        dv_vev[tau] = cx.val("dv")
                        cx.wait(sc, "dv")
                        cx.wait(sc, "gd", cx.val("gd"))
                        cx.inc("ac", sc.activation(stg_h[:, 0:1040], o_facc[:, 0:1040], AF.Copy))
                        cx.wait(gp, "ac")
                        cx.inc("gp", gp.tensor_sub(stg_l[:, 0:1040], o_facc[:, 0:1040],
                                                   stg_h[:, 0:1040]))
                        cx.wait(gp, "gp")
                        rb0 = 512 * half + 128 * tau
                        dma(gp, cc_out[G][rb0:rb0 + 128, 2048:3088],
                            stg_h[:, 0:1040], sem="gd")
                        dma(gp, cc_out[G][rb0:rb0 + 128, 3088:4128],
                            stg_l[:, 0:1040], sem="gd")

                afree[va] = cx.val("pe")

            nxt = ("qkT" if half == 0 else "wbT")
            w_pass("k", "kkT", "kv", 1 - fa, fa,
                   pre_s=[("kv", fa)], pre_v=[("vkT", 1 - fa)])
            w_pass("v", "vkT", "vv", 1 - fa, fa,
                   pre_s=[("vv", fa)], pre_v=[(nxt, 1 - fa)])
            if half == 0:
                w_pass("q", "qkT", "qv", 1 - fa, fa,
                       pre_s=[("qv", fa)], pre_v=[("wbT", 1 - fa)])

        barrier()
        if upto in ("ffn1", "phaseA"):
            return nc, cx, locals()

        def pcol(h, par):
            return 2048 * par + 128 * (h // 2) + 1024 * (h % 2)

        dv_hi = {}
        gp_lo = {}
        ac_f32 = {}
        slot_pe = {0: 0, 1: 0, 2: 0}
        ev_state = {}

        def emit_av(i, np_):
            par = i % 2
            bi = i % 3
            vb = 3072 + 1040 * bi
            cx.wait(te, "dv", dv_hi[i])
            for h in range(0, H, 2):
                pc = pcol(h, par)
                ob = psB[0:65, 128 * h:128 * (h + 1)]
                te.matmul(ob, wb_h[:, vb + 65 * h:vb + 65 * (h + 1)],
                          PT_h[:, pc:pc + 128],
                          start=(i == 0 and h % 4 == 0), stop=False)
                te.matmul(ob, wb_l[:, vb + 65 * h:vb + 65 * (h + 1)],
                          PT_h[:, pc:pc + 128], start=False, stop=False)
                te.matmul(ob, wb_h[:, vb + 65 * h:vb + 65 * (h + 1)],
                          PT_l[:, pc:pc + 128], start=False, stop=False)
            cx.inc("pe", te.nop())
            cx.wait(te, "gp", gp_lo[i])
            for h in range(1, H, 2):
                pc = pcol(h, par)
                ob = psB[0:65, 128 * h:128 * (h + 1)]
                te.matmul(ob, wb_h[:, vb + 65 * h:vb + 65 * (h + 1)],
                          PT_h[:, pc:pc + 128], start=False, stop=False)
                te.matmul(ob, wb_l[:, vb + 65 * h:vb + 65 * (h + 1)],
                          PT_h[:, pc:pc + 128], start=False, stop=False)
                last = te.matmul(ob, wb_h[:, vb + 65 * h:vb + 65 * (h + 1)],
                                 PT_l[:, pc:pc + 128], start=False,
                                 stop=(i == np_ - 1 and h % 4 == 3))
            cx.inc("pe", last)
            slot_pe[bi] = cx.val("pe")

        def emit_evac(jp, np_):
            te.wait_ge(cx.sem("pe"), cx.val("pe"))
            emit_av(np_ - 1, np_)
            te.wait_ge(cx.sem("pe"), cx.val("pe"))
            cx.wait(ve, "pe")
            cx.inc("dv", ve.tensor_copy(rl[64:65, 0:2048], psB[64:65, 0:2048]))
            dv_rec = cx.val("dv")
            cx.wait(te, "ac", ac_f32[np_ - 1])
            cx.wait(te, "dv", dv_rec)
            for h in range(H):
                cx.inc("pe", te.matmul(psA[0:64, 128 * h:128 * (h + 1)],
                          ones_f[64:65, 0:64], rl[64:65, 128 * h:128 * (h + 1)],
                          start=True, stop=True, tile_position=(64, 0)))
            te.wait_ge(cx.sem("pe"), cx.val("pe"))
            cx.wait(ve, "pe")
            cx.wait(ve, "gd", ev_state.get("gd_odma", 0))
            cx.inc("dv", ve.tensor_copy(o_facc[0:64, 0:2048], psA[0:64, 0:2048]))
            dv_copy = cx.val("dv")
            cx.wait(ve, "dv")
            cx.inc("dv", ve.reciprocal(o_facc[0:64, 0:2048], o_facc[0:64, 0:2048]))
            cx.wait(ve, "dv")
            for h in range(H):
                ob = ht if h < 8 else h2t
                ve.tensor_tensor(ob[0:64, 128 * (h % 8):128 * (h % 8 + 1)],
                                 psB[0:64, 128 * h:128 * (h + 1)],
                                 o_facc[0:64, 128 * h:128 * (h + 1)], ALU.mult)
            cx.inc("dv", ve.engine_nop())
            dv_mult = cx.val("dv")
            cx.wait(gp, "dv", dv_mult)
            for c in range(NCH):
                ob = ht if c < 4 else h2t
                srcb = bass.AP(ob, 256 * (c % 4), [[1024, 64], [128, 2], [1, 128]])
                dst = bass.AP(o_sp.tensor, (128 * jp) * D + 128 * c,
                              [[D, 64], [64 * D, 2], [1, 128]])
                dma(gp, dst, srcb, sem="gd")
            ev_state["gd_odma"] = cx.val("gd")
            ev_state["dv_copy"] = dv_copy

        for j in range(NT if max_j is None else max_j):
            cx.wait(sy, "pe", cx.val("pe"))
            cx.wait(sy, "sd")
            dma(sy, stg[:, 0:1024], q_sp[128 * j:128 * (j + 1), 0:1024])
            cx.wait(gp, "gd", cx.val("gd"))
            dma(gp, wa_h[:, 0:1024], q_sp[128 * j:128 * (j + 1), 0:1024], sem="gd")
            cx.wait(gp, "xd")
            cx.wait(gp, "gd")
            cx.inc("gp", gp.tensor_sub(wa_l[:, 0:1024], stg[:, 0:1024], wa_h[:, 0:1024]))
            gp_q = cx.val("gp")
            cx.wait(sc, "sd", cx.val("sd"))
            npair = 2 * j + 2
            if sdpa_stage < 2:
                continue
            if sdpa_stage >= 5 and j >= 1:
                emit_evac(j - 1, npair - 2)

            for i in range(npair):
                par = i % 2
                bi = i % 3
                Gk, rb0 = ccpos(i)
                gate = slot_pe[bi]
                cx.wait(sy, "pe", gate)
                dma(sy, wb_h[:, 1024 * bi:1024 * (bi + 1)],
                    cc_out[Gk][rb0:rb0 + 128, 0:1024])
                dma(sy, wb_h[:, 3072 + 1040 * bi:3072 + 1040 * (bi + 1)],
                    cc_out[Gk][rb0:rb0 + 128, 2048:3088])
                cx.wait(sc, "pe", gate)
                dma(sc, wb_l[:, 1024 * bi:1024 * (bi + 1)],
                    cc_out[Gk][rb0:rb0 + 128, 1024:2048], sem="vd")
                dma(sc, wb_l[:, 3072 + 1040 * bi:3072 + 1040 * (bi + 1)],
                    cc_out[Gk][rb0:rb0 + 128, 3088:4128], sem="vd")
                kvld = cx.val("xd")
                kvld_v = cx.val("vd")
                cx.wait(te, "ac", ac_f32[i - 1] if i >= 1 else cx.val("ac"))
                if i == 0 and j >= 1 and sdpa_stage >= 5:
                    cx.wait(te, "dv", ev_state["dv_copy"])
                cx.wait(te, "xd", kvld)
                cx.wait(te, "vd", kvld_v)
                cx.wait(te, "gp", gp_q)
                for c in range(NCH):
                    for sub in range(2):
                        h = 2 * c + sub
                        tp = (64 * sub, 0)
                        kcol = 1024 * bi + 128 * c
                        qcol = 128 * c
                        pc = pcol(h, 0)
                        cx.inc("pe", mm3(psA[:, pc:pc + 128],
                            wb_h[64 * sub:64 * (sub + 1), kcol:kcol + 128],
                            wb_l[64 * sub:64 * (sub + 1), kcol:kcol + 128],
                            wa_h[64 * sub:64 * (sub + 1), qcol:qcol + 128],
                            wa_l[64 * sub:64 * (sub + 1), qcol:qcol + 128],
                            start=True, stop=True, tp=tp))
                pe_s = cx.val("pe")
                te.wait_ge(cx.sem("pe"), pe_s)
                if sdpa_stage >= 4 and i >= 1:
                    emit_av(i - 1, npair)
                if sdpa_stage < 3:
                    continue
                if i >= npair - 2:
                    if i == npair - 2:
                        mkap = bass.AP(maskA, 0, [[128, 128], [0, 16], [1, 128]])
                    else:
                        mkap = bass.AP(maskB, 0, [[1, 128], [0, 16], [0, 128]])
                    cx.wait(ve, "pe", pe_s)
                    cx.inc("dv", ve.tensor_tensor(psA[:, 0:2048], psA[:, 0:2048],
                                                  mkap, ALU.add))
                    cx.wait(sc, "dv")
                else:
                    cx.wait(sc, "pe", pe_s)
                po2 = 2048 * par
                pstg = stg[:, 0:2048] if par == 0 else o_facc[:, 2048:4096]
                if i >= 2:
                    cx.wait(sc, "gp", gp_lo[i - 2])
                elif i == 0:
                    cx.wait(sc, "gp", gp_q)
                cx.inc("ac", sc.activation(pstg, psA[:, 0:2048],
                                           AF.Exp, scale=0.125))
                ac_f32[i] = cx.val("ac")
                cx.wait(ve, "ac", ac_f32[i])
                cx.inc("dv", ve.tensor_copy(PT_h[:, po2:po2 + 2048], pstg))
                dv_cast = cx.val("dv")
                cx.wait(ve, "dv")
                cx.inc("dv", ve.tensor_sub(PT_l[:, po2:po2 + 1024], pstg[:, 0:1024],
                                           PT_h[:, po2:po2 + 1024]))
                dv_hi[i] = cx.val("dv")
                cx.wait(gp, "dv", dv_cast)
                cx.inc("gp", gp.tensor_sub(PT_l[:, po2 + 1024:po2 + 2048],
                                           pstg[:, 1024:2048],
                                           PT_h[:, po2 + 1024:po2 + 2048]))
                gp_lo[i] = cx.val("gp")
        if sdpa_stage >= 5:
            NTj = NT if max_j is None else max_j
            if NTj >= 1:
                emit_evac(NTj - 1, 2 * NTj)
        barrier()
        if upto == "sdpa":
            return nc, cx, locals()

        afree[0] = cx.val("pe")
        afree[1] = cx.val("pe")
        for G in range(n_groups):
            pa = G % 2
            ks, vs = 1 - pa, pa
            p2h, p2l = AR[ks]
            v2h, v2l = AR[vs]
            f2h, f2l = AR[ks]
            t0 = G * TPG
            cx.wait(sy, "pe", cx.val("pe"))
            cx.wait(gp, "pe", cx.val("pe"))
            for tau in range(TPG):
                t = t0 + tau
                half = tau % 2
                cx.wait(sy, "dv", cx.val("dv"))
                dma(sy, stg[:, 1024 * half:1024 * (half + 1)],
                    o_sp[128 * t:128 * (t + 1), :])
                dsth = bass.AP(h3T_h, 128 * tau, [[4096, 128], [512, 8], [1, 128]])
                dstl = bass.AP(h3T_l, 128 * tau, [[4096, 128], [512, 8], [1, 128]])
                cx.wait(gp, "dv", cx.val("dv"))
                dma(gp, dsth, o_sp[128 * t:128 * (t + 1), :], sem="gd")
                cx.wait(ve, "xd")
                cx.wait(ve, "gd")
                srcf = bass.AP(stg, 1024 * half, [[2048, 128], [128, 8], [1, 128]])
                cx.inc("dv", ve.tensor_tensor(dstl, srcf, dsth, ALU.subtract))

            if G == 0:
                prefetch("pkT", ks)
            wv = wv_of[("pkT", ks)]
            prefetch("pv", vs)
            wv_pv = wv_of[("pv", vs)]
            lrow = psB[0:1, 0:512]
            cx.wait(te, "dv", cx.val("dv"))
            cx.wait(te, "ac", cx.val("ac"))
            ac_after = {}
            gp_after = {-1: cx.val("gp")}
            for ncx in range(NCH):
                sp = psA[:, 512 * (ncx % 2):512 * (ncx % 2 + 1)]
                if ncx >= 2:
                    cx.wait(te, "ac", ac_after[ncx - 2])
                else:
                    cx.wait(te, "ac", cx.val("ac"))
                cx.wait(te, "xd", wv)
                for cc in range(NCH):
                    last = mm3(sp,
                        p2h[:, 1024 * cc + 128 * ncx:1024 * cc + 128 * (ncx + 1)],
                        p2l[:, 1024 * cc + 128 * ncx:1024 * cc + 128 * (ncx + 1)],
                        h3T_h[:, 512 * cc:512 * (cc + 1)],
                        h3T_l[:, 512 * cc:512 * (cc + 1)],
                        start=(cc == 0), stop=(cc == NCH - 1))
                cx.inc("pe", last)
                cx.wait(sc, "pe")
                if ncx >= 2:
                    sfe, sfv = gp_after[ncx - 2]
                    cx.wait(sc, sfe, sfv)
                pf = Pf[ncx % 2]
                cx.inc("ac", sc.activation(pf[:], sp, AF.Exp, scale=1.0 / 32.0))
                ac_after[ncx] = cx.val("ac")
                cx.wait(ve, "ac")
                cx.inc("dv", ve.tensor_copy(PT_h[:, 512 * ncx:512 * (ncx + 1)], pf[:]))
                if ncx % 2 == 0:
                    cx.wait(gp, "dv")
                    cx.inc("gp", gp.tensor_sub(PT_l[:, 512 * ncx:512 * (ncx + 1)], pf[:],
                                               PT_h[:, 512 * ncx:512 * (ncx + 1)]))
                    gp_after[ncx] = ("gp", cx.val("gp"))
                else:
                    cx.wait(ve, "dv")
                    cx.inc("dv", ve.tensor_sub(PT_l[:, 512 * ncx:512 * (ncx + 1)], pf[:],
                                               PT_h[:, 512 * ncx:512 * (ncx + 1)]))
                    gp_after[ncx] = ("dv", cx.val("dv"))
            cx.wait(te, "gp")
            cx.wait(te, "dv")
            for ncx in range(NCH):
                te.matmul(lrow, ones_b[:, 0:1], PT_h[:, 512 * ncx:512 * (ncx + 1)],
                          start=(ncx == 0), stop=False)
                last = te.matmul(lrow, ones_b[:, 0:1], PT_l[:, 512 * ncx:512 * (ncx + 1)],
                                 start=False, stop=(ncx == NCH - 1))
            cx.inc("pe", last)
            cx.wait(ve, "pe")
            cx.inc("dv", ve.reciprocal(rl[0:1, 0:512], lrow))
            prc = psB[:, 1024:1152]
            cx.wait(te, "dv")
            cx.wait(te, "ac", cx.val("ac"))
            for tau in range(TPG):
                cx.inc("pe", te.matmul(prc[0:128, tau:tau + 1],
                                       rl[0:1, 128 * tau:128 * (tau + 1)],
                                       ident[0:1, 0:1], is_transpose=True))
            cx.wait(ve, "pe")
            for tau in range(TPG):
                cx.inc("dv", ve.tensor_copy(rl[:, 1024 + tau:1024 + tau + 1],
                                            prc[0:128, tau:tau + 1]))
            afree[ks] = cx.val("pe")
            prefetch("fkT", ks, c0=0, c1=1024)
            cx.wait(te, "vd", wv_pv)
            dv_pv = {}
            dv_h4 = {}
            for tau in range(TPG):
                t = t0 + tau
                po = psA[:, 1024:2048] if tau % 2 == 0 else psA[:, 0:1024]
                cx.wait(sy, "dv", cx.val("dv"))
                cx.wait(sy, "pe", cx.val("pe"))
                cx.wait(sy, "ac", cx.val("ac"))
                dma(sy, stg[:, 0:1024], h2_sp[128 * t:128 * (t + 1), :])
                dma(sy, xt[tau][:], x1_sp[128 * t:128 * (t + 1), :])
                h2v = cx.val("xd")
                if tau >= 2:
                    cx.wait(te, "dv", dv_pv[tau - 2])
                elif tau == 1:
                    cx.wait(te, "ac", cx.val("ac"))
                for ncx in range(NCH):
                    for hf in range(2):
                        last = mm3(po[:, 512 * hf:512 * (hf + 1)],
                            PT_h[:, 512 * ncx + 128 * tau:512 * ncx + 128 * (tau + 1)],
                            PT_l[:, 512 * ncx + 128 * tau:512 * ncx + 128 * (tau + 1)],
                            v2h[:, 1024 * ncx + 512 * hf:1024 * ncx + 512 * (hf + 1)],
                            v2l[:, 1024 * ncx + 512 * hf:1024 * ncx + 512 * (hf + 1)],
                            start=(ncx == 0), stop=(ncx == NCH - 1))
                        if ncx == NCH - 1:
                            cx.inc("pe", last)
                cx.wait(ve, "pe")
                cx.wait(ve, "xd", h2v)
                cx.wait(ve, "dv")
                cx.wait(ve, "sd", cx.val("sd"))
                cx.inc("dv", ve.scalar_tensor_tensor(ht[:], po[:, 0:1024],
                                                     rl[:, 1024 + tau:1024 + tau + 1],
                                                     stg[:, 0:1024], ALU.mult, ALU.add))
                dv_pv[tau] = cx.val("dv")
                cx.wait(ve, "dv")
                cx.inc("dv", ve.tensor_add(xt[tau][:], xt[tau][:], ht[:]))
                ln(ht[:])
                h4b = h2t if tau % 2 == 0 else ht2
                cx.wait(ve, "dv")
                cx.inc("dv", ln_apply(h4b[:], ht[:]))
                dv_h4[tau] = cx.val("dv")
                if tau >= 1:
                    pb = h2t if (tau - 1) % 2 == 0 else ht2
                    cx.wait(te, "dv", dv_h4[tau - 1])
                    transp_split(pb, lambda c: 128 * c, NCH, h4T_h, h4T_l,
                                 lambda c, _t=tau - 1: 512 * c + 128 * _t)

            pb = h2t if (TPG - 1) % 2 == 0 else ht2
            cx.wait(te, "dv", dv_h4[TPG - 1])
            transp_split(pb, lambda c: 128 * c, NCH, h4T_h, h4T_l,
                         lambda c: 512 * c + 128 * (TPG - 1))
            cx.wait(gp, "dv", cx.val("dv"))
            cx.inc("gp", gp.memset(o_facc[:], 0.0))
            lrow = psB[0:1, 0:512]
            afree[vs] = cx.val("pe")
            for Q in range(4):
                wv = wv_of[("fkT", ks)]
                prefetch("fv", vs, r0=1024 * Q, r1=1024 * (Q + 1))
                wv_fv = wv_of[("fv", vs)]
                cx.wait(te, "dv", cx.val("dv"))
                ac_after = {}
                gp_after = {-1: cx.val("gp")}
                for ncx in range(NCH):
                    sp = psA[:, 512 * (ncx % 2):512 * (ncx % 2 + 1)]
                    if ncx >= 2:
                        cx.wait(te, "ac", ac_after[ncx - 2])
                    else:
                        cx.wait(te, "ac", cx.val("ac"))
                    cx.wait(te, "xd", wv)
                    for cc in range(NCH):
                        last = mm3(sp,
                            p2h[:, 1024 * cc + 128 * ncx:1024 * cc + 128 * (ncx + 1)],
                            p2l[:, 1024 * cc + 128 * ncx:1024 * cc + 128 * (ncx + 1)],
                            h4T_h[:, 512 * cc:512 * (cc + 1)],
                            h4T_l[:, 512 * cc:512 * (cc + 1)],
                            start=(cc == 0), stop=(cc == NCH - 1))
                    cx.inc("pe", last)
                    cx.wait(sc, "pe")
                    if ncx >= 2:
                        sfe, sfv = gp_after[ncx - 2]
                        cx.wait(sc, sfe, sfv)
                    pf = Pf[ncx % 2]
                    cx.inc("ac", sc.activation(pf[:], sp, AF.Exp, scale=1.0 / 32.0))
                    ac_after[ncx] = cx.val("ac")
                    cx.wait(ve, "ac")
                    cx.inc("dv", ve.tensor_copy(PT_h[:, 512 * ncx:512 * (ncx + 1)], pf[:]))
                    if ncx % 2 == 0:
                        cx.wait(gp, "dv")
                        cx.inc("gp", gp.tensor_sub(PT_l[:, 512 * ncx:512 * (ncx + 1)], pf[:],
                                                   PT_h[:, 512 * ncx:512 * (ncx + 1)]))
                        gp_after[ncx] = ("gp", cx.val("gp"))
                    else:
                        cx.wait(ve, "dv")
                        cx.inc("dv", ve.tensor_sub(PT_l[:, 512 * ncx:512 * (ncx + 1)], pf[:],
                                                   PT_h[:, 512 * ncx:512 * (ncx + 1)]))
                        gp_after[ncx] = ("dv", cx.val("dv"))
                cx.wait(te, "gp")
                cx.wait(te, "dv")
                for ncx in range(NCH):
                    te.matmul(lrow, ones_b[:, 0:1], PT_h[:, 512 * ncx:512 * (ncx + 1)],
                              start=(Q == 0 and ncx == 0), stop=False)
                    last = te.matmul(lrow, ones_b[:, 0:1], PT_l[:, 512 * ncx:512 * (ncx + 1)],
                                     start=False, stop=(Q == 3 and ncx == NCH - 1))
                cx.inc("pe", last)
                afree[ks] = cx.val("pe")
                if Q < 3:
                    prefetch("fkT", ks, c0=1024 * (Q + 1), c1=1024 * (Q + 2))
                else:
                    prefetch("wbT", ks)
                cx.wait(te, "vd", wv_fv)
                dv_fv = {}
                for tau in range(TPG):
                    po = psA[:, 1024:2048] if tau % 2 == 0 else psA[:, 0:1024]
                    if tau >= 2:
                        cx.wait(te, "dv", dv_fv[tau - 2])
                    elif tau == 1:
                        cx.wait(te, "ac", cx.val("ac"))
                    else:
                        cx.wait(te, "dv", cx.val("dv"))
                    for ncx in range(NCH):
                        for hf in range(2):
                            last = mm3(po[:, 512 * hf:512 * (hf + 1)],
                                PT_h[:, 512 * ncx + 128 * tau:512 * ncx + 128 * (tau + 1)],
                                PT_l[:, 512 * ncx + 128 * tau:512 * ncx + 128 * (tau + 1)],
                                v2h[:, 1024 * ncx + 512 * hf:1024 * ncx + 512 * (hf + 1)],
                                v2l[:, 1024 * ncx + 512 * hf:1024 * ncx + 512 * (hf + 1)],
                                start=(ncx == 0), stop=(ncx == NCH - 1))
                            if ncx == NCH - 1:
                                cx.inc("pe", last)
                    cx.wait(ve, "pe")
                    cx.wait(ve, "gp", cx.val("gp"))
                    cx.inc("dv", ve.tensor_add(o_facc[:, 1024 * tau:1024 * (tau + 1)],
                                               o_facc[:, 1024 * tau:1024 * (tau + 1)],
                                               po[:, 0:1024]))
                    dv_fv[tau] = cx.val("dv")
                    cx.wait(ve, "dv")
                afree[vs] = cx.val("pe")
            cx.wait(ve, "pe")
            cx.inc("dv", ve.reciprocal(rl[0:1, 0:512], lrow))
            prc = psB[:, 1024:1152]
            cx.wait(te, "dv")
            for tau in range(TPG):
                cx.inc("pe", te.matmul(prc[0:128, tau:tau + 1],
                                       rl[0:1, 128 * tau:128 * (tau + 1)],
                                       ident[0:1, 0:1], is_transpose=True))
            cx.wait(ve, "pe")
            for tau in range(TPG):
                cx.inc("dv", ve.tensor_copy(rl[:, 1024 + tau:1024 + tau + 1],
                                            prc[0:128, tau:tau + 1]))
            cx.wait(ve, "dv")
            for tau in range(TPG):
                cx.inc("dv", ve.scalar_tensor_tensor(
                    xt[tau][:], o_facc[:, 1024 * tau:1024 * (tau + 1)],
                    rl[:, 1024 + tau:1024 + tau + 1], xt[tau][:], ALU.mult, ALU.add))

            xd0 = wv_of[("wbT", ks)]
            if G + 1 < n_groups:
                prefetch("pkT", vs)
            lnbuf = [(ht, h1T_h, h1T_l), (ht2, h1TB_h, h1TB_l)]
            cx.wait(sc, "xd", xd0)
            cx.wait(sc, "dv", cx.val("dv"))
            ln(xt[0][:], sb=8)
            cx.wait(ve, "dv")
            cx.wait(ve, "sd", cx.val("sd"))
            cx.inc("dv", ln_apply(lnbuf[0][0][:], xt[0][:], sb=8))
            cx.wait(te, "dv")
            transp_split(lnbuf[0][0], lambda c: 128 * c, NCH,
                         lnbuf[0][1], lnbuf[0][2], lambda c: 128 * c, alt=True)
            dv_h1 = cx.val("dv")
            dv_stt = cx.val("dv")
            for tau in range(TPG):
                t = t0 + tau
                xa = xt[tau][:]
                hb, hTh, hTl = lnbuf[tau % 2]
                pbase = psA[:, 0:1024]
                p16 = psA[:, 1024:1040]
                cx.wait(te, "dv", dv_h1)
                cx.wait(te, "xd", xd0)
                for cc in range(NCH):
                    lh = hTh[:, 128 * cc:128 * (cc + 1)]
                    llo = hTl[:, 128 * cc:128 * (cc + 1)]
                    mm3(p16[:, 0:16], lh, llo, rmsb_h[:, 16 * cc:16 * (cc + 1)],
                        rmsb_l[:, 16 * cc:16 * (cc + 1)], start=(cc == 0), stop=False)
                cx.inc("pe", te.matmul(p16[:, 0:8], ones_f[0:1, :], rbt[:], start=False, stop=True))
                pe_p16 = cx.val("pe")
                cx.wait(te, "dv", dv_stt)
                first = True
                for cc in range(NCH):
                    lh = hTh[:, 128 * cc:128 * (cc + 1)]
                    llo = hTl[:, 128 * cc:128 * (cc + 1)]
                    for hf in range(2):
                        mm3(pbase[:, 512 * hf:512 * (hf + 1)], lh, llo,
                            f2h[:, 1024 * cc + 512 * hf:1024 * cc + 512 * (hf + 1)],
                            f2l[:, 1024 * cc + 512 * hf:1024 * cc + 512 * (hf + 1)],
                            start=first, stop=False)
                    first = False
                te.matmul(pbase[:, 0:512], ones_b[0:1, :], bbp[0:1, 0:512], start=False, stop=False)
                te.matmul(pbase[:, 512:1024], ones_b[0:1, :], bbp[0:1, 512:1024], start=False, stop=False)
                te.matmul(pbase[:, 0:512], ones_b[0:1, :], bbp[0:1, 1024:1536], start=False, stop=False)
                cx.inc("pe", te.matmul(pbase[:, 512:1024], ones_b[0:1, :], bbp[0:1, 1536:2048], start=False, stop=False))
                cx.wait(sc, "pe", pe_p16)
                cx.inc("ac", sc.activation(g8[:], p16[:, 0:8], AF.Exp, accum_out=stat[:, 6:7]))
                cx.wait(ve, "ac")
                cx.inc("dv", ve.reciprocal(stat[:, 7:8], stat[:, 6:7]))
                cx.wait(ve, "dv")
                cx.inc("dv", ve.tensor_scalar(g8b[:], g8[:], stat[:, 7:8], None, ALU.mult))
                cx.wait(ve, "dv")
                cx.inc("dv", ve.tensor_copy(g8[:], g8b[:]))
                cx.wait(ve, "dv")
                for e in range(8):
                    cx.inc("dv", ve.tensor_scalar(P64[:, 8 * e:8 * (e + 1)], p16[:, 8:16],
                                     g8[:, e:e + 1], 2.0, ALU.mult, ALU.mult))
                if tau + 1 < TPG:
                    nb, nTh, nTl = lnbuf[(tau + 1) % 2]
                    cx.wait(sc, "xd", xd0)
                    ln(xt[tau + 1][:], sb=8)
                    cx.wait(ve, "dv")
                    cx.wait(ve, "sd", cx.val("sd"))
                    cx.inc("dv", ln_apply(nb[:], xt[tau + 1][:], sb=8))
                    cx.wait(te, "dv")
                    transp_split(nb, lambda c: 128 * c, NCH, nTh, nTl,
                                 lambda c: 128 * c, alt=True)
                    dv_h1 = cx.val("dv")
                pt64 = psB[:, 1024:1152]
                cx.wait(te, "dv")
                cx.inc("pe", te.matmul(pt64[0:64, 0:128], P64[:], ident[:], is_transpose=True))
                cx.wait(sc, "pe")
                cx.inc("ac", sc.activation(P64h[:], pt64[0:64, 0:128], AF.Copy))
                cx.wait(ve, "ac")
                cx.inc("dv", ve.tensor_sub(P64l[:], pt64[0:64, 0:128], P64h[:]))
                cx.wait(te, "dv")
                for hf in range(2):
                    cx.inc("pe", mm3(pbase[:, 512 * hf:512 * (hf + 1)], P64h[:], P64l[:],
                        b2h[:, 512 * hf:512 * (hf + 1)], b2l[:, 512 * hf:512 * (hf + 1)],
                        start=False, stop=True))
                cx.wait(ve, "pe")
                cx.wait(ve, "sd", cx.val("sd"))
                cx.inc("dv", ve.scalar_tensor_tensor(ht[:], pbase[:, 0:1024], 0.5, xa,
                                                     ALU.mult, ALU.add))
                dv_stt = cx.val("dv")
                cx.wait(sy, "dv")
                dma(sy, y_out[128 * t:128 * (t + 1), :], ht[:], sem="sd")
            afree[ks] = cx.val("pe")
        barrier()
        return nc, cx, locals()




from concourse.bass_utils import run_bass_kernel_spmd


def _host_inputs(core, x_batch, W):
    import ml_dtypes
    s = core % 2
    NT = 16
    tri16 = np.where(np.arange(128)[:, None] <= np.arange(128)[None, :],
                     0.0, NEG).astype(ml_dtypes.bfloat16)
    xr = x_batch.reshape(NT * 2, 128, 1024)
    return {
        "x": np.ascontiguousarray(xr[s::2].reshape(NT * 128, 1024)),
        "xp": np.ascontiguousarray(xr[1 - s::2].reshape(NT * 128, 1024)),
        "wbT": W["wbT"], "bb": W["bb"], "rb": W["rb"], "rmid": W["rmid"],
        "b2": W["b2"], "qkT": W["qkT"], "qv": W["qv"], "kkT": W["kkT"],
        "kv": W["kv"], "vkT": W["vkT"], "vv": W["vv"], "pkT": W["pkT"],
        "pv": W["pv"], "fkT": W["fkT"], "fv": W["fv"],
        "maskA": tri16,
        "maskB": (np.full((128, 1), NEG, dtype=ml_dtypes.bfloat16) if s == 0
                  else np.zeros((128, 1), dtype=ml_dtypes.bfloat16)),
        "ident": np.eye(128, dtype=np.float32),
    }


def kernel(L0_attn_x, L0_attn_qK, L0_attn_qV, L0_attn_kK, L0_attn_kV,
           L0_attn_vK, L0_attn_vV, L0_attn_pK, L0_attn_pV, L0_attn_fK,
           L0_attn_fV, L0_ffn_base_w, L0_ffn_base_b, L0_ffn_lora_A,
           L0_ffn_lora_B, L0_ffn_router_w, L0_ffn_router_b):
    x = np.ascontiguousarray(np.asarray(L0_attn_x, dtype=np.float32))
    W = {
        "wbT": np.ascontiguousarray(np.asarray(L0_ffn_base_w).T),
        "bb": np.asarray(L0_ffn_base_b).reshape(1, 1024),
        "rb": np.asarray(L0_ffn_router_b).reshape(1, 8),
        "rmid": np.ascontiguousarray(np.concatenate(
            [np.asarray(L0_ffn_router_w).T, np.asarray(L0_ffn_lora_A)], axis=1)),
        "b2": np.ascontiguousarray(np.asarray(L0_ffn_lora_B).reshape(64, 1024)),
        "qkT": np.ascontiguousarray(np.asarray(L0_attn_qK).T),
        "qv": np.ascontiguousarray(np.asarray(L0_attn_qV)),
        "kkT": np.ascontiguousarray(np.asarray(L0_attn_kK).T),
        "kv": np.ascontiguousarray(np.asarray(L0_attn_kV)),
        "vkT": np.ascontiguousarray(np.asarray(L0_attn_vK).T),
        "vv": np.ascontiguousarray(np.asarray(L0_attn_vV)),
        "pkT": np.ascontiguousarray(np.asarray(L0_attn_pK).T),
        "pv": np.ascontiguousarray(np.asarray(L0_attn_pV)),
        "fkT": np.ascontiguousarray(np.asarray(L0_attn_fK).T),
        "fv": np.ascontiguousarray(np.asarray(L0_attn_fV)),
    }
    nc, cx, _ = build(n_groups=4, upto="full")
    in_maps = [_host_inputs(c, x[c // 2], W) for c in range(8)]
    res = run_bass_kernel_spmd(nc, in_maps, core_ids=list(range(8)))
    out = np.empty((4, 4096, 1024), dtype=np.float32)
    for c in range(8):
        b, s = c // 2, c % 2
        y = np.asarray(res.results[c]["y"]).reshape(16, 128, 1024)
        out[b].reshape(32, 128, 1024)[s::2] = y
    return out


# version 26
# speedup vs baseline: 1.0651x; 1.0158x over previous
import contextlib
import numpy as np
import concourse.bass as bass
import concourse.mybir as mybir

F32 = mybir.dt.float32
BF16 = mybir.dt.bfloat16
AF = mybir.ActivationFunctionType
ALU = mybir.AluOpType

D = 1024
NCH = 8
H = 16
TPG = 4
NEG = -1e30
EPS = 1e-5

ENGS = ["sync", "tensor", "scalar", "vector", "gpsimd"]


class Cx:
    DMA_PROD = {"xd": "sync", "sd": "sync", "gd": "gpsimd", "vd": "scalar"}

    def __init__(self, nc, es):
        self.nc = nc
        self.es = es
        self._s = {}
        self._bound = {}

    def _rec(self, n):
        if n not in self._s:
            self._s[n] = [self.es.enter_context(self.nc.semaphore(n)), 0]
            self._bound[n] = 0
        return self._s[n]

    def sem(self, n):
        return self._rec(n)[0]

    def inc(self, n, instr, k=1):
        r = self._rec(n)
        instr.then_inc(r[0], k)
        r[1] += k
        return r[1]

    def val(self, n):
        return self._rec(n)[1]

    def wait(self, eng, n, v=None):
        r = self._rec(n)
        tv = r[1] if v is None else v
        prod = self.DMA_PROD.get(n)
        if prod is not None and tv > self._bound[n]:
            pe_eng = getattr(self.nc, prod)
            if getattr(eng, "engine", None) != getattr(pe_eng, "engine", None):
                pe_eng.wait_ge(r[0], tv)
            self._bound[n] = tv
        eng.wait_ge(r[0], tv)


def build(n_groups=4, upto="full", max_g2=None, max_tiles=None, tile_stage=99, max_j=None, sdpa_stage=99):
    NT = TPG * n_groups
    NTOK = 128 * NT

    nc = bass.Bass("TRN2", target_bir_lowering=False)
    es = contextlib.ExitStack()
    cx = Cx(nc, es)

    x_in = nc.dram_tensor("x", [NTOK, D], F32, kind="ExternalInput").ap()
    xp_in = nc.dram_tensor("xp", [NTOK, D], F32, kind="ExternalInput").ap()
    y_out = nc.dram_tensor("y", [NTOK, D], F32, kind="ExternalOutput").ap()
    w_in = {}
    for nm, shp in [("wbT", [D, D]), ("bb", [1, D]), ("rb", [1, 8]),
                    ("rmid", [D, 16]), ("b2", [64, D]),
                    ("qkT", [D, D]), ("qv", [D, D]), ("kkT", [D, D]), ("kv", [D, D]),
                    ("vkT", [D, D]), ("vv", [D, D]), ("pkT", [D, D]), ("pv", [D, D]),
                    ("fkT", [D, 4 * D]), ("fv", [4 * D, D]),

                    ("ident", [128, 128])]:
        w_in[nm] = nc.dram_tensor(nm, shp, F32, kind="ExternalInput").ap()
    w_in["maskA"] = nc.dram_tensor("maskA", [128, 128], BF16, kind="ExternalInput").ap()
    w_in["maskB"] = nc.dram_tensor("maskB", [128, 1], BF16, kind="ExternalInput").ap()

    wsp = {}
    for nm in ["wbT", "rmid", "qkT", "qv", "kkT", "kv", "vkT", "vv",
               "pkT", "pv", "fkT", "fv"]:
        shp = list(w_in[nm].shape)
        wsp[nm + "_h"] = nc.dram_tensor(nm + "_h", shp, BF16).ap()
        wsp[nm + "_l"] = nc.dram_tensor(nm + "_l", shp, BF16).ap()

    x1_sp = nc.dram_tensor("x1_sp", [NTOK, D], F32).ap()
    h2_sp = nc.dram_tensor("h2_sp", [NTOK, D], F32).ap()
    o_sp = nc.dram_tensor("o_sp", [NTOK, D], F32).ap()
    q_sp = nc.dram_tensor("q_sp", [NTOK, D], F32).ap()
    CCW = 4128
    cc_out = [nc.dram_tensor(f"cc_out{g}", [2 * TPG * 128, CCW], BF16, addr_space="Local").ap()
              for g in range(n_groups)]

    def ccpos(g):
        return g // (2 * TPG), 512 * (g % 2) + 128 * ((g % (2 * TPG)) // 2)

    with es:
        ident = es.enter_context(nc.sbuf_tensor("identt", [128, 128], F32))
        ones_f = es.enter_context(nc.sbuf_tensor("ones_f", [128, 128], F32))
        ones_b = es.enter_context(nc.sbuf_tensor("ones_b", [128, 128], BF16))
        maskA = es.enter_context(nc.sbuf_tensor("maskA_t", [128, 128], BF16))
        maskB = es.enter_context(nc.sbuf_tensor("maskB_t", [128, 1], BF16))
        bbp = es.enter_context(nc.sbuf_tensor("bbp", [1, 2 * D], BF16))
        bbh, bbl = bbp[0:1, 0:D], bbp[0:1, D:2 * D]
        rbt = es.enter_context(nc.sbuf_tensor("rbt", [1, 8], F32))
        b2h = es.enter_context(nc.sbuf_tensor("b2h", [64, D], BF16))
        b2l = es.enter_context(nc.sbuf_tensor("b2l", [64, D], BF16))
        rmsb_h = es.enter_context(nc.sbuf_tensor("rmsb_h", [128, 128], BF16))
        rmsb_l = es.enter_context(nc.sbuf_tensor("rmsb_l", [128, 128], BF16))
        wa_h = es.enter_context(nc.sbuf_tensor("wa_h", [128, NCH * 1024], BF16))
        wa_l = es.enter_context(nc.sbuf_tensor("wa_l", [128, NCH * 1024], BF16))
        wb_h = es.enter_context(nc.sbuf_tensor("wb_h", [128, NCH * 1024], BF16))
        wb_l = es.enter_context(nc.sbuf_tensor("wb_l", [128, NCH * 1024], BF16))
        stg = es.enter_context(nc.sbuf_tensor("stg", [128, 2048], F32))
        stg_h = es.enter_context(nc.sbuf_tensor("stg_h", [128, 2048], BF16))
        stg_l = es.enter_context(nc.sbuf_tensor("stg_l", [128, 2048], BF16))
        h4T_h = es.enter_context(nc.sbuf_tensor("h4T_h", [128, NCH * 512], BF16))
        h4T_l = es.enter_context(nc.sbuf_tensor("h4T_l", [128, NCH * 512], BF16))
        o_facc = es.enter_context(nc.sbuf_tensor("o_facc", [128, 4096], F32))
        h3T_h = es.enter_context(nc.sbuf_tensor("h3T_h", [128, NCH * 512], BF16))
        h3T_l = es.enter_context(nc.sbuf_tensor("h3T_l", [128, NCH * 512], BF16))
        PT_h = es.enter_context(nc.sbuf_tensor("PT_h", [128, NCH * 512], BF16))
        PT_l = es.enter_context(nc.sbuf_tensor("PT_l", [128, NCH * 512], BF16))
        Pf = [es.enter_context(nc.sbuf_tensor(f"Pf{i}", [128, 512], F32)) for i in range(2)]
        xt = [es.enter_context(nc.sbuf_tensor(f"xt{i}", [128, D], F32)) for i in range(TPG)]
        ht = es.enter_context(nc.sbuf_tensor("ht", [128, D], F32))
        ht2 = es.enter_context(nc.sbuf_tensor("ht2", [128, D], F32))
        h1TB_h = es.enter_context(nc.sbuf_tensor("h1TB_h", [128, D], BF16))
        h1TB_l = es.enter_context(nc.sbuf_tensor("h1TB_l", [128, D], BF16))
        h2t = es.enter_context(nc.sbuf_tensor("h2t", [128, D], F32))
        h1T_h = es.enter_context(nc.sbuf_tensor("h1T_h", [128, D], BF16))
        h1T_l = es.enter_context(nc.sbuf_tensor("h1T_l", [128, D], BF16))
        trash = es.enter_context(nc.sbuf_tensor("trash", [128, D], BF16))
        stat = es.enter_context(nc.sbuf_tensor("stat", [128, 16], F32))
        g8 = es.enter_context(nc.sbuf_tensor("g8", [128, 8], F32))
        g8b = es.enter_context(nc.sbuf_tensor("g8b", [128, 8], BF16))
        P64 = es.enter_context(nc.sbuf_tensor("P64", [128, 64], F32))
        P64h = es.enter_context(nc.sbuf_tensor("P64h", [64, 128], BF16))
        P64l = es.enter_context(nc.sbuf_tensor("P64l", [64, 128], BF16))
        ev32 = es.enter_context(nc.sbuf_tensor("ev32", [128, 512], F32))
        rl = es.enter_context(nc.sbuf_tensor("rl", [128, 2048], F32))
        epsb = es.enter_context(nc.sbuf_tensor("epsb", [128, 1], F32))

        psA = es.enter_context(nc.psum_tensor("psA", [128, 2048], F32))
        psB = es.enter_context(nc.psum_tensor("psB", [128, 2048], F32))

        sy, te, sc, ve, gp = nc.sync, nc.tensor, nc.scalar, nc.vector, nc.gpsimd

        def dma(eng, out, in_, sem="xd", cut=True):
            v = cx.inc(sem, eng.dma_start(out=out, in_=in_), 16)
            if cut:
                eng.nop(nofuse=True)
            return v

        def barrier():
            for snm in ("xd", "sd"):
                cx.wait(sy, snm)
            cx.wait(gp, "gd")
            r = cx._rec("bar")
            for e in ENGS:
                getattr(nc, e).sem_inc(r[0], 1)
                r[1] += 1
            v = r[1]
            for e in ENGS:
                getattr(nc, e).wait_ge(r[0], v)

        def mm3(pap, lh, ll, rh, rlo, start, stop, tp=None):
            te.matmul(pap, lh, rh, start=start, stop=False, tile_position=tp)
            te.matmul(pap, lh, rlo, start=False, stop=False, tile_position=tp)
            return te.matmul(pap, ll, rh, start=False, stop=stop, tile_position=tp)

        def ln(src_ap, sb=0):
            s1, sq = stat[:, sb + 2:sb + 3], stat[:, sb + 3:sb + 4]
            mu, rstd = stat[:, sb:sb + 1], stat[:, sb + 1:sb + 2]
            t1, t2 = stat[:, sb + 4:sb + 5], stat[:, sb + 5:sb + 6]
            cx.wait(sc, "dv")
            cx.inc("ac", sc.activation(trash[:, 0:D], src_ap, AF.Copy, accum_out=s1))
            cx.wait(sc, "ac")
            cx.inc("ac", sc.activation(trash[:, 0:D], src_ap, AF.Square, accum_out=sq))
            cx.wait(ve, "ac")
            ve.tensor_scalar(mu, s1, 1.0 / D, None, ALU.mult)
            cx.inc("dv", ve.tensor_scalar(t1, sq, 1.0 / D, None, ALU.mult))
            cx.wait(ve, "dv")
            cx.inc("dv", ve.tensor_tensor(t2, mu, mu, ALU.mult))
            cx.wait(ve, "dv")
            cx.inc("dv", ve.tensor_sub(t1, t1, t2))
            cx.wait(sc, "dv")
            cx.inc("ac", sc.activation(t2, t1, AF.Sqrt, bias=epsb[:, 0:1]))
            cx.wait(ve, "ac")
            cx.inc("dv", ve.reciprocal(rstd, t2))
            cx.wait(ve, "dv")

        def ln_apply(dst_ap, src_ap, sb=0):
            return ve.tensor_scalar(dst_ap, src_ap, stat[:, sb:sb + 1],
                                    stat[:, sb + 1:sb + 2], ALU.subtract, ALU.mult)

        pts_dv = {0: 0, 1: 0}

        def transp_split(src, scol_fn, nch, dsth, dstl, dcol_fn, alt=False):
            entry_dv = cx.val("dv")
            for gi, b0 in enumerate(range(0, nch, 4)):
                nb = min(4, nch - b0)
                if alt and gi % 2 == 1:
                    pts, reg = psB[:, 512:1024], 1
                else:
                    pts, reg = psB[:, 1536:2048], 0
                if alt:
                    cx.wait(te, "dv", max(entry_dv, pts_dv[reg]))
                else:
                    cx.wait(te, "dv", cx.val("dv"))
                for i in range(nb):
                    c = scol_fn(b0 + i)
                    cx.inc("pe", te.matmul(pts[:, 128 * i:128 * (i + 1)], src[:, c:c + 128],
                                           ident[:], is_transpose=True))
                cx.wait(sc, "pe")
                for i in range(nb):
                    d = dcol_fn(b0 + i)
                    cx.inc("ac", sc.activation(dsth[:, d:d + 128],
                                               pts[:, 128 * i:128 * (i + 1)], AF.Copy))
                cx.wait(ve, "ac")
                for i in range(nb):
                    d = dcol_fn(b0 + i)
                    cx.inc("dv", ve.tensor_sub(dstl[:, d:d + 128],
                                               pts[:, 128 * i:128 * (i + 1)],
                                               dsth[:, d:d + 128]))
                pts_dv[reg] = cx.val("dv")


        def load_pair(dsth, dstl, nm, r0, r1, c0, c1, dst_off=0, qe=None, semn="xd"):
            qe = sy if qe is None else qe
            cw = c1 - c0
            for i, rr in enumerate(range(r0, r1, 128)):
                dma(qe, dsth[:, dst_off + i * cw:dst_off + (i + 1) * cw],
                    wsp[nm + "_h"][rr:rr + 128, c0:c1], sem=semn)
                dma(qe, dstl[:, dst_off + i * cw:dst_off + (i + 1) * cw],
                    wsp[nm + "_l"][rr:rr + 128, c0:c1], sem=semn)

        cx.inc("gp", gp.memset(ones_f[:], 1.0))
        cx.inc("gp", gp.memset(epsb[:], EPS))
        cx.inc("gp", gp.memset(ones_b[:], 1.0))
        dma(sy, ident[:], w_in["ident"][:, :])
        dma(sy, maskA[:], w_in["maskA"][:, :])
        dma(sy, maskB[:], w_in["maskB"][:, :])
        dma(sy, rbt[:], w_in["rb"][:, :])
        dma(sy, stg[0:1, 0:D], w_in["bb"][:, :])
        cx.wait(sc, "xd")
        cx.inc("ac", sc.activation(bbh, stg[0:1, 0:D], AF.Copy))
        cx.wait(ve, "ac")
        cx.inc("dv", ve.tensor_sub(bbl, stg[0:1, 0:D], bbh))
        dma(sy, stg[0:64, 1024:1024 + D], w_in["b2"][:, :])
        cx.wait(sc, "xd")
        cx.inc("ac", sc.activation(b2h[:], stg[0:64, 1024:1024 + D], AF.Copy))
        cx.wait(ve, "ac")
        cx.inc("dv", ve.tensor_sub(b2l[:], stg[0:64, 1024:1024 + D], b2h[:]))
        cx.wait(sy, "dv")

        for nm in ["wbT", "rmid", "qkT", "qv", "kkT", "kv", "vkT", "vv",
                   "pkT", "pv", "fkT", "fv"]:
            src = w_in[nm]
            R, C = src.shape
            for r0 in range(0, R, 128):
                for c0 in range(0, C, 2048):
                    cw = min(2048, C - c0)
                    cx.wait(sy, "gd", cx.val("gd"))
                    dma(sy, stg[:, 0:cw], src[r0:r0 + 128, c0:c0 + cw])
                    dma(gp, stg_h[:, 0:cw], src[r0:r0 + 128, c0:c0 + cw], sem="gd")
                    cx.wait(gp, "xd")
                    cx.wait(gp, "gd")
                    cx.inc("gp", gp.tensor_sub(stg_l[:, 0:cw], stg[:, 0:cw], stg_h[:, 0:cw]))
                    cx.wait(gp, "gp")
                    dma(gp, wsp[nm + "_h"][r0:r0 + 128, c0:c0 + cw], stg_h[:, 0:cw], sem="gd")
                    dma(gp, wsp[nm + "_l"][r0:r0 + 128, c0:c0 + cw], stg_l[:, 0:cw], sem="gd")
        barrier()
        if upto == "prologue":
            return nc, cx, locals()

        AR = [(wa_h, wa_l), (wb_h, wb_l)]
        load_pair(rmsb_h, rmsb_l, "rmid", 0, D, 0, 16)
        afree = {0: cx.val("pe"), 1: cx.val("pe")}
        wv_of = {}

        VNAMES = ("kv", "vv", "qv", "pv", "fv")

        def prefetch(nm, idx, c0=0, c1=None, r0=0, r1=None):
            sh = wsp[nm + "_h"]
            if c1 is None:
                c1 = sh.shape[1]
            if r1 is None:
                r1 = sh.shape[0]
            if nm in VNAMES:
                qe, semn = sc, "vd"
            else:
                qe, semn = sy, "xd"
            cx.wait(qe, "pe", afree[idx])
            load_pair(AR[idx][0], AR[idx][1], nm, r0, r1, c0, c1, qe=qe, semn=semn)
            wv_of[(nm, idx)] = cx.val(semn)
            if semn == "xd":
                cx.wait(sy, "xd", wv_of[(nm, idx)])

        for G2 in range(2 * n_groups):
            if max_g2 is not None and G2 >= max_g2:
                break
            G, half = G2 // 2, G2 % 2
            spill = (half == 0)
            xsrc = x_in if half == 0 else xp_in
            t0 = G * TPG
            cx.wait(sy, "dv", cx.val("dv"))
            cx.wait(sy, "ac", cx.val("ac"))
            for i in range(TPG):
                dma(sy, xt[i][:], xsrc[128 * (t0 + i):128 * (t0 + i + 1), :])
            xv = cx.val("xd")
            cx.wait(sy, "xd", xv)
            fa = G2 % 2
            fWh, fWl = AR[fa]
            if G2 == 0:
                prefetch("wbT", 0)
            xd0 = wv_of[("wbT", fa)]
            prefetch("kkT", 1 - fa)

            dv_h3 = {}
            lnbuf = [(ht, h1T_h, h1T_l), (ht2, h1TB_h, h1TB_l)]
            ntile = TPG if max_tiles is None else min(TPG, max_tiles)
            cx.wait(sc, "xd", max(xd0, xv))
            ln(xt[0][:], sb=8)
            cx.wait(ve, "dv")
            cx.inc("dv", ln_apply(lnbuf[0][0][:], xt[0][:], sb=8))
            cx.wait(te, "dv")
            transp_split(lnbuf[0][0], lambda c: 128 * c, NCH,
                         lnbuf[0][1], lnbuf[0][2], lambda c: 128 * c, alt=True)
            dv_h1 = cx.val("dv")
            dv_stt = cx.val("dv")
            for i in range(ntile):
                t = t0 + i
                xa = xt[i][:]
                hb, hTh, hTl = lnbuf[i % 2]
                pbase = psA[:, 0:1024]
                p16 = psA[:, 1024:1040]
                cx.wait(te, "dv", dv_h1)
                cx.wait(te, "xd", max(xd0, xv))
                for cc in range(NCH):
                    lh = hTh[:, 128 * cc:128 * (cc + 1)]
                    llo = hTl[:, 128 * cc:128 * (cc + 1)]
                    mm3(p16[:, 0:16], lh, llo, rmsb_h[:, 16 * cc:16 * (cc + 1)],
                        rmsb_l[:, 16 * cc:16 * (cc + 1)], start=(cc == 0), stop=False)
                cx.inc("pe", te.matmul(p16[:, 0:8], ones_f[0:1, :], rbt[:], start=False, stop=True))
                pe_p16 = cx.val("pe")
                cx.wait(te, "dv", dv_stt)
                first = True
                for cc in range(NCH):
                    lh = hTh[:, 128 * cc:128 * (cc + 1)]
                    llo = hTl[:, 128 * cc:128 * (cc + 1)]
                    for hf in range(2):
                        mm3(pbase[:, 512 * hf:512 * (hf + 1)], lh, llo,
                            fWh[:, 1024 * cc + 512 * hf:1024 * cc + 512 * (hf + 1)],
                            fWl[:, 1024 * cc + 512 * hf:1024 * cc + 512 * (hf + 1)],
                            start=first, stop=False)
                    first = False
                te.matmul(pbase[:, 0:512], ones_b[0:1, :], bbp[0:1, 0:512], start=False, stop=False)
                te.matmul(pbase[:, 512:1024], ones_b[0:1, :], bbp[0:1, 512:1024], start=False, stop=False)
                te.matmul(pbase[:, 0:512], ones_b[0:1, :], bbp[0:1, 1024:1536], start=False, stop=False)
                cx.inc("pe", te.matmul(pbase[:, 512:1024], ones_b[0:1, :], bbp[0:1, 1536:2048], start=False, stop=False))
                cx.wait(sc, "pe", pe_p16)
                cx.inc("ac", sc.activation(g8[:], p16[:, 0:8], AF.Exp, accum_out=stat[:, 6:7]))
                cx.wait(ve, "ac")
                cx.inc("dv", ve.reciprocal(stat[:, 7:8], stat[:, 6:7]))
                cx.wait(ve, "dv")
                cx.inc("dv", ve.tensor_scalar(g8b[:], g8[:], stat[:, 7:8], None, ALU.mult))
                cx.wait(ve, "dv")
                cx.inc("dv", ve.tensor_copy(g8[:], g8b[:]))
                cx.wait(ve, "dv")
                for e in range(8):
                    cx.inc("dv", ve.tensor_scalar(P64[:, 8 * e:8 * (e + 1)], p16[:, 8:16],
                                     g8[:, e:e + 1], 2.0, ALU.mult, ALU.mult))
                if i + 1 < ntile:
                    nb, nTh, nTl = lnbuf[(i + 1) % 2]
                    cx.wait(sc, "xd", max(xd0, xv))
                    ln(xt[i + 1][:], sb=8)
                    cx.wait(ve, "dv")
                    cx.inc("dv", ln_apply(nb[:], xt[i + 1][:], sb=8))
                    cx.wait(te, "dv")
                    transp_split(nb, lambda c: 128 * c, NCH, nTh, nTl,
                                 lambda c: 128 * c, alt=True)
                    dv_h1 = cx.val("dv")
                pt64 = psB[:, 1024:1152]
                cx.wait(te, "dv")
                cx.inc("pe", te.matmul(pt64[0:64, 0:128], P64[:], ident[:], is_transpose=True))
                cx.wait(sc, "pe")
                cx.inc("ac", sc.activation(P64h[:], pt64[0:64, 0:128], AF.Copy))
                cx.wait(ve, "ac")
                cx.inc("dv", ve.tensor_sub(P64l[:], pt64[0:64, 0:128], P64h[:]))
                cx.wait(te, "dv")
                for hf in range(2):
                    cx.inc("pe", mm3(pbase[:, 512 * hf:512 * (hf + 1)], P64h[:], P64l[:],
                        b2h[:, 512 * hf:512 * (hf + 1)], b2l[:, 512 * hf:512 * (hf + 1)],
                        start=False, stop=True))
                cx.wait(ve, "pe")
                cx.inc("dv", ve.scalar_tensor_tensor(xa, pbase[:, 0:1024], 0.5, xa,
                                                     ALU.mult, ALU.add))
                dv_stt = cx.val("dv")
                if spill:
                    cx.wait(sy, "dv")
                    dma(sy, x1_sp[128 * t:128 * (t + 1), :], xa, sem="sd")
                ln(xa)
                cx.wait(ve, "dv")
                cx.wait(ve, "sd", cx.val("sd"))
                cx.inc("dv", ln_apply(h2t[:], xa))
                if spill:
                    cx.wait(sy, "dv")
                    dma(sy, h2_sp[128 * t:128 * (t + 1), :], h2t[:], sem="sd")
                ln(h2t[:])
                cx.wait(ve, "dv")
                cx.wait(ve, "gp", cx.val("gp"))
                cx.inc("dv", ln_apply(o_facc[:, 1024 * i:1024 * (i + 1)], h2t[:]))
                dv_h3[i] = cx.val("dv")
            for i in sorted(dv_h3):
                cx.wait(te, "dv", dv_h3[i])
                transp_split(o_facc, lambda c, _i=i: 1024 * _i + 128 * c, NCH,
                             h3T_h, h3T_l, lambda c, _i=i: 512 * c + 128 * _i, alt=True)
            afree[fa] = cx.val("pe")
            if upto == "ffn1":
                continue

            def w_pass(kind, nmK, nmV, ka, va, pre_s=(), pre_v=(), G=G, t0=t0, half=half):
                scale = 1.0 / 32.0
                kWh, kWl = AR[ka]
                vWh, vWl = AR[va]
                wv = wv_of[(nmK, ka)]
                for nm, idx in pre_s:
                    prefetch(nm, idx)
                lrow = psB[0:1, 0:512]
                cx.wait(te, "dv", cx.val("dv"))
                cx.wait(te, "ac", cx.val("ac"))
                ac_after = {}
                gp_after = {-1: cx.val("gp")}
                for ncx in range(NCH):
                    sp = psA[:, 512 * (ncx % 2):512 * (ncx % 2 + 1)]
                    if ncx >= 2:
                        cx.wait(te, "ac", ac_after[ncx - 2])
                    cx.wait(te, "xd", wv)
                    for cc in range(NCH):
                        last = mm3(sp,
                            kWh[:, 1024 * cc + 128 * ncx:1024 * cc + 128 * (ncx + 1)],
                            kWl[:, 1024 * cc + 128 * ncx:1024 * cc + 128 * (ncx + 1)],
                            h3T_h[:, 512 * cc:512 * (cc + 1)],
                            h3T_l[:, 512 * cc:512 * (cc + 1)],
                            start=(cc == 0), stop=(cc == NCH - 1))
                    cx.inc("pe", last)
                    cx.wait(sc, "pe")
                    if ncx >= 2:
                        sfe, sfv = gp_after[ncx - 2]
                        cx.wait(sc, sfe, sfv)
                    pf = Pf[ncx % 2]
                    cx.inc("ac", sc.activation(pf[:], sp, AF.Exp, scale=scale))
                    ac_after[ncx] = cx.val("ac")
                    cx.wait(ve, "ac")
                    cx.inc("dv", ve.tensor_copy(PT_h[:, 512 * ncx:512 * (ncx + 1)], pf[:]))
                    if ncx % 2 == 0:
                        cx.wait(gp, "dv")
                        cx.inc("gp", gp.tensor_sub(PT_l[:, 512 * ncx:512 * (ncx + 1)], pf[:],
                                                   PT_h[:, 512 * ncx:512 * (ncx + 1)]))
                        gp_after[ncx] = ("gp", cx.val("gp"))
                    else:
                        cx.wait(ve, "dv")
                        cx.inc("dv", ve.tensor_sub(PT_l[:, 512 * ncx:512 * (ncx + 1)], pf[:],
                                                   PT_h[:, 512 * ncx:512 * (ncx + 1)]))
                        gp_after[ncx] = ("dv", cx.val("dv"))
                afree[ka] = cx.val("pe")
                for nm, idx in pre_v:
                    prefetch(nm, idx)
                wvv = wv_of[(nmV, va)]
                cx.wait(te, "gp")
                cx.wait(te, "dv")
                for ncx in range(NCH):
                    te.matmul(lrow, ones_b[:, 0:1], PT_h[:, 512 * ncx:512 * (ncx + 1)],
                              start=(ncx == 0), stop=False)
                    last = te.matmul(lrow, ones_b[:, 0:1], PT_l[:, 512 * ncx:512 * (ncx + 1)],
                                     start=False, stop=(ncx == NCH - 1))
                cx.inc("pe", last)
                cx.wait(ve, "pe")
                cx.inc("dv", ve.reciprocal(rl[0:1, 0:512], lrow))
                if kind in ("q", "k"):
                    rbc = psB[:, 512:1024]
                    cx.wait(te, "dv")
                    cx.wait(te, "gp", cx.val("gp"))
                    cx.inc("pe", te.matmul(rbc, ones_f[0:1, :], rl[0:1, 0:512],
                                           start=True, stop=True))
                    cx.wait(ve, "pe")
                    cx.inc("dv", ve.tensor_copy(Pf[0][:], rbc))
                    rbc_sb = Pf[0][:]
                    cx.wait(te, "vd", wvv)
                    dv_evac = {}
                    for cc in range(NCH):
                        po = psA[:, 1024 + 512 * (cc % 2):1024 + 512 * (cc % 2 + 1)]
                        if cc >= 2:
                            cx.wait(te, "dv", dv_evac[cc - 2])
                        for ncx in range(NCH):
                            last = mm3(po,
                                vWh[:, 1024 * ncx + 128 * cc:1024 * ncx + 128 * (cc + 1)],
                                vWl[:, 1024 * ncx + 128 * cc:1024 * ncx + 128 * (cc + 1)],
                                PT_h[:, 512 * ncx:512 * (ncx + 1)],
                                PT_l[:, 512 * ncx:512 * (ncx + 1)],
                                start=(ncx == 0), stop=(ncx == NCH - 1))
                        cx.inc("pe", last)
                        cx.wait(ve, "pe")
                        cx.wait(ve, "sd", cx.val("sd"))
                        cx.wait(ve, "gd", cx.val("gd"))
                        cx.wait(ve, "dv")
                        cx.inc("dv", ve.tensor_tensor(ev32[:, 0:512], po, rbc_sb, ALU.mult))
                        dv_evac[cc] = cx.val("dv")
                        if kind == "q":
                            cx.wait(sy, "dv")
                            qdst = bass.AP(q_sp.tensor, (128 * t0) * D + 128 * cc,
                                           [[D, 128], [128 * D, TPG], [1, 128]])
                            qsrc = bass.AP(ev32, 0, [[512, 128], [128, TPG], [1, 128]])
                            dma(sy, qdst, qsrc, sem="sd")
                        else:
                            cx.wait(sc, "dv")
                            cx.wait(sc, "gd", cx.val("gd"))
                            cx.inc("ac", sc.activation(stg_h[:, 0:512], ev32[:, 0:512], AF.Copy))
                            cx.wait(gp, "ac")
                            cx.inc("gp", gp.tensor_sub(stg_l[:, 0:512], ev32[:, 0:512],
                                                       stg_h[:, 0:512]))
                            cx.wait(gp, "gp")
                            kof = (512 * half) * CCW + 128 * cc
                            kdims = [[CCW, 128], [128 * CCW, TPG], [1, 128]]
                            ksrc = [[2048, 128], [128, TPG], [1, 128]]
                            dma(gp, bass.AP(cc_out[G].tensor, kof, kdims),
                                bass.AP(stg_h, 0, ksrc), sem="gd")
                            dma(gp, bass.AP(cc_out[G].tensor, kof + 1024, kdims),
                                bass.AP(stg_l, 0, ksrc), sem="gd")
                else:
                    cx.wait(gp, "pe", cx.val("pe"))
                    cx.inc("gp", gp.memset(o_facc[0:128, 0:1040], 1.0))
                    prc = psB[:, 1024:1152]
                    cx.wait(te, "dv")
                    cx.wait(te, "ac", cx.val("ac"))
                    for tau in range(TPG):
                        cx.inc("pe", te.matmul(prc[0:128, tau:tau + 1],
                                               rl[0:1, 128 * tau:128 * (tau + 1)],
                                               ident[0:1, 0:1], is_transpose=True))
                    cx.wait(ve, "pe")
                    for tau in range(TPG):
                        cx.inc("dv", ve.tensor_copy(rl[:, 1024 + tau:1024 + tau + 1],
                                                    prc[0:128, tau:tau + 1]))
                    cx.wait(te, "vd", wvv)
                    dv_vev = {}
                    for tau in range(TPG):
                        po = psA[:, 1024:2048] if tau % 2 else psA[:, 0:1024]
                        if tau < 2:
                            cx.wait(te, "ac", cx.val("ac"))
                        else:
                            cx.wait(te, "dv", dv_vev[tau - 2])
                        for ncx in range(NCH):
                            for hf in range(2):
                                last = mm3(po[:, 512 * hf:512 * (hf + 1)],
                                    PT_h[:, 512 * ncx + 128 * tau:512 * ncx + 128 * (tau + 1)],
                                    PT_l[:, 512 * ncx + 128 * tau:512 * ncx + 128 * (tau + 1)],
                                    vWh[:, 1024 * ncx + 512 * hf:1024 * ncx + 512 * (hf + 1)],
                                    vWl[:, 1024 * ncx + 512 * hf:1024 * ncx + 512 * (hf + 1)],
                                    start=(ncx == 0), stop=(ncx == NCH - 1))
                                if ncx == NCH - 1:
                                    cx.inc("pe", last)
                        cx.wait(ve, "pe")
                        cx.wait(ve, "dv")
                        cx.wait(ve, "gp", cx.val("gp"))
                        vstr = bass.AP(o_facc, 0, [[4096, 128], [65, 16], [1, 64]])
                        cx.inc("dv", ve.tensor_scalar(vstr, po[:, 0:1024],
                                                      rl[:, 1024 + tau:1024 + tau + 1],
                                                      None, ALU.mult))
                        dv_vev[tau] = cx.val("dv")
                        cx.wait(sc, "dv")
                        cx.wait(sc, "gd", cx.val("gd"))
                        cx.inc("ac", sc.activation(stg_h[:, 0:1040], o_facc[:, 0:1040], AF.Copy))
                        cx.wait(gp, "ac")
                        cx.inc("gp", gp.tensor_sub(stg_l[:, 0:1040], o_facc[:, 0:1040],
                                                   stg_h[:, 0:1040]))
                        cx.wait(gp, "gp")
                        rb0 = 512 * half + 128 * tau
                        dma(gp, cc_out[G][rb0:rb0 + 128, 2048:3088],
                            stg_h[:, 0:1040], sem="gd")
                        dma(gp, cc_out[G][rb0:rb0 + 128, 3088:4128],
                            stg_l[:, 0:1040], sem="gd")

                afree[va] = cx.val("pe")

            nxt = ("qkT" if half == 0 else "wbT")
            w_pass("k", "kkT", "kv", 1 - fa, fa,
                   pre_s=[("kv", fa)], pre_v=[("vkT", 1 - fa)])
            w_pass("v", "vkT", "vv", 1 - fa, fa,
                   pre_s=[("vv", fa)], pre_v=[(nxt, 1 - fa)])
            if half == 0:
                w_pass("q", "qkT", "qv", 1 - fa, fa,
                       pre_s=[("qv", fa)], pre_v=[("wbT", 1 - fa)])

        barrier()
        if upto in ("ffn1", "phaseA"):
            return nc, cx, locals()

        def pcol(h, par):
            return 2048 * par + 128 * (h // 2) + 1024 * (h % 2)

        dv_hi = {}
        gp_lo = {}
        ac_f32 = {}
        slot_pe = {0: 0, 1: 0, 2: 0}
        ev_state = {}

        def emit_av(i, np_):
            par = i % 2
            bi = i % 3
            vb = 3072 + 1040 * bi
            cx.wait(te, "dv", dv_hi[i])
            for h in range(0, H, 2):
                pc = pcol(h, par)
                ob = psB[0:65, 128 * h:128 * (h + 1)]
                te.matmul(ob, wb_h[:, vb + 65 * h:vb + 65 * (h + 1)],
                          PT_h[:, pc:pc + 128],
                          start=(i == 0 and h % 4 == 0), stop=False)
                te.matmul(ob, wb_l[:, vb + 65 * h:vb + 65 * (h + 1)],
                          PT_h[:, pc:pc + 128], start=False, stop=False)
                te.matmul(ob, wb_h[:, vb + 65 * h:vb + 65 * (h + 1)],
                          PT_l[:, pc:pc + 128], start=False, stop=False)
            cx.inc("pe", te.nop())
            cx.wait(te, "gp", gp_lo[i])
            for h in range(1, H, 2):
                pc = pcol(h, par)
                ob = psB[0:65, 128 * h:128 * (h + 1)]
                te.matmul(ob, wb_h[:, vb + 65 * h:vb + 65 * (h + 1)],
                          PT_h[:, pc:pc + 128], start=False, stop=False)
                te.matmul(ob, wb_l[:, vb + 65 * h:vb + 65 * (h + 1)],
                          PT_h[:, pc:pc + 128], start=False, stop=False)
                last = te.matmul(ob, wb_h[:, vb + 65 * h:vb + 65 * (h + 1)],
                                 PT_l[:, pc:pc + 128], start=False,
                                 stop=(i == np_ - 1 and h % 4 == 3))
            cx.inc("pe", last)
            slot_pe[bi] = cx.val("pe")

        def emit_evac(jp, np_):
            te.wait_ge(cx.sem("pe"), cx.val("pe"))
            emit_av(np_ - 1, np_)
            te.wait_ge(cx.sem("pe"), cx.val("pe"))
            cx.wait(ve, "pe")
            cx.inc("dv", ve.tensor_copy(rl[64:65, 0:2048], psB[64:65, 0:2048]))
            dv_rec = cx.val("dv")
            cx.wait(te, "ac", ac_f32[np_ - 1])
            cx.wait(te, "dv", dv_rec)
            for h in range(H):
                cx.inc("pe", te.matmul(psA[0:64, 128 * h:128 * (h + 1)],
                          ones_f[64:65, 0:64], rl[64:65, 128 * h:128 * (h + 1)],
                          start=True, stop=True, tile_position=(64, 0)))
            te.wait_ge(cx.sem("pe"), cx.val("pe"))
            cx.wait(ve, "pe")
            cx.wait(ve, "gd", ev_state.get("gd_odma", 0))
            cx.inc("dv", ve.tensor_copy(o_facc[0:64, 0:2048], psA[0:64, 0:2048]))
            dv_copy = cx.val("dv")
            cx.wait(ve, "dv")
            cx.inc("dv", ve.reciprocal(o_facc[0:64, 0:2048], o_facc[0:64, 0:2048]))
            cx.wait(ve, "dv")
            for h in range(H):
                ob = ht if h < 8 else h2t
                ve.tensor_tensor(ob[0:64, 128 * (h % 8):128 * (h % 8 + 1)],
                                 psB[0:64, 128 * h:128 * (h + 1)],
                                 o_facc[0:64, 128 * h:128 * (h + 1)], ALU.mult)
            cx.inc("dv", ve.engine_nop())
            dv_mult = cx.val("dv")
            cx.wait(gp, "dv", dv_mult)
            for c in range(NCH):
                ob = ht if c < 4 else h2t
                srcb = bass.AP(ob, 256 * (c % 4), [[1024, 64], [128, 2], [1, 128]])
                dst = bass.AP(o_sp.tensor, (128 * jp) * D + 128 * c,
                              [[D, 64], [64 * D, 2], [1, 128]])
                dma(gp, dst, srcb, sem="gd")
            ev_state["gd_odma"] = cx.val("gd")
            ev_state["dv_copy"] = dv_copy

        for j in range(NT if max_j is None else max_j):
            cx.wait(sy, "pe", cx.val("pe"))
            cx.wait(sy, "sd")
            dma(sy, stg[:, 0:1024], q_sp[128 * j:128 * (j + 1), 0:1024])
            cx.wait(gp, "gd", cx.val("gd"))
            dma(gp, wa_h[:, 0:1024], q_sp[128 * j:128 * (j + 1), 0:1024], sem="gd")
            cx.wait(gp, "xd")
            cx.wait(gp, "gd")
            cx.inc("gp", gp.tensor_sub(wa_l[:, 0:1024], stg[:, 0:1024], wa_h[:, 0:1024]))
            gp_q = cx.val("gp")
            cx.wait(sc, "sd", cx.val("sd"))
            npair = 2 * j + 2
            if sdpa_stage < 2:
                continue
            if sdpa_stage >= 5 and j >= 1:
                emit_evac(j - 1, npair - 2)

            for i in range(npair):
                par = i % 2
                bi = i % 3
                Gk, rb0 = ccpos(i)
                gate = slot_pe[bi]
                cx.wait(sy, "pe", gate)
                dma(sy, wb_h[:, 1024 * bi:1024 * (bi + 1)],
                    cc_out[Gk][rb0:rb0 + 128, 0:1024])
                dma(sy, wb_h[:, 3072 + 1040 * bi:3072 + 1040 * (bi + 1)],
                    cc_out[Gk][rb0:rb0 + 128, 2048:3088])
                cx.wait(sc, "pe", gate)
                dma(sc, wb_l[:, 1024 * bi:1024 * (bi + 1)],
                    cc_out[Gk][rb0:rb0 + 128, 1024:2048], sem="vd")
                dma(sc, wb_l[:, 3072 + 1040 * bi:3072 + 1040 * (bi + 1)],
                    cc_out[Gk][rb0:rb0 + 128, 3088:4128], sem="vd")
                kvld = cx.val("xd")
                kvld_v = cx.val("vd")
                cx.wait(te, "ac", ac_f32[i - 1] if i >= 1 else cx.val("ac"))
                if i == 0 and j >= 1 and sdpa_stage >= 5:
                    cx.wait(te, "dv", ev_state["dv_copy"])
                cx.wait(te, "xd", kvld)
                cx.wait(te, "vd", kvld_v)
                cx.wait(te, "gp", gp_q)
                for c in range(NCH):
                    for sub in range(2):
                        h = 2 * c + sub
                        tp = (64 * sub, 0)
                        kcol = 1024 * bi + 128 * c
                        qcol = 128 * c
                        pc = pcol(h, 0)
                        cx.inc("pe", mm3(psA[:, pc:pc + 128],
                            wb_h[64 * sub:64 * (sub + 1), kcol:kcol + 128],
                            wb_l[64 * sub:64 * (sub + 1), kcol:kcol + 128],
                            wa_h[64 * sub:64 * (sub + 1), qcol:qcol + 128],
                            wa_l[64 * sub:64 * (sub + 1), qcol:qcol + 128],
                            start=True, stop=True, tp=tp))
                pe_s = cx.val("pe")
                te.wait_ge(cx.sem("pe"), pe_s)
                if sdpa_stage >= 4 and i >= 1:
                    emit_av(i - 1, npair)
                if sdpa_stage < 3:
                    continue
                if i >= npair - 2:
                    if i == npair - 2:
                        mkap = bass.AP(maskA, 0, [[128, 128], [0, 16], [1, 128]])
                    else:
                        mkap = bass.AP(maskB, 0, [[1, 128], [0, 16], [0, 128]])
                    cx.wait(ve, "pe", pe_s)
                    cx.inc("dv", ve.tensor_tensor(psA[:, 0:2048], psA[:, 0:2048],
                                                  mkap, ALU.add))
                    cx.wait(sc, "dv")
                else:
                    cx.wait(sc, "pe", pe_s)
                po2 = 2048 * par
                pstg = stg[:, 0:2048] if par == 0 else o_facc[:, 2048:4096]
                if i >= 2:
                    cx.wait(sc, "gp", gp_lo[i - 2])
                elif i == 0:
                    cx.wait(sc, "gp", gp_q)
                cx.inc("ac", sc.activation(pstg, psA[:, 0:2048],
                                           AF.Exp, scale=0.125))
                ac_f32[i] = cx.val("ac")
                cx.wait(ve, "ac", ac_f32[i])
                cx.inc("dv", ve.tensor_copy(PT_h[:, po2:po2 + 2048], pstg))
                dv_cast = cx.val("dv")
                cx.wait(ve, "dv")
                cx.inc("dv", ve.tensor_sub(PT_l[:, po2:po2 + 1024], pstg[:, 0:1024],
                                           PT_h[:, po2:po2 + 1024]))
                dv_hi[i] = cx.val("dv")
                cx.wait(gp, "dv", dv_cast)
                cx.inc("gp", gp.tensor_sub(PT_l[:, po2 + 1024:po2 + 2048],
                                           pstg[:, 1024:2048],
                                           PT_h[:, po2 + 1024:po2 + 2048]))
                gp_lo[i] = cx.val("gp")
        if sdpa_stage >= 5:
            NTj = NT if max_j is None else max_j
            if NTj >= 1:
                emit_evac(NTj - 1, 2 * NTj)
        barrier()
        if upto == "sdpa":
            return nc, cx, locals()

        afree[0] = cx.val("pe")
        afree[1] = cx.val("pe")
        for G in range(n_groups):
            pa = G % 2
            ks, vs = 1 - pa, pa
            p2h, p2l = AR[ks]
            v2h, v2l = AR[vs]
            f2h, f2l = AR[ks]
            t0 = G * TPG
            cx.wait(sy, "pe", cx.val("pe"))
            cx.wait(gp, "pe", cx.val("pe"))
            for tau in range(TPG):
                t = t0 + tau
                half = tau % 2
                cx.wait(sy, "dv", cx.val("dv"))
                dma(sy, stg[:, 1024 * half:1024 * (half + 1)],
                    o_sp[128 * t:128 * (t + 1), :])
                dsth = bass.AP(h3T_h, 128 * tau, [[4096, 128], [512, 8], [1, 128]])
                dstl = bass.AP(h3T_l, 128 * tau, [[4096, 128], [512, 8], [1, 128]])
                cx.wait(gp, "dv", cx.val("dv"))
                dma(gp, dsth, o_sp[128 * t:128 * (t + 1), :], sem="gd")
                cx.wait(ve, "xd")
                cx.wait(ve, "gd")
                srcf = bass.AP(stg, 1024 * half, [[2048, 128], [128, 8], [1, 128]])
                cx.inc("dv", ve.tensor_tensor(dstl, srcf, dsth, ALU.subtract))

            if G == 0:
                prefetch("pkT", ks)
            wv = wv_of[("pkT", ks)]
            prefetch("pv", vs)
            wv_pv = wv_of[("pv", vs)]
            lrow = psB[0:1, 0:512]
            cx.wait(te, "dv", cx.val("dv"))
            cx.wait(te, "ac", cx.val("ac"))
            ac_after = {}
            gp_after = {-1: cx.val("gp")}
            for ncx in range(NCH):
                sp = psA[:, 512 * (ncx % 2):512 * (ncx % 2 + 1)]
                if ncx >= 2:
                    cx.wait(te, "ac", ac_after[ncx - 2])
                else:
                    cx.wait(te, "ac", cx.val("ac"))
                cx.wait(te, "xd", wv)
                for cc in range(NCH):
                    last = mm3(sp,
                        p2h[:, 1024 * cc + 128 * ncx:1024 * cc + 128 * (ncx + 1)],
                        p2l[:, 1024 * cc + 128 * ncx:1024 * cc + 128 * (ncx + 1)],
                        h3T_h[:, 512 * cc:512 * (cc + 1)],
                        h3T_l[:, 512 * cc:512 * (cc + 1)],
                        start=(cc == 0), stop=(cc == NCH - 1))
                cx.inc("pe", last)
                cx.wait(sc, "pe")
                if ncx >= 2:
                    sfe, sfv = gp_after[ncx - 2]
                    cx.wait(sc, sfe, sfv)
                pf = Pf[ncx % 2]
                cx.inc("ac", sc.activation(pf[:], sp, AF.Exp, scale=1.0 / 32.0))
                ac_after[ncx] = cx.val("ac")
                cx.wait(ve, "ac")
                cx.inc("dv", ve.tensor_copy(PT_h[:, 512 * ncx:512 * (ncx + 1)], pf[:]))
                if ncx % 2 == 0:
                    cx.wait(gp, "dv")
                    cx.inc("gp", gp.tensor_sub(PT_l[:, 512 * ncx:512 * (ncx + 1)], pf[:],
                                               PT_h[:, 512 * ncx:512 * (ncx + 1)]))
                    gp_after[ncx] = ("gp", cx.val("gp"))
                else:
                    cx.wait(ve, "dv")
                    cx.inc("dv", ve.tensor_sub(PT_l[:, 512 * ncx:512 * (ncx + 1)], pf[:],
                                               PT_h[:, 512 * ncx:512 * (ncx + 1)]))
                    gp_after[ncx] = ("dv", cx.val("dv"))
            cx.wait(te, "gp")
            cx.wait(te, "dv")
            for ncx in range(NCH):
                te.matmul(lrow, ones_b[:, 0:1], PT_h[:, 512 * ncx:512 * (ncx + 1)],
                          start=(ncx == 0), stop=False)
                last = te.matmul(lrow, ones_b[:, 0:1], PT_l[:, 512 * ncx:512 * (ncx + 1)],
                                 start=False, stop=(ncx == NCH - 1))
            cx.inc("pe", last)
            cx.wait(ve, "pe")
            cx.inc("dv", ve.reciprocal(rl[0:1, 0:512], lrow))
            prc = psB[:, 1024:1152]
            cx.wait(te, "dv")
            cx.wait(te, "ac", cx.val("ac"))
            for tau in range(TPG):
                cx.inc("pe", te.matmul(prc[0:128, tau:tau + 1],
                                       rl[0:1, 128 * tau:128 * (tau + 1)],
                                       ident[0:1, 0:1], is_transpose=True))
            cx.wait(ve, "pe")
            for tau in range(TPG):
                cx.inc("dv", ve.tensor_copy(rl[:, 1024 + tau:1024 + tau + 1],
                                            prc[0:128, tau:tau + 1]))
            afree[ks] = cx.val("pe")
            prefetch("fkT", ks, c0=0, c1=1024)
            cx.wait(te, "vd", wv_pv)
            dv_pv = {}
            dv_h4 = {}
            for tau in range(TPG):
                t = t0 + tau
                po = psA[:, 1024:2048] if tau % 2 == 0 else psA[:, 0:1024]
                cx.wait(sy, "dv", cx.val("dv"))
                cx.wait(sy, "pe", cx.val("pe"))
                cx.wait(sy, "ac", cx.val("ac"))
                dma(sy, stg[:, 0:1024], h2_sp[128 * t:128 * (t + 1), :])
                dma(sy, xt[tau][:], x1_sp[128 * t:128 * (t + 1), :])
                h2v = cx.val("xd")
                if tau >= 2:
                    cx.wait(te, "dv", dv_pv[tau - 2])
                elif tau == 1:
                    cx.wait(te, "ac", cx.val("ac"))
                for ncx in range(NCH):
                    for hf in range(2):
                        last = mm3(po[:, 512 * hf:512 * (hf + 1)],
                            PT_h[:, 512 * ncx + 128 * tau:512 * ncx + 128 * (tau + 1)],
                            PT_l[:, 512 * ncx + 128 * tau:512 * ncx + 128 * (tau + 1)],
                            v2h[:, 1024 * ncx + 512 * hf:1024 * ncx + 512 * (hf + 1)],
                            v2l[:, 1024 * ncx + 512 * hf:1024 * ncx + 512 * (hf + 1)],
                            start=(ncx == 0), stop=(ncx == NCH - 1))
                        if ncx == NCH - 1:
                            cx.inc("pe", last)
                cx.wait(ve, "pe")
                cx.wait(ve, "xd", h2v)
                cx.wait(ve, "dv")
                cx.wait(ve, "sd", cx.val("sd"))
                cx.inc("dv", ve.scalar_tensor_tensor(ht[:], po[:, 0:1024],
                                                     rl[:, 1024 + tau:1024 + tau + 1],
                                                     stg[:, 0:1024], ALU.mult, ALU.add))
                dv_pv[tau] = cx.val("dv")
                cx.wait(ve, "dv")
                cx.inc("dv", ve.tensor_add(xt[tau][:], xt[tau][:], ht[:]))
                ln(ht[:])
                h4b = h2t if tau % 2 == 0 else ht2
                cx.wait(ve, "dv")
                cx.inc("dv", ln_apply(h4b[:], ht[:]))
                dv_h4[tau] = cx.val("dv")
                if tau >= 1:
                    pb = h2t if (tau - 1) % 2 == 0 else ht2
                    cx.wait(te, "dv", dv_h4[tau - 1])
                    transp_split(pb, lambda c: 128 * c, NCH, h4T_h, h4T_l,
                                 lambda c, _t=tau - 1: 512 * c + 128 * _t, alt=True)

            pb = h2t if (TPG - 1) % 2 == 0 else ht2
            cx.wait(te, "dv", dv_h4[TPG - 1])
            transp_split(pb, lambda c: 128 * c, NCH, h4T_h, h4T_l,
                         lambda c: 512 * c + 128 * (TPG - 1), alt=True)
            cx.wait(gp, "dv", cx.val("dv"))
            cx.inc("gp", gp.memset(o_facc[:], 0.0))
            lrow = psB[0:1, 0:512]
            afree[vs] = cx.val("pe")
            for Q in range(4):
                wv = wv_of[("fkT", ks)]
                prefetch("fv", vs, r0=1024 * Q, r1=1024 * (Q + 1))
                wv_fv = wv_of[("fv", vs)]
                cx.wait(te, "dv", cx.val("dv"))
                ac_after = {}
                gp_after = {-1: cx.val("gp")}
                for ncx in range(NCH):
                    sp = psA[:, 512 * (ncx % 2):512 * (ncx % 2 + 1)]
                    if ncx >= 2:
                        cx.wait(te, "ac", ac_after[ncx - 2])
                    else:
                        cx.wait(te, "ac", cx.val("ac"))
                    cx.wait(te, "xd", wv)
                    for cc in range(NCH):
                        last = mm3(sp,
                            p2h[:, 1024 * cc + 128 * ncx:1024 * cc + 128 * (ncx + 1)],
                            p2l[:, 1024 * cc + 128 * ncx:1024 * cc + 128 * (ncx + 1)],
                            h4T_h[:, 512 * cc:512 * (cc + 1)],
                            h4T_l[:, 512 * cc:512 * (cc + 1)],
                            start=(cc == 0), stop=(cc == NCH - 1))
                    cx.inc("pe", last)
                    cx.wait(sc, "pe")
                    if ncx >= 2:
                        sfe, sfv = gp_after[ncx - 2]
                        cx.wait(sc, sfe, sfv)
                    pf = Pf[ncx % 2]
                    cx.inc("ac", sc.activation(pf[:], sp, AF.Exp, scale=1.0 / 32.0))
                    ac_after[ncx] = cx.val("ac")
                    cx.wait(ve, "ac")
                    cx.inc("dv", ve.tensor_copy(PT_h[:, 512 * ncx:512 * (ncx + 1)], pf[:]))
                    if ncx % 2 == 0:
                        cx.wait(gp, "dv")
                        cx.inc("gp", gp.tensor_sub(PT_l[:, 512 * ncx:512 * (ncx + 1)], pf[:],
                                                   PT_h[:, 512 * ncx:512 * (ncx + 1)]))
                        gp_after[ncx] = ("gp", cx.val("gp"))
                    else:
                        cx.wait(ve, "dv")
                        cx.inc("dv", ve.tensor_sub(PT_l[:, 512 * ncx:512 * (ncx + 1)], pf[:],
                                                   PT_h[:, 512 * ncx:512 * (ncx + 1)]))
                        gp_after[ncx] = ("dv", cx.val("dv"))
                cx.wait(te, "gp")
                cx.wait(te, "dv")
                for ncx in range(NCH):
                    te.matmul(lrow, ones_b[:, 0:1], PT_h[:, 512 * ncx:512 * (ncx + 1)],
                              start=(Q == 0 and ncx == 0), stop=False)
                    last = te.matmul(lrow, ones_b[:, 0:1], PT_l[:, 512 * ncx:512 * (ncx + 1)],
                                     start=False, stop=(Q == 3 and ncx == NCH - 1))
                cx.inc("pe", last)
                afree[ks] = cx.val("pe")
                if Q < 3:
                    prefetch("fkT", ks, c0=1024 * (Q + 1), c1=1024 * (Q + 2))
                else:
                    prefetch("wbT", ks)
                cx.wait(te, "vd", wv_fv)
                dv_fv = {}
                for tau in range(TPG):
                    po = psA[:, 1024:2048] if tau % 2 == 0 else psA[:, 0:1024]
                    if tau >= 2:
                        cx.wait(te, "dv", dv_fv[tau - 2])
                    elif tau == 1:
                        cx.wait(te, "ac", cx.val("ac"))
                    else:
                        cx.wait(te, "dv", cx.val("dv"))
                    for ncx in range(NCH):
                        for hf in range(2):
                            last = mm3(po[:, 512 * hf:512 * (hf + 1)],
                                PT_h[:, 512 * ncx + 128 * tau:512 * ncx + 128 * (tau + 1)],
                                PT_l[:, 512 * ncx + 128 * tau:512 * ncx + 128 * (tau + 1)],
                                v2h[:, 1024 * ncx + 512 * hf:1024 * ncx + 512 * (hf + 1)],
                                v2l[:, 1024 * ncx + 512 * hf:1024 * ncx + 512 * (hf + 1)],
                                start=(ncx == 0), stop=(ncx == NCH - 1))
                            if ncx == NCH - 1:
                                cx.inc("pe", last)
                    cx.wait(ve, "pe")
                    cx.wait(ve, "gp", cx.val("gp"))
                    cx.inc("dv", ve.tensor_add(o_facc[:, 1024 * tau:1024 * (tau + 1)],
                                               o_facc[:, 1024 * tau:1024 * (tau + 1)],
                                               po[:, 0:1024]))
                    dv_fv[tau] = cx.val("dv")
                    cx.wait(ve, "dv")
                afree[vs] = cx.val("pe")
            cx.wait(ve, "pe")
            cx.inc("dv", ve.reciprocal(rl[0:1, 0:512], lrow))
            prc = psB[:, 1024:1152]
            cx.wait(te, "dv")
            for tau in range(TPG):
                cx.inc("pe", te.matmul(prc[0:128, tau:tau + 1],
                                       rl[0:1, 128 * tau:128 * (tau + 1)],
                                       ident[0:1, 0:1], is_transpose=True))
            cx.wait(ve, "pe")
            for tau in range(TPG):
                cx.inc("dv", ve.tensor_copy(rl[:, 1024 + tau:1024 + tau + 1],
                                            prc[0:128, tau:tau + 1]))
            cx.wait(ve, "dv")
            for tau in range(TPG):
                cx.inc("dv", ve.scalar_tensor_tensor(
                    xt[tau][:], o_facc[:, 1024 * tau:1024 * (tau + 1)],
                    rl[:, 1024 + tau:1024 + tau + 1], xt[tau][:], ALU.mult, ALU.add))

            xd0 = wv_of[("wbT", ks)]
            if G + 1 < n_groups:
                prefetch("pkT", vs)
            lnbuf = [(ht, h1T_h, h1T_l), (ht2, h1TB_h, h1TB_l)]
            cx.wait(sc, "xd", xd0)
            cx.wait(sc, "dv", cx.val("dv"))
            ln(xt[0][:], sb=8)
            cx.wait(ve, "dv")
            cx.wait(ve, "sd", cx.val("sd"))
            cx.inc("dv", ln_apply(lnbuf[0][0][:], xt[0][:], sb=8))
            cx.wait(te, "dv")
            transp_split(lnbuf[0][0], lambda c: 128 * c, NCH,
                         lnbuf[0][1], lnbuf[0][2], lambda c: 128 * c, alt=True)
            dv_h1 = cx.val("dv")
            dv_stt = cx.val("dv")
            for tau in range(TPG):
                t = t0 + tau
                xa = xt[tau][:]
                hb, hTh, hTl = lnbuf[tau % 2]
                pbase = psA[:, 0:1024]
                p16 = psA[:, 1024:1040]
                cx.wait(te, "dv", dv_h1)
                cx.wait(te, "xd", xd0)
                for cc in range(NCH):
                    lh = hTh[:, 128 * cc:128 * (cc + 1)]
                    llo = hTl[:, 128 * cc:128 * (cc + 1)]
                    mm3(p16[:, 0:16], lh, llo, rmsb_h[:, 16 * cc:16 * (cc + 1)],
                        rmsb_l[:, 16 * cc:16 * (cc + 1)], start=(cc == 0), stop=False)
                cx.inc("pe", te.matmul(p16[:, 0:8], ones_f[0:1, :], rbt[:], start=False, stop=True))
                pe_p16 = cx.val("pe")
                cx.wait(te, "dv", dv_stt)
                first = True
                for cc in range(NCH):
                    lh = hTh[:, 128 * cc:128 * (cc + 1)]
                    llo = hTl[:, 128 * cc:128 * (cc + 1)]
                    for hf in range(2):
                        mm3(pbase[:, 512 * hf:512 * (hf + 1)], lh, llo,
                            f2h[:, 1024 * cc + 512 * hf:1024 * cc + 512 * (hf + 1)],
                            f2l[:, 1024 * cc + 512 * hf:1024 * cc + 512 * (hf + 1)],
                            start=first, stop=False)
                    first = False
                te.matmul(pbase[:, 0:512], ones_b[0:1, :], bbp[0:1, 0:512], start=False, stop=False)
                te.matmul(pbase[:, 512:1024], ones_b[0:1, :], bbp[0:1, 512:1024], start=False, stop=False)
                te.matmul(pbase[:, 0:512], ones_b[0:1, :], bbp[0:1, 1024:1536], start=False, stop=False)
                cx.inc("pe", te.matmul(pbase[:, 512:1024], ones_b[0:1, :], bbp[0:1, 1536:2048], start=False, stop=False))
                cx.wait(sc, "pe", pe_p16)
                cx.inc("ac", sc.activation(g8[:], p16[:, 0:8], AF.Exp, accum_out=stat[:, 6:7]))
                cx.wait(ve, "ac")
                cx.inc("dv", ve.reciprocal(stat[:, 7:8], stat[:, 6:7]))
                cx.wait(ve, "dv")
                cx.inc("dv", ve.tensor_scalar(g8b[:], g8[:], stat[:, 7:8], None, ALU.mult))
                cx.wait(ve, "dv")
                cx.inc("dv", ve.tensor_copy(g8[:], g8b[:]))
                cx.wait(ve, "dv")
                for e in range(8):
                    cx.inc("dv", ve.tensor_scalar(P64[:, 8 * e:8 * (e + 1)], p16[:, 8:16],
                                     g8[:, e:e + 1], 2.0, ALU.mult, ALU.mult))
                if tau + 1 < TPG:
                    nb, nTh, nTl = lnbuf[(tau + 1) % 2]
                    cx.wait(sc, "xd", xd0)
                    ln(xt[tau + 1][:], sb=8)
                    cx.wait(ve, "dv")
                    cx.wait(ve, "sd", cx.val("sd"))
                    cx.inc("dv", ln_apply(nb[:], xt[tau + 1][:], sb=8))
                    cx.wait(te, "dv")
                    transp_split(nb, lambda c: 128 * c, NCH, nTh, nTl,
                                 lambda c: 128 * c, alt=True)
                    dv_h1 = cx.val("dv")
                pt64 = psB[:, 1024:1152]
                cx.wait(te, "dv")
                cx.inc("pe", te.matmul(pt64[0:64, 0:128], P64[:], ident[:], is_transpose=True))
                cx.wait(sc, "pe")
                cx.inc("ac", sc.activation(P64h[:], pt64[0:64, 0:128], AF.Copy))
                cx.wait(ve, "ac")
                cx.inc("dv", ve.tensor_sub(P64l[:], pt64[0:64, 0:128], P64h[:]))
                cx.wait(te, "dv")
                for hf in range(2):
                    cx.inc("pe", mm3(pbase[:, 512 * hf:512 * (hf + 1)], P64h[:], P64l[:],
                        b2h[:, 512 * hf:512 * (hf + 1)], b2l[:, 512 * hf:512 * (hf + 1)],
                        start=False, stop=True))
                cx.wait(ve, "pe")
                cx.wait(ve, "sd", cx.val("sd"))
                cx.inc("dv", ve.scalar_tensor_tensor(ht[:], pbase[:, 0:1024], 0.5, xa,
                                                     ALU.mult, ALU.add))
                dv_stt = cx.val("dv")
                cx.wait(sy, "dv")
                dma(sy, y_out[128 * t:128 * (t + 1), :], ht[:], sem="sd")
            afree[ks] = cx.val("pe")
        barrier()
        return nc, cx, locals()




from concourse.bass_utils import run_bass_kernel_spmd


def _host_inputs(core, x_batch, W):
    import ml_dtypes
    s = core % 2
    NT = 16
    tri16 = np.where(np.arange(128)[:, None] <= np.arange(128)[None, :],
                     0.0, NEG).astype(ml_dtypes.bfloat16)
    xr = x_batch.reshape(NT * 2, 128, 1024)
    return {
        "x": np.ascontiguousarray(xr[s::2].reshape(NT * 128, 1024)),
        "xp": np.ascontiguousarray(xr[1 - s::2].reshape(NT * 128, 1024)),
        "wbT": W["wbT"], "bb": W["bb"], "rb": W["rb"], "rmid": W["rmid"],
        "b2": W["b2"], "qkT": W["qkT"], "qv": W["qv"], "kkT": W["kkT"],
        "kv": W["kv"], "vkT": W["vkT"], "vv": W["vv"], "pkT": W["pkT"],
        "pv": W["pv"], "fkT": W["fkT"], "fv": W["fv"],
        "maskA": tri16,
        "maskB": (np.full((128, 1), NEG, dtype=ml_dtypes.bfloat16) if s == 0
                  else np.zeros((128, 1), dtype=ml_dtypes.bfloat16)),
        "ident": np.eye(128, dtype=np.float32),
    }


def kernel(L0_attn_x, L0_attn_qK, L0_attn_qV, L0_attn_kK, L0_attn_kV,
           L0_attn_vK, L0_attn_vV, L0_attn_pK, L0_attn_pV, L0_attn_fK,
           L0_attn_fV, L0_ffn_base_w, L0_ffn_base_b, L0_ffn_lora_A,
           L0_ffn_lora_B, L0_ffn_router_w, L0_ffn_router_b):
    x = np.ascontiguousarray(np.asarray(L0_attn_x, dtype=np.float32))
    W = {
        "wbT": np.ascontiguousarray(np.asarray(L0_ffn_base_w).T),
        "bb": np.asarray(L0_ffn_base_b).reshape(1, 1024),
        "rb": np.asarray(L0_ffn_router_b).reshape(1, 8),
        "rmid": np.ascontiguousarray(np.concatenate(
            [np.asarray(L0_ffn_router_w).T, np.asarray(L0_ffn_lora_A)], axis=1)),
        "b2": np.ascontiguousarray(np.asarray(L0_ffn_lora_B).reshape(64, 1024)),
        "qkT": np.ascontiguousarray(np.asarray(L0_attn_qK).T),
        "qv": np.ascontiguousarray(np.asarray(L0_attn_qV)),
        "kkT": np.ascontiguousarray(np.asarray(L0_attn_kK).T),
        "kv": np.ascontiguousarray(np.asarray(L0_attn_kV)),
        "vkT": np.ascontiguousarray(np.asarray(L0_attn_vK).T),
        "vv": np.ascontiguousarray(np.asarray(L0_attn_vV)),
        "pkT": np.ascontiguousarray(np.asarray(L0_attn_pK).T),
        "pv": np.ascontiguousarray(np.asarray(L0_attn_pV)),
        "fkT": np.ascontiguousarray(np.asarray(L0_attn_fK).T),
        "fv": np.ascontiguousarray(np.asarray(L0_attn_fV)),
    }
    nc, cx, _ = build(n_groups=4, upto="full")
    in_maps = [_host_inputs(c, x[c // 2], W) for c in range(8)]
    res = run_bass_kernel_spmd(nc, in_maps, core_ids=list(range(8)))
    out = np.empty((4, 4096, 1024), dtype=np.float32)
    for c in range(8):
        b, s = c // 2, c % 2
        y = np.asarray(res.results[c]["y"]).reshape(16, 128, 1024)
        out[b].reshape(32, 128, 1024)[s::2] = y
    return out
